# version 01
# baseline (speedup 1.0000x reference)
import numpy as np
import ml_dtypes
from contextlib import ExitStack

import concourse.bass as bass
import concourse.mybir as mybir
from concourse.bass_utils import run_bass_kernel_spmd

F32 = mybir.dt.float32
BF16 = mybir.dt.bfloat16
I32 = mybir.dt.int32
AF = mybir.ActivationFunctionType
ALU = mybir.AluOpType
AX = mybir.AxisListType
BF = ml_dtypes.bfloat16

P = 128
T = 2048
TB = 512
NTB = T // TB
NCH = 8
D = 1024
GRID = np.linspace(-1.2, 0.2, 8).astype(np.float32)
LN_EPS = 1e-5
COS_EPS = 1e-6
PI = float(np.pi)
RQ2PI = float(np.sqrt(2.0 / np.pi))
GELU_C = 0.044715
B2_STOP = None


def _mk(w):
    w = np.asarray(w, np.float32)
    K, M = w.shape
    KC, MC = K // P, M // P
    a = w.astype(BF).reshape(KC, P, MC, P)
    return np.ascontiguousarray(a.transpose(1, 2, 0, 3).reshape(P, MC * KC * P))

def _blocks(w, rcs):
    w = np.asarray(w, np.float32)
    out = np.empty((P, len(rcs) * P), BF)
    for i, (r, c) in enumerate(rcs):
        out[:, i * P:(i + 1) * P] = w[r:r + P, c:c + P].astype(BF)
    return out

def _vec(v, nch=None):
    v = np.asarray(v, np.float32)
    n = v.shape[0] // P if nch is None else nch
    return np.ascontiguousarray(v.reshape(n, P).T)

def _smallmat(w, ncols):
    w = np.asarray(w, np.float32)
    KC = w.shape[0] // P
    return np.ascontiguousarray(w.astype(BF).reshape(KC, P, ncols).transpose(1, 0, 2).reshape(P, KC * ncols))

def prep_shared(inp):
    d = {}
    d["a_gate"] = _smallmat(np.asarray(inp["L0_ffn_gate_w"], np.float32).T, 8)
    W1 = np.asarray(inp["L0_ffn_W1"], np.float32)
    W2 = np.asarray(inp["L0_ffn_W2"], np.float32)
    d["a_w1"] = np.concatenate([_mk(W1[e].T) for e in range(8)], axis=1)
    d["a_w2"] = np.concatenate([_mk(W2[e].T) for e in range(8)], axis=1)
    d["a_ws1"] = _mk(np.asarray(inp["L0_ffn_Ws1"], np.float32).T)
    d["a_ws2"] = _mk(np.asarray(inp["L0_ffn_Ws2"], np.float32).T)
    WQ = np.asarray(inp["L0_attn_W_Q"], np.float32)[:, :, 0].T
    WK = np.asarray(inp["L0_attn_W_K"], np.float32)[:, :, 0].T
    d["b_wqk"] = np.concatenate([_smallmat(WQ, 16), _smallmat(WK, 16)], axis=1)
    WV = np.asarray(inp["L0_attn_W_V"], np.float32)
    d["b_wv"] = _mk(WV.transpose(1, 0, 2).reshape(D, D))
    WO = np.asarray(inp["L0_attn_W_O"], np.float32)
    d["b_wo"] = _mk(WO.transpose(2, 1, 0).reshape(D, D))
    d["b_ln"] = np.concatenate([
        _vec(inp["L0_attn_ln1_w"]), _vec(inp["L0_attn_ln1_b"]),
        _vec(inp["L0_attn_ln2_w"]), _vec(inp["L0_attn_ln2_b"])], axis=1)
    d["b_w1"] = _mk(np.asarray(inp["L0_attn_W1"], np.float32))
    d["b_b1"] = _vec(inp["L0_attn_b1"])
    d["b_w2"] = _mk(np.asarray(inp["L0_attn_W2"], np.float32))
    d["b_b2"] = _vec(inp["L0_attn_b2"])
    d["d_gate"] = _smallmat(np.asarray(inp["L1_ffn_gate_W"], np.float32), 8)
    mW1 = np.asarray(inp["L1_ffn_mlp_W1"], np.float32)
    mW2 = np.asarray(inp["L1_ffn_mlp_W2"], np.float32)
    d["d_w1"] = np.concatenate([_mk(mW1[e]) for e in range(4)], axis=1)
    d["d_w2"] = np.concatenate([_mk(mW2[e]) for e in range(4)], axis=1)
    d["d_bias"] = np.concatenate(
        [np.concatenate([_vec(inp["L1_ffn_mlp_b1"][e], 32) for e in range(4)], axis=1),
         np.concatenate([_vec(inp["L1_ffn_mlp_b2"][e], 8) for e in range(4)], axis=1),
         np.concatenate([_vec(inp["L1_ffn_kan_ln_g0"][e], 8) for e in range(4)], axis=1),
         np.concatenate([_vec(inp["L1_ffn_kan_ln_b0"][e], 8) for e in range(4)], axis=1),
         np.concatenate([_vec(inp["L1_ffn_kan_ln_g1"][e], 16) for e in range(4)], axis=1),
         np.concatenate([_vec(inp["L1_ffn_kan_ln_b1"][e], 16) for e in range(4)], axis=1),
         np.concatenate([_vec(inp["L1_ffn_kan_sl_b0"][e], 16) for e in range(4)], axis=1),
         np.concatenate([_vec(inp["L1_ffn_kan_sl_b1"][e], 8) for e in range(4)], axis=1),
         ], axis=1)
    kw0 = np.asarray(inp["L1_ffn_kan_sl_W0"], np.float32)
    kw1 = np.asarray(inp["L1_ffn_kan_sl_W1"], np.float32)
    o0 = [(kh * 4096 + kc * P, m * P) for kh in range(2) for m in range(16) for kc in range(32)]
    o1 = [(q * 4096 + kc * P, m * P) for q in range(4) for m in range(8) for kc in range(32)]
    d["d_kw0"] = np.concatenate(
        [_blocks(kw0[e].reshape(D, 8, 2048).transpose(1, 0, 2).reshape(8192, 2048), o0) for e in range(4)], axis=1)
    d["d_kw1"] = np.concatenate(
        [_blocks(kw1[e].reshape(2048, 8, D).transpose(1, 0, 2).reshape(16384, D), o1) for e in range(4)], axis=1)
    for nm in ("q", "k", "v", "o"):
        d[f"e_w{nm}"] = _mk(np.asarray(inp[f"L1_attn_W{nm}"], np.float32))
    d["e_bias"] = np.concatenate([_vec(inp[f"L1_attn_b{nm}"]) for nm in ("q", "k", "v", "o")], axis=1)
    d["c_ident_bf"] = np.eye(P).astype(BF)
    d["c_ident_f32"] = np.eye(P).astype(np.float32)
    d["c_ones_bf"] = np.ones((P, 1), BF)
    d["c_ones_f32"] = np.ones((P, 1), np.float32)
    d["c_onesrow"] = np.ones((1, P), np.float32)
    sel8 = np.zeros((8, 8 * P), BF)
    for e in range(8):
        sel8[e, e * P:(e + 1) * P] = 1.0
    d["c_sel8"] = sel8
    hsel = np.zeros((16, NCH * P), np.float32)
    for c in range(NCH):
        for p in range(P):
            hsel[2 * c + (p >= 64), c * P + p] = 1.0
    d["c_hsel"] = hsel
    return d

def x_fm(xslice):
    return np.ascontiguousarray(
        np.asarray(xslice, np.float32).reshape(T, NCH, P).transpose(2, 1, 0).reshape(P, NCH * T))

def x_un(fm):
    return np.ascontiguousarray(np.asarray(fm).reshape(P, NCH, T).transpose(2, 1, 0).reshape(T, D))

def _dt_of(a):
    if a.dtype == BF:
        return BF16
    if a.dtype == np.int32:
        return I32
    return F32


class Sem:
    def __init__(self, C, name, step=1):
        self.h = C.stk.enter_context(C.nc.semaphore(name))
        self.n = 0
        self.step = step

    def inc(self, instr, k=1):
        instr.then_inc(self.h, self.step * k)
        self.n += self.step * k
        return self.n

    def wait(self, eng, val=None):
        eng.wait_ge(self.h, self.n if val is None else val)


RCS = 16
RING_N = 4

C_IDENT, C_ONES, C_ONESROW, C_SEL, C_BIAS, C_HSEL = 0, 128, 129, 257, 1281, 2048
CB_IDENT, CB_ONES, CB_AGATE, CB_DGATE, CB_WQK, CB_SEL8 = 0, 128, 129, 193, 257, 513
S_MROW, S_TROW, S_RROW, S_TMP = 0, 512, 1024, 1536
S_LNSC = 2048
S_RT = 3072


class Ctx:
    def __init__(self, nc, stk, arrs):
        self.nc = nc
        self.stk = stk
        self.pe, self.act, self.dve, self.pool, self.sp = \
            nc.tensor, nc.scalar, nc.vector, nc.gpsimd, nc.sync
        self.dram = {k: nc.dram_tensor(k, list(v.shape), _dt_of(v), kind="ExternalInput").ap()
                     for k, v in arrs.items()}
        self.sW = Sem(self, "sW", 16)
        self.sX = Sem(self, "sX", 16)
        self.sPE = Sem(self, "sPE")
        self.sACT = Sem(self, "sACT")
        self.sDVE = Sem(self, "sDVE")
        self.sPOOL = Sem(self, "sPOOL")
        e = stk.enter_context
        self.X = e(nc.sbuf_tensor("Xr", [P, NCH * T], F32))
        self.BIG = e(nc.sbuf_tensor("BIGa", [P, 16384], BF16))
        self.ZF = e(nc.sbuf_tensor("ZFa", [P, 4096], F32))
        self.MID = e(nc.sbuf_tensor("MIDa", [P, 4096], F32))
        self.SMALL = e(nc.sbuf_tensor("SMALLa", [P, 4096], F32))
        self.YB = e(nc.sbuf_tensor("YBa", [P, 4096], BF16))
        self.LNY = e(nc.sbuf_tensor("LNYa", [P, 2048], BF16))
        self.ZH = e(nc.sbuf_tensor("ZHa", [P, 8192], BF16))
        self.WR = e(nc.sbuf_tensor("WRa", [P, RING_N * RCS * P], BF16))
        self.CST = e(nc.sbuf_tensor("CSTa", [P, 3072], F32))
        self.CSTB = e(nc.sbuf_tensor("CSTBa", [P, 1600], BF16))
        self.SC2 = e(nc.sbuf_tensor("SC2a", [P, 2048], BF16))
        self.pb = [e(nc.psum_tensor(f"pb{i}", [P, 512], F32)) for i in range(8)]
        self.bankg = {}
        self.r_fed = 0
        self.r_used = 0
        self.r_pending = []
        self.r_slot_target = [0] * RING_N

        self.last_mm = None
        self.mm_unstamped = 0

    def stamp_pe(self):
        if self.last_mm is not None and self.mm_unstamped > 0:
            self.sPE.inc(self.last_mm)
            self.mm_unstamped = 0
        return self.sPE.n

    def mm(self, out, lhsT, rhs, start, stop):
        i = self.pe.matmul(out, lhsT, rhs, start=start, stop=stop)
        self.last_mm = i
        self.mm_unstamped += 1
        return i

    def tr(self, out, in_, ident):
        i = self.pe.transpose(out, in_, ident)
        self.last_mm = i
        self.mm_unstamped += 1
        return i

    def dop(self, instr):
        v = self.sDVE.inc(instr)
        self.dve.wait_ge(self.sDVE.h, v)
        return v

    def aop(self, instr):
        v = self.sACT.inc(instr)
        self.act.wait_ge(self.sACT.h, v)
        return v

    def bank_wait(self, i):
        g = self.bankg.get(i)
        if g is not None:
            self.pe.wait_ge(g[0], g[1])

    def bank_guard(self, i, sem, val=None):
        self.bankg[i] = (sem.h, sem.n if val is None else val)

    def barrier(self):
        self.stamp_pe()
        sems = [self.sW, self.sX, self.sPE, self.sACT, self.sDVE, self.sPOOL]
        for eng in (self.pe, self.act, self.dve, self.pool, self.sp):
            for s in sems:
                if s.n > 0:
                    eng.wait_ge(s.h, s.n)

    def setup(self):
        g = self.pool
        dm = self.dram
        for dst, src in [
            (self.CSTB[:, CB_IDENT:CB_IDENT + 128], dm["c_ident_bf"]),
            (self.CSTB[:, CB_ONES:CB_ONES + 1], dm["c_ones_bf"]),
            (self.CSTB[:, CB_AGATE:CB_AGATE + 64], dm["a_gate"]),
            (self.CSTB[:, CB_DGATE:CB_DGATE + 64], dm["d_gate"]),
            (self.CSTB[:, CB_WQK:CB_WQK + 256], dm["b_wqk"]),
            (self.CSTB[0:8, CB_SEL8:CB_SEL8 + 1024].rearrange("p (a b) -> p a b", a=8),
             dm["c_sel8"].rearrange("p (a b) -> p a b", a=8)),
            (self.CST[:, C_IDENT:C_IDENT + 128], dm["c_ident_f32"]),
            (self.CST[:, C_ONES:C_ONES + 1], dm["c_ones_f32"]),
            (self.CST[0:1, C_ONESROW:C_ONESROW + 128], dm["c_onesrow"]),
            (self.CST[0:16, C_HSEL:C_HSEL + 1024].rearrange("p (a b) -> p a b", a=8),
             dm["c_hsel"].rearrange("p (a b) -> p a b", a=8)),
        ]:
            self.sX.inc(g.dma_start(dst, src))
        self.ident_bf = self.CSTB[:, CB_IDENT:CB_IDENT + 128]
        self.ident_f32 = self.CST[:, C_IDENT:C_IDENT + 128]
        self.ones_bf = self.CSTB[:, CB_ONES:CB_ONES + 1]
        self.ones_f32 = self.CST[:, C_ONES:C_ONES + 1]
        self.onesrow = self.CST[0:1, C_ONESROW:C_ONESROW + 128]
        cp = self.CST[:, 2016 - C_BIAS + C_BIAS:2032]
        cp = self.CST[:, 2016:2032]
        i = None
        for j, v in enumerate([LN_EPS, PI / 2.0, 1.0] + [-float(g) for g in GRID]):
            i = self.dve.memset(cp[:, j:j + 1], float(v))
            self.nc.const_aps.register(F32, float(v), cp[:, j:j + 1])
        self.sDVE.inc(i)
        for eng in (self.pe, self.act, self.dve, self.pool, self.sp):
            self.sX.wait(eng)
            self.sDVE.wait(eng)

    def load_bias(self, name, ncols, col_off=0, src_col=0):
        dst = self.CST[:, C_BIAS + col_off: C_BIAS + col_off + ncols]
        src = self.dram[name]
        if src.shape[1] != ncols or src_col:
            src = src[:, src_col:src_col + ncols]
        self.sX.inc(self.pool.dma_start(dst, src))
        return dst

    def bias(self, col, n=1):
        return self.CST[:, C_BIAS + col: C_BIAS + col + n]

    def ring_queue(self, name, chunk0, nchunks):
        assert nchunks % RCS == 0, nchunks
        self.r_pending.append([self.dram[name], chunk0, chunk0 + nchunks])
        self._ring_fill()

    def _ring_fill(self):
        while self.r_pending and self.r_fed < self.r_used - (self.r_used % RCS) + RING_N * RCS:
            reg = self.r_pending[0]
            dram, nxt, end = reg
            n = min(RCS, end - nxt)
            slot = (self.r_fed // RCS) % RING_N
            self.sp.wait_ge(self.sPE.h, self.sPE.n)
            dst = self.WR[:, slot * RCS * P:(slot * RCS + n) * P]
            src = dram[:, nxt * P:(nxt + n) * P]
            self.r_slot_target[slot] = self.sW.inc(self.sp.dma_start(dst, src))
            self.r_fed += n
            reg[1] += n
            if reg[1] >= end:
                self.r_pending.pop(0)

    def wchunk(self):
        i = self.r_used
        slot = (i // RCS) % RING_N
        off = i % RCS
        if off == 0:
            self.stamp_pe()
            self._ring_fill()
            assert self.r_fed > i, "ring underflow: consumption passed queued chunks"
            self.pe.wait_ge(self.sW.h, self.r_slot_target[slot])
        self.r_used += 1
        return self.WR[:, (slot * RCS + off) * P:(slot * RCS + off + 1) * P]

    def ln(self, src, dst, nch, W=TB, affine=None, also_f32=None, src_bf=False):
        pe, act, dve = self.pe, self.act, self.dve
        NF = float(nch * P)
        ps1, ps2, pm, pr = self.pb[2], self.pb[3], self.pb[4], self.pb[5]
        act.wait_ge(self.sPE.h, self.stamp_pe())
        act.wait_ge(self.sDVE.h, self.sDVE.n)
        pe.wait_ge(self.sDVE.h, self.sDVE.n)
        pe.wait_ge(self.sACT.h, self.sACT.n)
        self.bank_wait(2), self.bank_wait(3)
        sq = self.SC2
        ones1 = self.ones_bf if src_bf else self.ones_f32
        pe_stamps = []
        for c in range(nch):
            s = src(c)
            sqv = sq[:, (c % 2) * 512:(c % 2) * 512 + W]
            if c >= 2:
                act.wait_ge(self.sPE.h, pe_stamps[c - 2])
            va = self.sACT.inc(act.activation(sqv, s, AF.Square))
            self.mm(ps1[0:1, 0:W], ones1, s, start=(c == 0), stop=(c == nch - 1))
            pe.wait_ge(self.sACT.h, va)
            self.mm(ps2[0:1, 0:W], self.ones_bf, sqv, start=(c == 0), stop=(c == nch - 1))
            pe_stamps.append(self.stamp_pe())
        vpe = self.sPE.n
        R = self.SMALL
        m_row = R[0:1, S_MROW:S_MROW + W]
        t_row = R[0:1, S_TROW:S_TROW + W]
        r_row = R[0:1, S_RROW:S_RROW + W]
        tmp = R[0:1, S_TMP:S_TMP + W]
        act.wait_ge(self.sPE.h, vpe)
        va = self.sACT.inc(act.mul(m_row, ps1[0:1, 0:W], 1.0 / NF))
        dve.wait_ge(self.sACT.h, va)
        dve.wait_ge(self.sPE.h, vpe)
        self.dop(dve.tensor_scalar(t_row, ps2[0:1, 0:W], 1.0 / NF, None, ALU.mult))
        self.dop(dve.tensor_tensor(tmp, m_row, m_row, ALU.mult))
        vd = self.dop(dve.tensor_tensor(t_row, t_row, tmp, ALU.subtract))
        self.bank_guard(2, self.sACT)
        self.bank_guard(3, self.sDVE)
        act.wait_ge(self.sDVE.h, vd)
        va = self.sACT.inc(act.activation(t_row, t_row, AF.Sqrt, bias=LN_EPS))
        dve.wait_ge(self.sACT.h, va)
        vd = self.dop(dve.reciprocal(r_row, t_row))
        self.bank_wait(4), self.bank_wait(5)
        pe.wait_ge(self.sDVE.h, vd)
        self.mm(pm[:, 0:W], self.onesrow, m_row, start=True, stop=True)
        self.mm(pr[:, 0:W], self.onesrow, r_row, start=True, stop=True)
        vb = self.stamp_pe()
        dve.wait_ge(self.sPE.h, vb)
        i = None
        for c in range(nch):
            s = src(c)
            o = dst(c)
            tscr = R[:, S_LNSC + (c % 2) * 512: S_LNSC + (c % 2) * 512 + W]
            dve.tensor_tensor(tscr, s, pm[:, 0:W], ALU.subtract)
            if affine is None:
                i = dve.tensor_tensor(o, tscr, pr[:, 0:W], ALU.mult)
            else:
                dve.tensor_tensor(tscr, tscr, pr[:, 0:W], ALU.mult)
                w_ap, b_ap = affine(c)
                i = dve.tensor_scalar(o, tscr, w_ap, b_ap, ALU.mult, ALU.add)
            if also_f32 is not None:
                self.sDVE.inc(i)
                i = dve.tensor_copy(also_f32(c), o)
        vd = self.sDVE.inc(i)
        self.bank_guard(4, self.sDVE)
        self.bank_guard(5, self.sDVE)
        return vd

    def routing(self, gate_ap, renorm, scale=0.5):
        pe, act, dve = self.pe, self.act, self.dve
        pg = self.pb[6]
        pt = self.pb[7]
        self.bank_wait(6)
        for kc in range(NCH):
            self.mm(pg[0:8, 0:TB], gate_ap[:, kc * 8:(kc + 1) * 8],
                    self.YB[:, kc * TB:(kc + 1) * TB], start=(kc == 0), stop=(kc == NCH - 1))
        v = self.stamp_pe()
        act.wait_ge(self.sPE.h, v)
        R = self.SMALL
        LGF = R[0:8, S_RT + 256:S_RT + 768]
        va = self.sACT.inc(act.copy(LGF, pg[0:8, 0:TB]))
        self.bank_guard(6, self.sACT)
        self.bank_wait(7)
        pe.wait_ge(self.sACT.h, va)
        for i in range(4):
            self.tr(pt[:, i * 8:(i + 1) * 8],
                    LGF[0:8, i * P:(i + 1) * P], self.ident_f32[0:8, 0:8])
        v = self.stamp_pe()
        dve.wait_ge(self.sPE.h, v)
        LG = R[:, S_RT:S_RT + 32]
        self.dop(dve.tensor_copy(LG, pt[:, 0:32]))
        self.bank_guard(7, self.sDVE)
        v3 = lambda off: R[:, S_RT + off:S_RT + off + 32].rearrange("p (a b) -> p a b", a=4)
        MX = R[:, S_RT + 32:S_RT + 36]
        NM = R[:, S_RT + 36:S_RT + 40]
        SM = R[:, S_RT + 40:S_RT + 44]
        RC = R[:, S_RT + 44:S_RT + 48]
        M1 = R[:, S_RT + 48:S_RT + 52]
        M2 = R[:, S_RT + 52:S_RT + 56]
        EX = R[:, S_RT + 64:S_RT + 96]
        PRb = R[:, S_RT + 96:S_RT + 128]
        GE = R[:, S_RT + 128:S_RT + 160]
        KP = R[:, S_RT + 160:S_RT + 192]
        WT = R[:, S_RT + 192:S_RT + 224]
        self.dop(dve.tensor_reduce(MX, LG.rearrange("p (a b) -> p a b", a=4), AX.X, ALU.max))
        vd = self.dop(dve.tensor_scalar(NM, MX, -1.0, None, ALU.mult))
        act.wait_ge(self.sDVE.h, vd)
        ia = None
        for i in range(4):
            ia = act.activation(EX[:, i * 8:(i + 1) * 8], LG[:, i * 8:(i + 1) * 8],
                                AF.Exp, bias=NM[:, i:i + 1])
        va = self.sACT.inc(ia)
        dve.wait_ge(self.sACT.h, va)
        self.dop(dve.tensor_reduce(SM, EX.rearrange("p (a b) -> p a b", a=4), AX.X, ALU.add))
        self.dop(dve.reciprocal(RC, SM))
        ii = None
        for i in range(4):
            ii = dve.tensor_single_scalar(PRb[:, i * 8:(i + 1) * 8], EX[:, i * 8:(i + 1) * 8],
                                          RC[:, i:i + 1], ALU.mult)
        self.dop(ii)
        self.dop(dve.tensor_reduce(M1, PRb.rearrange("p (a b) -> p a b", a=4), AX.X, ALU.max))
        for i in range(4):
            ii = dve.tensor_single_scalar(GE[:, i * 8:(i + 1) * 8], PRb[:, i * 8:(i + 1) * 8],
                                          M1[:, i:i + 1], ALU.is_ge)
        self.dop(ii)
        self.dop(dve.tensor_tensor(GE, PRb, GE, ALU.subtract))
        self.dop(dve.tensor_reduce(M2, GE.rearrange("p (a b) -> p a b", a=4), AX.X, ALU.max))
        for i in range(4):
            ii = dve.tensor_single_scalar(KP[:, i * 8:(i + 1) * 8], PRb[:, i * 8:(i + 1) * 8],
                                          M2[:, i:i + 1], ALU.is_ge)
        self.dop(ii)
        if renorm:
            self.dop(dve.tensor_tensor(MX, M1, M2, ALU.add))
            self.dop(dve.reciprocal(RC, MX))
            self.dop(dve.tensor_scalar(RC, RC, scale, None, ALU.mult))
            self.dop(dve.tensor_tensor(WT, PRb, KP, ALU.mult))
            for i in range(4):
                ii = dve.tensor_single_scalar(WT[:, i * 8:(i + 1) * 8], WT[:, i * 8:(i + 1) * 8],
                                              RC[:, i:i + 1], ALU.mult)
            self.dop(ii)
        else:
            self.dop(dve.scalar_tensor_tensor(WT, PRb, scale, KP, ALU.mult, ALU.mult))
        vd = self.sDVE.n
        self.bank_wait(7)
        pe.wait_ge(self.sDVE.h, vd)
        for i in range(4):
            self.tr(pt[0:8, i * P:(i + 1) * P], WT[:, i * 8:(i + 1) * 8], self.ident_f32)
        v = self.stamp_pe()
        act.wait_ge(self.sPE.h, v)
        wfm = self.SC2[0:8, 1024:1024 + TB]
        va = self.sACT.inc(act.copy(wfm, pt[0:8, 0:TB]))
        self.bank_guard(7, self.sACT)
        return wfm, va

    def w_bcast(self, wfm, wfm_ready, e, bank):
        self.bank_wait(bank)
        self.pe.wait_ge(self.sACT.h, wfm_ready)
        sel = self.CSTB[0:8, CB_SEL8 + e * P:CB_SEL8 + (e + 1) * P]
        self.mm(self.pb[bank][:, 0:TB], sel, wfm, start=True, stop=True)
        return self.stamp_pe()



def emit_stage_A(C):
    for tb in range(NTB):
        C.barrier()
        xc = lambda c, tb=tb: C.X[:, c * T + tb * TB: c * T + tb * TB + TB]
        yc = lambda c: C.YB[:, c * TB:(c + 1) * TB]
        vy = C.ln(xc, yc, NCH)
        for eng in (C.pe,):
            eng.wait_ge(C.sDVE.h, vy)
        gate = C.CSTB[:, CB_AGATE:CB_AGATE + 64]
        wfm, wv = C.routing(gate, renorm=False, scale=0.5)
        for e in range(8):
            vb = C.w_bcast(wfm, wv, e, 4)
            for m in range(12):
                b = m % 2
                C.bank_wait(b)
                for kc in range(NCH):
                    C.mm(C.pb[b][:, 0:TB], C.wchunk(), C.YB[:, kc * TB:(kc + 1) * TB],
                         start=(kc == 0), stop=(kc == NCH - 1))
                v = C.stamp_pe()
                C.act.wait_ge(C.sPE.h, v)
                if m < 6:
                    i = C.act.copy(C.BIG[:, m * TB:(m + 1) * TB], C.pb[b][:, 0:TB])
                else:
                    i = C.act.activation(C.BIG[:, 3072 + (m - 6) * TB:3072 + (m - 5) * TB],
                                         C.pb[b][:, 0:TB], AF.Silu)
                va = C.sACT.inc(i)
                C.bank_guard(b, C.sACT, va)
            C.dve.wait_ge(C.sACT.h, C.sACT.n)
            C.dve.wait_ge(C.sPE.h, C.sPE.n)
            ii = None
            for j in range(6):
                ys = C.BIG[:, j * TB:(j + 1) * TB]
                gs = C.BIG[:, 3072 + j * TB:3072 + (j + 1) * TB]
                C.dve.tensor_tensor(ys, ys, gs, ALU.mult)
                ii = C.dve.tensor_tensor(ys, ys, C.pb[4][:, 0:TB], ALU.mult)
            vd = C.sDVE.inc(ii)
            C.bank_guard(4, C.sDVE, vd)
            C.pe.wait_ge(C.sDVE.h, vd)
            for m in range(8):
                b = 2 + (m % 2)
                C.bank_wait(b)
                for kc in range(6):
                    C.mm(C.pb[b][:, 0:TB], C.wchunk(), C.BIG[:, kc * TB:(kc + 1) * TB],
                         start=(kc == 0), stop=(kc == 5))
                v = C.stamp_pe()
                C.dve.wait_ge(C.sPE.h, v)
                x_ap = xc(m)
                vd = C.sDVE.inc(C.dve.tensor_tensor(x_ap, x_ap, C.pb[b][:, 0:TB], ALU.add))
                C.bank_guard(b, C.sDVE, vd)
        for m in range(24):
            b = m % 2
            C.bank_wait(b)
            for kc in range(NCH):
                C.mm(C.pb[b][:, 0:TB], C.wchunk(), C.YB[:, kc * TB:(kc + 1) * TB],
                     start=(kc == 0), stop=(kc == NCH - 1))
            v = C.stamp_pe()
            C.act.wait_ge(C.sPE.h, v)
            if m < 12:
                i = C.act.mul(C.BIG[:, m * TB:(m + 1) * TB], C.pb[b][:, 0:TB], 0.5)
            else:
                i = C.act.activation(C.BIG[:, 6144 + (m - 12) * TB:6144 + (m - 11) * TB],
                                     C.pb[b][:, 0:TB], AF.Silu)
            va = C.sACT.inc(i)
            C.bank_guard(b, C.sACT, va)
        C.dve.wait_ge(C.sACT.h, C.sACT.n)
        ii = None
        for j in range(12):
            ys = C.BIG[:, j * TB:(j + 1) * TB]
            gs = C.BIG[:, 6144 + j * TB:6144 + (j + 1) * TB]
            ii = C.dve.tensor_tensor(ys, ys, gs, ALU.mult)
        vd = C.sDVE.inc(ii)
        C.pe.wait_ge(C.sDVE.h, vd)
        for m in range(8):
            b = 2 + (m % 2)
            C.bank_wait(b)
            for kc in range(12):
                C.mm(C.pb[b][:, 0:TB], C.wchunk(), C.BIG[:, kc * TB:(kc + 1) * TB],
                     start=(kc == 0), stop=(kc == 11))
            v = C.stamp_pe()
            C.dve.wait_ge(C.sPE.h, v)
            x_ap = xc(m)
            vd = C.sDVE.inc(C.dve.tensor_tensor(x_ap, x_ap, C.pb[b][:, 0:TB], ALU.add))
            C.bank_guard(b, C.sDVE, vd)

def queue_stage_A(C):
    for tb in range(NTB):
        for e in range(8):
            C.ring_queue("a_w1", e * 96, 96)
            C.ring_queue("a_w2", e * 48, 48)
        C.ring_queue("a_ws1", 0, 192)
        C.ring_queue("a_ws2", 0, 96)

DB_B1, DB_B2, DB_G0, DB_B0, DB_G1, DB_B1K, DB_KB0, DB_KB1, DB_GRID = \
    0, 128, 160, 192, 224, 288, 352, 416, 448

def _zc(C, j):
    return C.ZF[:, j * TB:(j + 1) * TB] if j < 8 else C.MID[:, (j - 8) * TB:(j - 7) * TB]

def emit_stage_D(C):
    db = lambda col, n=1: C.bias(col, n)
    C.load_bias("d_bias", 448)
    for eng in (C.act, C.dve, C.pe):
        C.sX.wait(eng)
    for tb in range(NTB):
        C.barrier()
        xc = lambda c, tb=tb: C.X[:, c * T + tb * TB: c * T + tb * TB + TB]
        yc = lambda c: C.YB[:, c * TB:(c + 1) * TB]
        vy = C.ln(xc, yc, NCH)
        C.pe.wait_ge(C.sDVE.h, vy)
        gate = C.CSTB[:, CB_DGATE:CB_DGATE + 64]
        wfm, wv = C.routing(gate, renorm=True, scale=0.5)
        for e in range(4):
            vb = C.w_bcast(wfm, wv, e, 4)
            C.act.wait_ge(C.sPE.h, vb)
            wmid = C.MID[:, 0:TB]
            g = C.bankg.get("wmid")
            if g is not None:
                C.act.wait_ge(g[0], g[1])
            va = C.sACT.inc(C.act.copy(wmid, C.pb[4][:, 0:TB]))
            C.bank_guard(4, C.sACT, va)
            for m in range(32):
                b = m % 2
                C.bank_wait(b)
                for kc in range(NCH):
                    C.mm(C.pb[b][:, 0:TB], C.wchunk(), C.YB[:, kc * TB:(kc + 1) * TB],
                         start=(kc == 0), stop=(kc == NCH - 1))
                v = C.stamp_pe()
                C.act.wait_ge(C.sPE.h, v)
                va = C.sACT.inc(C.act.activation(
                    C.BIG[:, m * TB:(m + 1) * TB], C.pb[b][:, 0:TB], AF.Gelu,
                    bias=db(DB_B1 + e * 32 + m)))
                C.bank_guard(b, C.sACT, va)
            C.pe.wait_ge(C.sACT.h, C.sACT.n)
            for m in range(8):
                b = 2 + (m % 2)
                C.bank_wait(b)
                for kc in range(32):
                    C.mm(C.pb[b][:, 0:TB], C.wchunk(), C.BIG[:, kc * TB:(kc + 1) * TB],
                         start=(kc == 0), stop=(kc == 31))
                v = C.stamp_pe()
                C.dve.wait_ge(C.sPE.h, v)
                C.dve.wait_ge(C.sACT.h, va)
                tt = C.SMALL[:, S_LNSC + (m % 2) * 512:S_LNSC + (m % 2) * 512 + TB]
                C.dve.scalar_tensor_tensor(tt, C.pb[b][:, 0:TB], db(DB_B2 + e * 8 + m),
                                           wmid, ALU.add, ALU.mult)
                x_ap = xc(m)
                vd = C.sDVE.inc(C.dve.tensor_tensor(x_ap, x_ap, tt, ALU.add))
                C.bank_guard(b, C.sDVE, vd)
            C.bankg["wmid"] = (C.sDVE.h, C.sDVE.n)
        C.ln(yc, yc, NCH, src_bf=True)
        for e in range(4):
            C.barrier()
            ii = None
            for c in range(NCH):
                for gidx in range(8):
                    ii = C.dve.tensor_scalar(db(DB_GRID + c * 8 + gidx), db(DB_B0 + e * 8 + c),
                                             float(GRID[gidx]), None, ALU.subtract)
            vgb = C.sDVE.inc(ii)
            C.act.wait_ge(C.sDVE.h, vgb)
            for kh in range(2):
                if kh > 0:
                    C.barrier()
                tile_st = _build_basis(C, src=yc, nsrc=NCH, kh_g0=kh * 4, ngrp=4,
                                       scale_bias=(lambda c: (db(DB_G0 + e * 8 + c),)),
                                       grid_bias=lambda c, gi: db(DB_GRID + c * 8 + gi))
                for m in range(16):
                    b = m % 2
                    C.bank_wait(b)
                    for kc in range(32):
                        if m == 0:
                            C.pe.wait_ge(C.sDVE.h, tile_st[kc])
                        C.mm(C.pb[b][:, 0:TB], C.wchunk(),
                             C.BIG[:, kc * TB:(kc + 1) * TB], start=(kc == 0), stop=(kc == 31))
                    v = C.stamp_pe()
                    zc = _zc(C, m)
                    if kh == 0:
                        C.act.wait_ge(C.sPE.h, v)
                        va = C.sACT.inc(C.act.copy(zc, C.pb[b][:, 0:TB]))
                        C.bank_guard(b, C.sACT, va)
                    else:
                        C.dve.wait_ge(C.sPE.h, v)
                        vd = C.sDVE.inc(C.dve.scalar_tensor_tensor(
                            zc, C.pb[b][:, 0:TB], db(DB_KB0 + e * 16 + m), zc, ALU.add, ALU.add))
                        C.bank_guard(b, C.sDVE, vd)
            C.barrier()
            C.ln(lambda j: _zc(C, j), lambda j: C.ZH[:, j * TB:(j + 1) * TB], 16,
                 affine=lambda j: (db(DB_G1 + e * 16 + j), db(DB_B1K + e * 16 + j)))
            vb = C.w_bcast(wfm, wv, 4 + e, 4)
            C.act.wait_ge(C.sPE.h, vb)
            C.barrier()
            wmid = C.MID[:, 0:TB]
            va = C.sACT.inc(C.act.copy(wmid, C.pb[4][:, 0:TB]))
            C.bank_guard(4, C.sACT, va)
            C.barrier()
            for q in range(4):
                tile_st = _build_basis(C, src=lambda j: C.ZH[:, j * TB:(j + 1) * TB],
                                       nsrc=16, kh_g0=q * 2, ngrp=2,
                                       scale_bias=None,
                                       grid_bias=lambda c, gi: C.CST[:, 2019 + gi:2020 + gi])
                for m in range(8):
                    for kc in range(32):
                        if q == 0 and m == 0:
                            C.pe.wait_ge(C.sDVE.h, tile_st[kc])
                        elif m == 0:
                            C.pe.wait_ge(C.sDVE.h, tile_st[kc])
                        C.mm(C.pb[m][:, 0:TB], C.wchunk(),
                             C.BIG[:, kc * TB:(kc + 1) * TB],
                             start=(q == 0 and kc == 0), stop=(q == 3 and kc == 31))
                if q < 3:
                    C.barrier()
            v = C.stamp_pe()
            C.dve.wait_ge(C.sPE.h, v)
            C.dve.wait_ge(C.sACT.h, C.sACT.n)
            for m in range(8):
                tt = C.SMALL[:, S_LNSC + (m % 2) * 512:S_LNSC + (m % 2) * 512 + TB]
                C.dve.scalar_tensor_tensor(tt, C.pb[m][:, 0:TB], db(DB_KB1 + e * 8 + m),
                                           wmid, ALU.add, ALU.mult)
                x_ap = xc(m)
                vd = C.sDVE.inc(C.dve.tensor_tensor(x_ap, x_ap, tt, ALU.add))
                C.bank_guard(m, C.sDVE, vd)
            C.barrier()

def _build_basis(C, src, nsrc, kh_g0, ngrp, scale_bias, grid_bias):
    stamps = []
    i = 0
    prev_act = {}
    for gg in range(ngrp):
        gidx = kh_g0 + gg
        for dc in range(nsrc):
            sg = C.SC2[:, (i % 2) * 512:(i % 2) * 512 + TB]
            tile = C.BIG[:, i * TB:(i + 1) * TB]
            if i >= 2:
                C.act.wait_ge(C.sDVE.h, stamps[i - 2])
            if scale_bias is not None:
                sc = scale_bias(dc)[0]
                ia = C.act.activation(sg, src(dc), AF.Sigmoid, bias=grid_bias(dc, gidx), scale=sc)
            else:
                ia = C.act.activation(sg, src(dc), AF.Sigmoid, bias=grid_bias(dc, gidx))
            va = C.sACT.inc(ia)
            C.dve.wait_ge(C.sACT.h, va)
            C.dve.tensor_scalar(tile, sg, -4.0, 4.0, ALU.mult, ALU.add)
            vd = C.sDVE.inc(C.dve.tensor_tensor(tile, tile, sg, ALU.mult))
            stamps.append(vd)
            i += 1
    return stamps

def queue_stage_D(C):
    for tb in range(NTB):
        for e in range(4):
            C.ring_queue("d_w1", e * 256, 256)
            C.ring_queue("d_w2", e * 256, 256)
        for e in range(4):
            C.ring_queue("d_kw0", e * 1024, 1024)
            C.ring_queue("d_kw1", e * 1024, 1024)

B_TRIG = 0
B_CAR = 3072
B_SUM = 3090
B_PRE = 3108
B_TMP = 3126

def emit_stage_B(C, mode):
    R = C.SMALL
    db = lambda col, n=1: C.bias(col, n)
    C.load_bias("b_ln", 32, col_off=500)
    if mode == "full":
        C.load_bias("b_b1", 32, col_off=532)
        C.load_bias("b_b2", 8, col_off=564)
    for eng in (C.act, C.dve, C.pe):
        C.sX.wait(eng)
    ln1 = lambda c: (C.bias(500 + c), C.bias(508 + c))
    ln2 = lambda c: (C.bias(516 + c), C.bias(524 + c))
    dve, act, pe = C.dve, C.act, C.pe
    if mode == "sums":
        ii = dve.memset(R[:, B_SUM:B_SUM + 18], 0.0)
        C.sDVE.inc(ii)
    else:
        ii = dve.memset(R[:, B_PRE:B_PRE + 18], 0.0)
        C.dop(ii)
        AS = C.ASUM
        PM = C.PMASK
        for r in range(8):
            ii = dve.scalar_tensor_tensor(R[:, B_PRE:B_PRE + 18], AS[:, r * 18:(r + 1) * 18],
                                          PM[:, r:r + 1], R[:, B_PRE:B_PRE + 18],
                                          ALU.mult, ALU.add)
            C.dop(ii)
        C.dop(dve.tensor_copy(R[:, B_CAR:B_CAR + 18], R[:, B_PRE:B_PRE + 18]))
    tA, tB, tC, tD, tE, tF = (R[0:16, B_TRIG + i * 512:B_TRIG + (i + 1) * 512] for i in range(6))
    for tb in range(NTB):
        C.barrier()
        xc = lambda c, tb=tb: C.X[:, c * T + tb * TB: c * T + tb * TB + TB]
        yc = lambda c: C.YB[:, c * TB:(c + 1) * TB]
        y1c = lambda c: C.BIG[:, c * TB:(c + 1) * TB]
        vy = C.ln(xc, yc, NCH)
        C.dve.wait_ge(C.sDVE.h, vy)
        C.ln(yc, y1c, NCH, affine=ln1, src_bf=True)
        C.barrier()
        C.bank_wait(6), C.bank_wait(7)
        wq = C.CSTB[:, CB_WQK:CB_WQK + 128]
        wk = C.CSTB[:, CB_WQK + 128:CB_WQK + 256]
        for kc in range(NCH):
            C.mm(C.pb[6][0:16, 0:TB], wq[:, kc * 16:(kc + 1) * 16], y1c(kc),
                 start=(kc == 0), stop=(kc == NCH - 1))
        for kc in range(NCH):
            C.mm(C.pb[7][0:16, 0:TB], wk[:, kc * 16:(kc + 1) * 16], y1c(kc),
                 start=(kc == 0), stop=(kc == NCH - 1))
        v = C.stamp_pe()
        act.wait_ge(C.sPE.h, v)
        if mode == "full":
            va = C.sACT.inc(act.activation(tC, C.pb[6][0:16, 0:TB], AF.Tanh))
            act.wait_ge(C.sACT.h, va)
            va = C.sACT.inc(act.activation(tA, tC, AF.Sin, bias=PI / 2.0, scale=PI / 4.0))
            va = C.sACT.inc(act.activation(tB, tC, AF.Sin, scale=PI / 4.0))
            C.bank_guard(6, C.sACT, va)
        va = C.sACT.inc(act.activation(tC, C.pb[7][0:16, 0:TB], AF.Tanh))
        act.wait_ge(C.sACT.h, va)
        vck = C.sACT.inc(act.activation(tD, tC, AF.Sin, scale=PI / 4.0))
        act.wait_ge(C.sACT.h, vck)
        vcc = C.sACT.inc(act.activation(tC, tC, AF.Sin, bias=PI / 2.0, scale=PI / 4.0))
        C.bank_guard(7, C.sACT, vcc)
        dve.wait_ge(C.sACT.h, vcc)
        if mode == "full":
            C.dop(dve.tensor_tensor_scan(tE, tC, tC, R[0:16, B_CAR + 16:B_CAR + 17],
                                         ALU.add, ALU.bypass))
            C.dop(dve.tensor_copy(R[0:16, B_CAR + 16:B_CAR + 17], tE[0:16, 511:512]))
            C.dop(dve.tensor_tensor_scan(tF, tD, tD, R[0:16, B_CAR + 17:B_CAR + 18],
                                         ALU.add, ALU.bypass))
            C.dop(dve.tensor_copy(R[0:16, B_CAR + 17:B_CAR + 18], tF[0:16, 511:512]))
            C.dop(dve.tensor_tensor(tE, tA, tE, ALU.mult))
            C.dop(dve.tensor_tensor(tF, tB, tF, ALU.mult))
            C.dop(dve.tensor_tensor(tE, tE, tF, ALU.add))
            C.dop(dve.tensor_scalar(tE, tE, COS_EPS, None, ALU.add))
            C.dop(dve.reciprocal(tE, tE))
        else:
            C.dop(dve.tensor_reduce(R[0:16, B_TMP:B_TMP + 1], tC, AX.X, ALU.add))
            C.dop(dve.tensor_tensor(R[0:16, B_SUM + 16:B_SUM + 17],
                                    R[0:16, B_SUM + 16:B_SUM + 17],
                                    R[0:16, B_TMP:B_TMP + 1], ALU.add))
            C.dop(dve.tensor_reduce(R[0:16, B_TMP:B_TMP + 1], tD, AX.X, ALU.add))
            C.dop(dve.tensor_tensor(R[0:16, B_SUM + 17:B_SUM + 18],
                                    R[0:16, B_SUM + 17:B_SUM + 18],
                                    R[0:16, B_TMP:B_TMP + 1], ALU.add))
        ZS = lambda i: C.ZF[:, i * 512:(i + 1) * 512]
        for c in range(NCH):
            b = c % 2
            C.bank_wait(b)
            for kc in range(NCH):
                C.mm(C.pb[b][:, 0:TB], C.wchunk(), y1c(kc), start=(kc == 0), stop=(kc == NCH - 1))
            v = C.stamp_pe()
            act.wait_ge(C.sPE.h, v)
            va = C.sACT.inc(act.copy(ZS(2), C.pb[b][:, 0:TB]))
            C.bank_guard(b, C.sACT, va)
            hs = C.CST[0:16, C_HSEL + c * P:C_HSEL + (c + 1) * P]
            C.bank_wait(4), C.bank_wait(5)
            pe.wait_ge(C.sACT.h, vcc)
            C.mm(C.pb[4][:, 0:TB], hs, tC, start=True, stop=True)
            C.mm(C.pb[5][:, 0:TB], hs, tD, start=True, stop=True)
            v = C.stamp_pe()
            dve.wait_ge(C.sPE.h, v)
            dve.wait_ge(C.sACT.h, va)
            if mode == "sums":
                C.dop(dve.tensor_tensor(ZS(3), ZS(2), C.pb[4][:, 0:TB], ALU.mult))
                C.dop(dve.tensor_reduce(R[:, B_TMP:B_TMP + 1], ZS(3), AX.X, ALU.add))
                C.dop(dve.tensor_tensor(R[:, B_SUM + c:B_SUM + c + 1],
                                        R[:, B_SUM + c:B_SUM + c + 1],
                                        R[:, B_TMP:B_TMP + 1], ALU.add))
                C.dop(dve.tensor_tensor(ZS(3), ZS(2), C.pb[5][:, 0:TB], ALU.mult))
                vg = C.dop(dve.tensor_reduce(R[:, B_TMP:B_TMP + 1], ZS(3), AX.X, ALU.add))
                C.dop(dve.tensor_tensor(R[:, B_SUM + 8 + c:B_SUM + 9 + c],
                                        R[:, B_SUM + 8 + c:B_SUM + 9 + c],
                                        R[:, B_TMP:B_TMP + 1], ALU.add))
                C.bank_guard(4, C.sDVE, vg), C.bank_guard(5, C.sDVE, C.sDVE.n)
                continue
            C.dop(dve.tensor_tensor(ZS(3), ZS(2), C.pb[4][:, 0:TB], ALU.mult))
            C.dop(dve.tensor_tensor_scan(ZS(0), ZS(3), ZS(3), R[:, B_CAR + c:B_CAR + c + 1],
                                         ALU.add, ALU.bypass))
            C.dop(dve.tensor_copy(R[:, B_CAR + c:B_CAR + c + 1], ZS(0)[:, 511:512]))
            C.dop(dve.tensor_tensor(ZS(4), ZS(2), C.pb[5][:, 0:TB], ALU.mult))
            C.dop(dve.tensor_tensor_scan(ZS(1), ZS(4), ZS(4), R[:, B_CAR + 8 + c:B_CAR + 9 + c],
                                         ALU.add, ALU.bypass))
            vg = C.dop(dve.tensor_copy(R[:, B_CAR + 8 + c:B_CAR + 9 + c], ZS(1)[:, 511:512]))
            C.bank_guard(4, C.sDVE, vg), C.bank_guard(5, C.sDVE, vg)
            C.bank_wait(4), C.bank_wait(5)
            C.mm(C.pb[4][:, 0:TB], hs, tA, start=True, stop=True)
            C.mm(C.pb[5][:, 0:TB], hs, tB, start=True, stop=True)
            v = C.stamp_pe()
            dve.wait_ge(C.sPE.h, v)
            C.dop(dve.tensor_tensor(ZS(5), ZS(0), C.pb[4][:, 0:TB], ALU.mult))
            C.dop(dve.tensor_tensor(ZS(4), ZS(1), C.pb[5][:, 0:TB], ALU.mult))
            vg = C.dop(dve.tensor_tensor(ZS(5), ZS(5), ZS(4), ALU.add))
            C.bank_guard(4, C.sDVE, vg), C.bank_guard(5, C.sDVE, vg)
            C.bank_wait(4)
            C.mm(C.pb[4][:, 0:TB], hs, tE, start=True, stop=True)
            v = C.stamp_pe()
            dve.wait_ge(C.sPE.h, v)
            vg = C.dop(dve.tensor_tensor(C.BIG[:, 4096 + c * TB:4096 + (c + 1) * TB],
                                         ZS(5), C.pb[4][:, 0:TB], ALU.mult))
            C.bank_guard(4, C.sDVE, vg)
        if mode == "sums":
            continue
        C.barrier()
        for m in range(NCH):
            b = 2 + (m % 2)
            C.bank_wait(b)
            for kc in range(NCH):
                C.mm(C.pb[b][:, 0:TB], C.wchunk(),
                     C.BIG[:, 4096 + kc * TB:4096 + (kc + 1) * TB],
                     start=(kc == 0), stop=(kc == NCH - 1))
            v = C.stamp_pe()
            dve.wait_ge(C.sPE.h, v)
            x2 = C.MID[:, m * TB:(m + 1) * TB]
            dve.tensor_tensor(x2, C.pb[b][:, 0:TB], yc(m), ALU.add)
            x_ap = xc(m)
            vd = C.sDVE.inc(dve.tensor_tensor(x_ap, x_ap, x2, ALU.add))
            C.bank_guard(b, C.sDVE, vd)
        C.barrier()
        if B2_STOP == "x2":
            continue
        vy2 = C.ln(lambda m: C.MID[:, m * TB:(m + 1) * TB], yc, NCH, affine=ln2)
        C.pe.wait_ge(C.sDVE.h, vy2)
        if B2_STOP == "y2":
            C.barrier()
            ii = None
            for c in range(NCH):
                ii = C.dve.tensor_copy(xc(c), yc(c))
            C.sDVE.inc(ii)
            continue
        for m in range(32):
            b = m % 2
            C.bank_wait(b)
            for kc in range(NCH):
                C.mm(C.pb[b][:, 0:TB], C.wchunk(), yc(kc), start=(kc == 0), stop=(kc == NCH - 1))
            v = C.stamp_pe()
            act.wait_ge(C.sPE.h, v)
            s = C.ZF[:, 0:TB]
            w_ = C.ZF[:, 512:1024]
            th = C.ZF[:, 1024:1536]
            a_ = C.ZF[:, 1536:2048]
            g = C.bankg.get("gelu")
            if g is not None:
                act.wait_ge(g[0], g[1])
            u_ = C.ZF[:, 2048:2560]
            dve.wait_ge(C.sPE.h, v)
            vd = C.sDVE.inc(dve.tensor_scalar(u_, C.pb[b][:, 0:TB], db(532 + m), None, ALU.add))
            if B2_STOP == "u" and m < NCH:
                C.dop(dve.tensor_copy(xc(m), u_))
            act.wait_ge(C.sDVE.h, vd)
            va = C.sACT.inc(act.activation(s, u_, AF.Square))
            dve.wait_ge(C.sACT.h, va)
            dve.tensor_scalar(w_, s, GELU_C, 1.0, ALU.mult, ALU.add)
            vd = C.sDVE.inc(dve.tensor_tensor(w_, w_, u_, ALU.mult))
            act.wait_ge(C.sDVE.h, vd)
            va = C.sACT.inc(act.activation(th, w_, AF.Tanh, scale=RQ2PI))
            dve.wait_ge(C.sACT.h, va)
            dve.tensor_tensor(a_, th, u_, ALU.mult)
            dve.tensor_tensor(a_, a_, u_, ALU.add)
            vd = C.sDVE.inc(dve.tensor_scalar(C.BIG[:, m * TB:(m + 1) * TB], a_, 0.5, None, ALU.mult))
            C.bank_guard(b, C.sDVE, vd)
            C.bankg["gelu"] = (C.sDVE.h, vd)
        if B2_STOP == "h":
            C.barrier()
            ii = None
            for c in range(NCH):
                ii = C.dve.tensor_copy(xc(c), C.BIG[:, c * TB:(c + 1) * TB])
            C.sDVE.inc(ii)
            continue
        C.pe.wait_ge(C.sDVE.h, C.sDVE.n)
        for m in range(NCH):
            b = 2 + (m % 2)
            C.bank_wait(b)
            for kc in range(32):
                C.mm(C.pb[b][:, 0:TB], C.wchunk(), C.BIG[:, kc * TB:(kc + 1) * TB],
                     start=(kc == 0), stop=(kc == 31))
            v = C.stamp_pe()
            dve.wait_ge(C.sPE.h, v)
            x_ap = xc(m)
            vd = C.sDVE.inc(dve.scalar_tensor_tensor(
                x_ap, C.pb[b][:, 0:TB], db(564 + m), x_ap, ALU.add, ALU.add))
            C.bank_guard(b, C.sDVE, vd)
    if mode == "sums":
        C.barrier()
        C.sX.inc(C.pool.dma_start(C.sums_out, R[:, B_SUM:B_SUM + 18]))

def queue_stage_B(C, mode):
    for tb in range(NTB):
        C.ring_queue("b_wv", 0, 64)
        if mode == "full":
            C.ring_queue("b_wo", 0, 64)
            C.ring_queue("b_w1", 0, 256)
            C.ring_queue("b_w2", 0, 256)


def emit_stage_E(C):
    db = lambda col, n=1: C.bias(col, n)
    C.load_bias("e_bias", 32)
    for eng in (C.act, C.dve, C.pe):
        C.sX.wait(eng)
    R = C.SMALL
    for tb in range(NTB):
        C.barrier()
        xc = lambda c, tb=tb: C.X[:, c * T + tb * TB: c * T + tb * TB + TB]
        yc = lambda c: C.YB[:, c * TB:(c + 1) * TB]
        vy = C.ln(xc, yc, NCH)
        C.pe.wait_ge(C.sDVE.h, vy)
        for ip, (dst, bcol) in enumerate([(0, 0), (1, 8), (2, 16)]):
            for m in range(NCH):
                b = m % 2
                C.bank_wait(b)
                for kc in range(NCH):
                    C.mm(C.pb[b][:, 0:TB], C.wchunk(), C.YB[:, kc * TB:(kc + 1) * TB],
                         start=(kc == 0), stop=(kc == NCH - 1))
                v = C.stamp_pe()
                C.dve.wait_ge(C.sPE.h, v)
                if ip < 2:
                    o = C.BIG[:, ip * 4096 + m * TB: ip * 4096 + (m + 1) * TB]
                else:
                    o = C.MID[:, m * TB:(m + 1) * TB]
                vd = C.sDVE.inc(C.dve.tensor_scalar(o, C.pb[b][:, 0:TB], db(bcol + m), None, ALU.add))
                C.bank_guard(b, C.sDVE, vd)
        C.barrier()
        SCALE = float(1.0 / np.sqrt(128.0))
        EX4 = R[:, S_RT:S_RT + 256]
        PRT = R[:, S_RT + 256:S_RT + 512]
        MX4 = R[:, S_RT + 512:S_RT + 516]
        NM4 = R[:, S_RT + 516:S_RT + 520]
        SM4 = R[:, S_RT + 520:S_RT + 524]
        RC4 = R[:, S_RT + 524:S_RT + 528]
        for blk in range(8):
            qb = lambda h: C.BIG[:, h * TB + blk * 64: h * TB + blk * 64 + 64]
            kb = lambda h: C.BIG[:, 4096 + h * TB + blk * 64: 4096 + h * TB + blk * 64 + 64]
            vb = lambda h: C.MID[:, h * TB + blk * 64: h * TB + blk * 64 + 64]
            C.bank_wait(6)
            for hp in range(4):
                C.mm(C.pb[6][0:64, hp * 64:(hp + 1) * 64], qb(2 * hp), kb(2 * hp),
                     start=True, stop=True)
                C.mm(C.pb[6][64:128, hp * 64:(hp + 1) * 64], qb(2 * hp + 1), kb(2 * hp + 1),
                     start=True, stop=True)
            v = C.stamp_pe()
            C.dve.wait_ge(C.sPE.h, v)
            C.dop(C.dve.tensor_reduce(MX4, C.pb[6][:, 0:256].rearrange("p (a b) -> p a b", a=4),
                                      AX.X, ALU.max))
            vd = C.dop(C.dve.tensor_scalar(NM4, MX4, -SCALE, None, ALU.mult))
            C.act.wait_ge(C.sDVE.h, vd)
            ia = None
            for hp in range(4):
                ia = C.act.activation(EX4[:, hp * 64:(hp + 1) * 64], C.pb[6][:, hp * 64:(hp + 1) * 64],
                                      AF.Exp, bias=NM4[:, hp:hp + 1], scale=SCALE)
            va = C.sACT.inc(ia)
            C.bank_guard(6, C.sACT, va)
            C.dve.wait_ge(C.sACT.h, va)
            C.dop(C.dve.tensor_reduce(SM4, EX4.rearrange("p (a b) -> p a b", a=4), AX.X, ALU.add))
            C.dop(C.dve.reciprocal(RC4, SM4))
            ii = None
            for hp in range(4):
                ii = C.dve.tensor_single_scalar(PRT[:, hp * 64:(hp + 1) * 64],
                                                EX4[:, hp * 64:(hp + 1) * 64],
                                                RC4[:, hp:hp + 1], ALU.mult)
            vd = C.dop(ii)
            C.bank_wait(7)
            C.pe.wait_ge(C.sDVE.h, vd)
            for hp in range(4):
                C.tr(C.pb[7][0:64, hp * 128:(hp + 1) * 128], PRT[:, hp * 64:(hp + 1) * 64],
                     C.ident_f32)
            v = C.stamp_pe()
            C.act.wait_ge(C.sPE.h, v)
            prT = C.SC2[0:64, 0:512]
            va = C.sACT.inc(C.act.copy(prT, C.pb[7][0:64, 0:512]))
            C.bank_guard(7, C.sACT, va)
            vT = C.SC2[0:64, 512:1536]
            for g in range(2):
                C.bank_wait(5)
                for j in range(4):
                    h = g * 4 + j
                    C.tr(C.pb[5][0:64, j * 128:(j + 1) * 128], vb(h), C.ident_f32)
                v = C.stamp_pe()
                C.act.wait_ge(C.sPE.h, v)
                va = C.sACT.inc(C.act.copy(vT[0:64, g * 512:(g + 1) * 512], C.pb[5][0:64, 0:512]))
                C.bank_guard(5, C.sACT, va)
            C.bank_wait(4)
            C.pe.wait_ge(C.sACT.h, C.sACT.n)
            for h in range(8):
                C.mm(C.pb[4][:, h * 64:(h + 1) * 64],
                     vT[0:64, h * 128:(h + 1) * 128], prT[0:64, h * 64:(h + 1) * 64],
                     start=True, stop=True)
            v = C.stamp_pe()
            C.act.wait_ge(C.sPE.h, v)
            ia = None
            for h in range(8):
                ia = C.act.copy(C.ZH[:, h * TB + blk * 64: h * TB + blk * 64 + 64],
                                C.pb[4][:, h * 64:(h + 1) * 64])
            va = C.sACT.inc(ia)
            C.bank_guard(4, C.sACT, va)
        C.pe.wait_ge(C.sACT.h, C.sACT.n)
        for m in range(NCH):
            b = 2 + (m % 2)
            C.bank_wait(b)
            for kc in range(NCH):
                C.mm(C.pb[b][:, 0:TB], C.wchunk(), C.ZH[:, kc * TB:(kc + 1) * TB],
                     start=(kc == 0), stop=(kc == NCH - 1))
            v = C.stamp_pe()
            C.dve.wait_ge(C.sPE.h, v)
            x_ap = xc(m)
            vd = C.sDVE.inc(C.dve.scalar_tensor_tensor(
                x_ap, C.pb[b][:, 0:TB], db(24 + m), x_ap, ALU.add, ALU.add))
            C.bank_guard(b, C.sDVE, vd)

def queue_stage_E(C):
    for tb in range(NTB):
        for nm in ("e_wq", "e_wk", "e_wv", "e_wo"):
            C.ring_queue(nm, 0, 64)

STAGES = {}


ALWAYS_KEYS = ("a_gate", "d_gate", "b_wqk", "c_ident_bf", "c_ident_f32", "c_ones_bf",
               "c_ones_f32", "c_onesrow", "c_sel8", "c_hsel")

def build_unfused_stage(emit_fn, queue_fn, arrs, keys=None, extra=None):
    nc = bass.Bass("TRN2", target_bir_lowering=False, debug=False)
    stk = ExitStack()
    with stk:
        if keys is None:
            shapes = dict(arrs)
        else:
            shapes = {k: arrs[k] for k in list(keys) + list(ALWAYS_KEYS)}
        shapes["xin"] = np.zeros((P, NCH * T), np.float32)
        C = Ctx(nc, stk, shapes)
        C.xout = nc.dram_tensor("xout", [P, NCH * T], F32, kind="ExternalOutput").ap()
        if extra == "b_sums":
            C.sums_out = nc.dram_tensor("sums_out", [P, 18], F32, kind="ExternalOutput").ap()
        elif extra == "b_full":
            asum_d = nc.dram_tensor("allsums", [P, 144], F32, kind="ExternalInput").ap()
            pm_d = nc.dram_tensor("pmask", [P, 8], F32, kind="ExternalInput").ap()
            C.ASUM = C.SMALL[:, 3200:3344]
            C.PMASK = C.SMALL[:, 3344:3352]
            C.sX.inc(C.pool.dma_start(C.ASUM, asum_d))
            C.sX.inc(C.pool.dma_start(C.PMASK, pm_d))
        C.setup()
        C.sX.inc(C.pool.dma_start(C.X[:, :], C.dram["xin"]))
        for eng in (C.pe, C.act, C.dve, C.pool, C.sp):
            C.sX.wait(eng)
        queue_fn(C)
        emit_fn(C)
        C.barrier()
        C.sX.inc(C.pool.dma_start(C.xout, C.X[:, :]))
        C.sX.wait(C.pool)
    return nc

_PREPPED = {}

def _get_shared(inputs):
    key = id(inputs.get("L0_attn_x"))
    if key not in _PREPPED:
        _PREPPED.clear()
        _PREPPED[key] = prep_shared(inputs)
    return _PREPPED[key]

K_A = ["a_w1", "a_w2", "a_ws1", "a_ws2"]
K_B1 = ["b_wv", "b_ln"]
K_B2 = ["b_wv", "b_wo", "b_w1", "b_w2", "b_ln", "b_b1", "b_b2"]
K_D = ["d_w1", "d_w2", "d_kw0", "d_kw1", "d_bias"]
K_E = ["e_wq", "e_wk", "e_wv", "e_wo", "e_bias"]

def _pmask(c):
    m = np.zeros((P, 8), np.float32)
    for r in range((c // 4) * 4, c):
        m[:, r] = 1.0
    return m

_NCS = {}

def _get_ncs(shared):
    if "A" not in _NCS:
        _NCS["A"] = build_unfused_stage(emit_stage_A, queue_stage_A, shared, K_A)
        _NCS["B1"] = build_unfused_stage(lambda C: emit_stage_B(C, "sums"),
                                         lambda C: queue_stage_B(C, "sums"),
                                         shared, K_B1, extra="b_sums")
        _NCS["B2"] = build_unfused_stage(lambda C: emit_stage_B(C, "full"),
                                         lambda C: queue_stage_B(C, "full"),
                                         shared, K_B2, extra="b_full")
        _NCS["D"] = build_unfused_stage(emit_stage_D, queue_stage_D, shared, K_D)
        _NCS["E"] = build_unfused_stage(emit_stage_E, queue_stage_E, shared, K_E)
    return _NCS

def kernel(**inputs):
    shared = _get_shared(inputs)
    ncs = _get_ncs(shared)
    x = np.asarray(inputs["L0_attn_x"], np.float32)
    xs = [x_fm(x[c // 4, (c % 4) * T:(c % 4 + 1) * T, :]) for c in range(8)]
    cores = list(range(8))

    def run(name, keys, extra_fn=None):
        base = {k: shared[k] for k in keys + list(ALWAYS_KEYS)}
        maps = []
        for c in cores:
            m = dict(base)
            m["xin"] = xs[c]
            if extra_fn is not None:
                m.update(extra_fn(c))
            maps.append(m)
        return run_bass_kernel_spmd(ncs[name], maps, core_ids=cores)

    for st in ("A", "B", "A", "D", "E", "D"):
        if st == "B":
            r1 = run("B1", K_B1)
            allsums = np.concatenate([r1.results[c]["sums_out"] for c in cores], axis=1)
            res = run("B2", K_B2, lambda c: dict(allsums=allsums, pmask=_pmask(c)))
        else:
            res = run(st, {"A": K_A, "D": K_D, "E": K_E}[st])
        xs = [np.asarray(res.results[c]["xout"]) for c in cores]

    out = np.empty((2, 8192, D), np.float32)
    for c in cores:
        out[c // 4, (c % 4) * T:(c % 4 + 1) * T, :] = x_un(xs[c])
    return out


# version 02 (final)
# speedup vs baseline: 1.0394x; 1.0394x over previous
import numpy as np
import ml_dtypes
from contextlib import ExitStack

import concourse.bass as bass
import concourse.mybir as mybir
from concourse.bass_utils import run_bass_kernel_spmd

F32 = mybir.dt.float32
BF16 = mybir.dt.bfloat16
I32 = mybir.dt.int32
AF = mybir.ActivationFunctionType
ALU = mybir.AluOpType
AX = mybir.AxisListType
BF = ml_dtypes.bfloat16

P = 128
T = 2048
TB = 512
NTB = T // TB
NCH = 8
D = 1024
GRID = np.linspace(-1.2, 0.2, 8).astype(np.float32)
LN_EPS = 1e-5
COS_EPS = 1e-6
PI = float(np.pi)
RQ2PI = float(np.sqrt(2.0 / np.pi))
GELU_C = 0.044715
B2_STOP = None


def _mk(w):
    w = np.asarray(w, np.float32)
    K, M = w.shape
    KC, MC = K // P, M // P
    a = w.astype(BF).reshape(KC, P, MC, P)
    return np.ascontiguousarray(a.transpose(1, 2, 0, 3).reshape(P, MC * KC * P))

def _blocks(w, rcs):
    w = np.asarray(w, np.float32)
    out = np.empty((P, len(rcs) * P), BF)
    for i, (r, c) in enumerate(rcs):
        out[:, i * P:(i + 1) * P] = w[r:r + P, c:c + P].astype(BF)
    return out

def _vec(v, nch=None):
    v = np.asarray(v, np.float32)
    n = v.shape[0] // P if nch is None else nch
    return np.ascontiguousarray(v.reshape(n, P).T)

def _smallmat(w, ncols):
    w = np.asarray(w, np.float32)
    KC = w.shape[0] // P
    return np.ascontiguousarray(w.astype(BF).reshape(KC, P, ncols).transpose(1, 0, 2).reshape(P, KC * ncols))

def prep_shared(inp):
    d = {}
    d["a_gate"] = _smallmat(np.asarray(inp["L0_ffn_gate_w"], np.float32).T, 8)
    W1 = np.asarray(inp["L0_ffn_W1"], np.float32)
    W2 = np.asarray(inp["L0_ffn_W2"], np.float32)
    d["a_w1"] = np.concatenate([_mk(W1[e].T) for e in range(8)], axis=1)
    d["a_w2"] = np.concatenate([_mk(W2[e].T) for e in range(8)], axis=1)
    d["a_ws1"] = _mk(np.asarray(inp["L0_ffn_Ws1"], np.float32).T)
    d["a_ws2"] = _mk(np.asarray(inp["L0_ffn_Ws2"], np.float32).T)
    WQ = np.asarray(inp["L0_attn_W_Q"], np.float32)[:, :, 0].T
    WK = np.asarray(inp["L0_attn_W_K"], np.float32)[:, :, 0].T
    d["b_wqk"] = np.concatenate([_smallmat(WQ, 16), _smallmat(WK, 16)], axis=1)
    WV = np.asarray(inp["L0_attn_W_V"], np.float32)
    d["b_wv"] = _mk(WV.transpose(1, 0, 2).reshape(D, D))
    WO = np.asarray(inp["L0_attn_W_O"], np.float32)
    d["b_wo"] = _mk(WO.transpose(2, 1, 0).reshape(D, D))
    d["b_ln"] = np.concatenate([
        _vec(inp["L0_attn_ln1_w"]), _vec(inp["L0_attn_ln1_b"]),
        _vec(inp["L0_attn_ln2_w"]), _vec(inp["L0_attn_ln2_b"])], axis=1)
    d["b_w1"] = _mk(np.asarray(inp["L0_attn_W1"], np.float32))
    d["b_b1"] = _vec(inp["L0_attn_b1"])
    d["b_w2"] = _mk(np.asarray(inp["L0_attn_W2"], np.float32))
    d["b_b2"] = _vec(inp["L0_attn_b2"])
    d["d_gate"] = _smallmat(np.asarray(inp["L1_ffn_gate_W"], np.float32), 8)
    mW1 = np.asarray(inp["L1_ffn_mlp_W1"], np.float32)
    mW2 = np.asarray(inp["L1_ffn_mlp_W2"], np.float32)
    d["d_w1"] = np.concatenate([_mk(mW1[e]) for e in range(4)], axis=1)
    d["d_w2"] = np.concatenate([_mk(mW2[e]) for e in range(4)], axis=1)
    d["d_bias"] = np.concatenate(
        [np.concatenate([_vec(inp["L1_ffn_mlp_b1"][e], 32) for e in range(4)], axis=1),
         np.concatenate([_vec(inp["L1_ffn_mlp_b2"][e], 8) for e in range(4)], axis=1),
         np.concatenate([_vec(inp["L1_ffn_kan_ln_g0"][e], 8) for e in range(4)], axis=1),
         np.concatenate([_vec(inp["L1_ffn_kan_ln_b0"][e], 8) for e in range(4)], axis=1),
         np.concatenate([_vec(inp["L1_ffn_kan_ln_g1"][e], 16) for e in range(4)], axis=1),
         np.concatenate([_vec(inp["L1_ffn_kan_ln_b1"][e], 16) for e in range(4)], axis=1),
         np.concatenate([_vec(inp["L1_ffn_kan_sl_b0"][e], 16) for e in range(4)], axis=1),
         np.concatenate([_vec(inp["L1_ffn_kan_sl_b1"][e], 8) for e in range(4)], axis=1),
         ], axis=1)
    kw0 = np.asarray(inp["L1_ffn_kan_sl_W0"], np.float32)
    kw1 = np.asarray(inp["L1_ffn_kan_sl_W1"], np.float32)
    o0 = [(kh * 4096 + kc * P, m * P) for kh in range(2) for m in range(16) for kc in range(32)]
    o1 = [(q * 4096 + kc * P, m * P) for q in range(4) for m in range(8) for kc in range(32)]
    d["d_kw0"] = np.concatenate(
        [_blocks(kw0[e].reshape(D, 8, 2048).transpose(1, 0, 2).reshape(8192, 2048), o0) for e in range(4)], axis=1)
    d["d_kw1"] = np.concatenate(
        [_blocks(kw1[e].reshape(2048, 8, D).transpose(1, 0, 2).reshape(16384, D), o1) for e in range(4)], axis=1)
    for nm in ("q", "k", "v", "o"):
        d[f"e_w{nm}"] = _mk(np.asarray(inp[f"L1_attn_W{nm}"], np.float32))
    d["e_bias"] = np.concatenate([_vec(inp[f"L1_attn_b{nm}"]) for nm in ("q", "k", "v", "o")], axis=1)
    d["c_ident_bf"] = np.eye(P).astype(BF)
    d["c_ident_f32"] = np.eye(P).astype(np.float32)
    d["c_ones_bf"] = np.ones((P, 1), BF)
    d["c_ones_f32"] = np.ones((P, 1), np.float32)
    d["c_onesrow"] = np.ones((1, P), np.float32)
    sel8 = np.zeros((8, 8 * P), BF)
    for e in range(8):
        sel8[e, e * P:(e + 1) * P] = 1.0
    d["c_sel8"] = sel8
    hsel = np.zeros((16, NCH * P), np.float32)
    for c in range(NCH):
        for p in range(P):
            hsel[2 * c + (p >= 64), c * P + p] = 1.0
    d["c_hsel"] = hsel
    return d

def x_fm(xslice):
    return np.ascontiguousarray(
        np.asarray(xslice, np.float32).reshape(T, NCH, P).transpose(2, 1, 0).reshape(P, NCH * T))

def x_un(fm):
    return np.ascontiguousarray(np.asarray(fm).reshape(P, NCH, T).transpose(2, 1, 0).reshape(T, D))

def _dt_of(a):
    if a.dtype == BF:
        return BF16
    if a.dtype == np.int32:
        return I32
    return F32


class Sem:
    def __init__(self, C, name, step=1):
        self.h = C.stk.enter_context(C.nc.semaphore(name))
        self.n = 0
        self.step = step

    def inc(self, instr, k=1):
        instr.then_inc(self.h, self.step * k)
        self.n += self.step * k
        return self.n

    def wait(self, eng, val=None):
        eng.wait_ge(self.h, self.n if val is None else val)


RCS = 16
RING_N = 4

C_IDENT, C_ONES, C_ONESROW, C_SEL, C_BIAS, C_HSEL = 0, 128, 129, 257, 1281, 2048
CB_IDENT, CB_ONES, CB_AGATE, CB_DGATE, CB_WQK, CB_SEL8 = 0, 128, 129, 193, 257, 513
S_MROW, S_TROW, S_RROW, S_TMP = 0, 512, 1024, 1536
S_LNSC = 2048
S_RT = 3072


class Ctx:
    def __init__(self, nc, stk, arrs):
        self.nc = nc
        self.stk = stk
        self.pe, self.act, self.dve, self.pool, self.sp = \
            nc.tensor, nc.scalar, nc.vector, nc.gpsimd, nc.sync
        self.dram = {k: nc.dram_tensor(k, list(v.shape), _dt_of(v), kind="ExternalInput").ap()
                     for k, v in arrs.items()}
        self.sW = Sem(self, "sW", 16)
        self.sX = Sem(self, "sX", 16)
        self.sPE = Sem(self, "sPE")
        self.sACT = Sem(self, "sACT")
        self.sDVE = Sem(self, "sDVE")
        self.sPOOL = Sem(self, "sPOOL")
        e = stk.enter_context
        self.X = e(nc.sbuf_tensor("Xr", [P, NCH * T], F32))
        self.BIG = e(nc.sbuf_tensor("BIGa", [P, 16384], BF16))
        self.ZF = e(nc.sbuf_tensor("ZFa", [P, 4096], F32))
        self.MID = e(nc.sbuf_tensor("MIDa", [P, 4096], F32))
        self.SMALL = e(nc.sbuf_tensor("SMALLa", [P, 4096], F32))
        self.YB = e(nc.sbuf_tensor("YBa", [P, 4096], BF16))
        self.LNY = e(nc.sbuf_tensor("LNYa", [P, 2048], BF16))
        self.ZH = e(nc.sbuf_tensor("ZHa", [P, 8192], BF16))
        self.WR = e(nc.sbuf_tensor("WRa", [P, RING_N * RCS * P], BF16))
        self.CST = e(nc.sbuf_tensor("CSTa", [P, 3072], F32))
        self.CSTB = e(nc.sbuf_tensor("CSTBa", [P, 1600], BF16))
        self.SC2 = e(nc.sbuf_tensor("SC2a", [P, 2048], BF16))
        self.pb = [e(nc.psum_tensor(f"pb{i}", [P, 512], F32)) for i in range(8)]
        self.bankg = {}
        self.r_fed = 0
        self.r_used = 0
        self.r_pending = []
        self.r_slot_target = [0] * RING_N

        self.last_mm = None
        self.mm_unstamped = 0

    def stamp_pe(self):
        if self.last_mm is not None and self.mm_unstamped > 0:
            self.sPE.inc(self.last_mm)
            self.mm_unstamped = 0
        return self.sPE.n

    def mm(self, out, lhsT, rhs, start, stop):
        i = self.pe.matmul(out, lhsT, rhs, start=start, stop=stop)
        self.last_mm = i
        self.mm_unstamped += 1
        return i

    def tr(self, out, in_, ident):
        i = self.pe.transpose(out, in_, ident)
        self.last_mm = i
        self.mm_unstamped += 1
        return i

    def dop(self, instr):
        v = self.sDVE.inc(instr)
        self.dve.wait_ge(self.sDVE.h, v)
        return v

    def aop(self, instr):
        v = self.sACT.inc(instr)
        self.act.wait_ge(self.sACT.h, v)
        return v

    def bank_wait(self, i):
        g = self.bankg.get(i)
        if g is not None:
            self.pe.wait_ge(g[0], g[1])

    def bank_guard(self, i, sem, val=None):
        self.bankg[i] = (sem.h, sem.n if val is None else val)

    def barrier(self):
        self.stamp_pe()
        sems = [self.sW, self.sX, self.sPE, self.sACT, self.sDVE, self.sPOOL]
        for eng in (self.pe, self.act, self.dve, self.pool, self.sp):
            for s in sems:
                if s.n > 0:
                    eng.wait_ge(s.h, s.n)

    def setup(self):
        g = self.pool
        dm = self.dram
        for dst, src in [
            (self.CSTB[:, CB_IDENT:CB_IDENT + 128], dm["c_ident_bf"]),
            (self.CSTB[:, CB_ONES:CB_ONES + 1], dm["c_ones_bf"]),
            (self.CSTB[:, CB_AGATE:CB_AGATE + 64], dm["a_gate"]),
            (self.CSTB[:, CB_DGATE:CB_DGATE + 64], dm["d_gate"]),
            (self.CSTB[:, CB_WQK:CB_WQK + 256], dm["b_wqk"]),
            (self.CSTB[0:8, CB_SEL8:CB_SEL8 + 1024].rearrange("p (a b) -> p a b", a=8),
             dm["c_sel8"].rearrange("p (a b) -> p a b", a=8)),
            (self.CST[:, C_IDENT:C_IDENT + 128], dm["c_ident_f32"]),
            (self.CST[:, C_ONES:C_ONES + 1], dm["c_ones_f32"]),
            (self.CST[0:1, C_ONESROW:C_ONESROW + 128], dm["c_onesrow"]),
            (self.CST[0:16, C_HSEL:C_HSEL + 1024].rearrange("p (a b) -> p a b", a=8),
             dm["c_hsel"].rearrange("p (a b) -> p a b", a=8)),
        ]:
            self.sX.inc(g.dma_start(dst, src))
        self.ident_bf = self.CSTB[:, CB_IDENT:CB_IDENT + 128]
        self.ident_f32 = self.CST[:, C_IDENT:C_IDENT + 128]
        self.ones_bf = self.CSTB[:, CB_ONES:CB_ONES + 1]
        self.ones_f32 = self.CST[:, C_ONES:C_ONES + 1]
        self.onesrow = self.CST[0:1, C_ONESROW:C_ONESROW + 128]
        cp = self.CST[:, 2016 - C_BIAS + C_BIAS:2032]
        cp = self.CST[:, 2016:2032]
        i = None
        for j, v in enumerate([LN_EPS, PI / 2.0, 1.0] + [-float(g) for g in GRID]):
            i = self.dve.memset(cp[:, j:j + 1], float(v))
            self.nc.const_aps.register(F32, float(v), cp[:, j:j + 1])
        self.sDVE.inc(i)
        for eng in (self.pe, self.act, self.dve, self.pool, self.sp):
            self.sX.wait(eng)
            self.sDVE.wait(eng)

    def load_bias(self, name, ncols, col_off=0, src_col=0):
        dst = self.CST[:, C_BIAS + col_off: C_BIAS + col_off + ncols]
        src = self.dram[name]
        if src.shape[1] != ncols or src_col:
            src = src[:, src_col:src_col + ncols]
        self.sX.inc(self.pool.dma_start(dst, src))
        return dst

    def bias(self, col, n=1):
        return self.CST[:, C_BIAS + col: C_BIAS + col + n]

    def ring_queue(self, name, chunk0, nchunks):
        assert nchunks % RCS == 0, nchunks
        self.r_pending.append([self.dram[name], chunk0, chunk0 + nchunks])
        self._ring_fill()

    def _ring_fill(self):
        while self.r_pending and self.r_fed < self.r_used - (self.r_used % RCS) + RING_N * RCS:
            reg = self.r_pending[0]
            dram, nxt, end = reg
            n = min(RCS, end - nxt)
            slot = (self.r_fed // RCS) % RING_N
            self.sp.wait_ge(self.sPE.h, self.sPE.n)
            dst = self.WR[:, slot * RCS * P:(slot * RCS + n) * P]
            src = dram[:, nxt * P:(nxt + n) * P]
            self.r_slot_target[slot] = self.sW.inc(self.sp.dma_start(dst, src))
            self.r_fed += n
            reg[1] += n
            if reg[1] >= end:
                self.r_pending.pop(0)

    def wchunk(self):
        i = self.r_used
        slot = (i // RCS) % RING_N
        off = i % RCS
        if off == 0:
            self.stamp_pe()
            self._ring_fill()
            assert self.r_fed > i, "ring underflow: consumption passed queued chunks"
            self.pe.wait_ge(self.sW.h, self.r_slot_target[slot])
        self.r_used += 1
        return self.WR[:, (slot * RCS + off) * P:(slot * RCS + off + 1) * P]

    def ln(self, src, dst, nch, W=TB, affine=None, also_f32=None, src_bf=False):
        pe, act, dve = self.pe, self.act, self.dve
        NF = float(nch * P)
        ps1, ps2, pm, pr = self.pb[2], self.pb[3], self.pb[4], self.pb[5]
        act.wait_ge(self.sPE.h, self.stamp_pe())
        act.wait_ge(self.sDVE.h, self.sDVE.n)
        pe.wait_ge(self.sDVE.h, self.sDVE.n)
        pe.wait_ge(self.sACT.h, self.sACT.n)
        self.bank_wait(2), self.bank_wait(3)
        sq = self.SC2
        ones1 = self.ones_bf if src_bf else self.ones_f32
        pe_stamps = []
        for c in range(nch):
            s = src(c)
            sqv = sq[:, (c % 2) * 512:(c % 2) * 512 + W]
            if c >= 2:
                act.wait_ge(self.sPE.h, pe_stamps[c - 2])
            va = self.sACT.inc(act.activation(sqv, s, AF.Square))
            self.mm(ps1[0:1, 0:W], ones1, s, start=(c == 0), stop=(c == nch - 1))
            pe.wait_ge(self.sACT.h, va)
            self.mm(ps2[0:1, 0:W], self.ones_bf, sqv, start=(c == 0), stop=(c == nch - 1))
            pe_stamps.append(self.stamp_pe())
        vpe = self.sPE.n
        R = self.SMALL
        m_row = R[0:1, S_MROW:S_MROW + W]
        t_row = R[0:1, S_TROW:S_TROW + W]
        r_row = R[0:1, S_RROW:S_RROW + W]
        tmp = R[0:1, S_TMP:S_TMP + W]
        act.wait_ge(self.sPE.h, vpe)
        va = self.sACT.inc(act.mul(m_row, ps1[0:1, 0:W], 1.0 / NF))
        dve.wait_ge(self.sACT.h, va)
        dve.wait_ge(self.sPE.h, vpe)
        self.dop(dve.tensor_scalar(t_row, ps2[0:1, 0:W], 1.0 / NF, None, ALU.mult))
        self.dop(dve.tensor_tensor(tmp, m_row, m_row, ALU.mult))
        vd = self.dop(dve.tensor_tensor(t_row, t_row, tmp, ALU.subtract))
        self.bank_guard(2, self.sACT)
        self.bank_guard(3, self.sDVE)
        act.wait_ge(self.sDVE.h, vd)
        va = self.sACT.inc(act.activation(t_row, t_row, AF.Sqrt, bias=LN_EPS))
        dve.wait_ge(self.sACT.h, va)
        vd = self.dop(dve.reciprocal(r_row, t_row))
        self.bank_wait(4), self.bank_wait(5)
        pe.wait_ge(self.sDVE.h, vd)
        self.mm(pm[:, 0:W], self.onesrow, m_row, start=True, stop=True)
        self.mm(pr[:, 0:W], self.onesrow, r_row, start=True, stop=True)
        vb = self.stamp_pe()
        dve.wait_ge(self.sPE.h, vb)
        i = None
        for c in range(nch):
            s = src(c)
            o = dst(c)
            tscr = R[:, S_LNSC + (c % 2) * 512: S_LNSC + (c % 2) * 512 + W]
            dve.tensor_tensor(tscr, s, pm[:, 0:W], ALU.subtract)
            if affine is None:
                i = dve.tensor_tensor(o, tscr, pr[:, 0:W], ALU.mult)
            else:
                dve.tensor_tensor(tscr, tscr, pr[:, 0:W], ALU.mult)
                w_ap, b_ap = affine(c)
                i = dve.tensor_scalar(o, tscr, w_ap, b_ap, ALU.mult, ALU.add)
            if also_f32 is not None:
                self.sDVE.inc(i)
                i = dve.tensor_copy(also_f32(c), o)
        vd = self.sDVE.inc(i)
        self.bank_guard(4, self.sDVE)
        self.bank_guard(5, self.sDVE)
        return vd

    def routing(self, gate_ap, renorm, scale=0.5):
        pe, act, dve = self.pe, self.act, self.dve
        pg = self.pb[6]
        pt = self.pb[7]
        self.bank_wait(6)
        for kc in range(NCH):
            self.mm(pg[0:8, 0:TB], gate_ap[:, kc * 8:(kc + 1) * 8],
                    self.YB[:, kc * TB:(kc + 1) * TB], start=(kc == 0), stop=(kc == NCH - 1))
        v = self.stamp_pe()
        act.wait_ge(self.sPE.h, v)
        R = self.SMALL
        LGF = R[0:8, S_RT + 256:S_RT + 768]
        va = self.sACT.inc(act.copy(LGF, pg[0:8, 0:TB]))
        self.bank_guard(6, self.sACT)
        self.bank_wait(7)
        pe.wait_ge(self.sACT.h, va)
        for i in range(4):
            self.tr(pt[:, i * 8:(i + 1) * 8],
                    LGF[0:8, i * P:(i + 1) * P], self.ident_f32[0:8, 0:8])
        v = self.stamp_pe()
        dve.wait_ge(self.sPE.h, v)
        LG = R[:, S_RT:S_RT + 32]
        self.dop(dve.tensor_copy(LG, pt[:, 0:32]))
        self.bank_guard(7, self.sDVE)
        v3 = lambda off: R[:, S_RT + off:S_RT + off + 32].rearrange("p (a b) -> p a b", a=4)
        MX = R[:, S_RT + 32:S_RT + 36]
        NM = R[:, S_RT + 36:S_RT + 40]
        SM = R[:, S_RT + 40:S_RT + 44]
        RC = R[:, S_RT + 44:S_RT + 48]
        M1 = R[:, S_RT + 48:S_RT + 52]
        M2 = R[:, S_RT + 52:S_RT + 56]
        EX = R[:, S_RT + 64:S_RT + 96]
        PRb = R[:, S_RT + 96:S_RT + 128]
        GE = R[:, S_RT + 128:S_RT + 160]
        KP = R[:, S_RT + 160:S_RT + 192]
        WT = R[:, S_RT + 192:S_RT + 224]
        self.dop(dve.tensor_reduce(MX, LG.rearrange("p (a b) -> p a b", a=4), AX.X, ALU.max))
        vd = self.dop(dve.tensor_scalar(NM, MX, -1.0, None, ALU.mult))
        act.wait_ge(self.sDVE.h, vd)
        ia = None
        for i in range(4):
            ia = act.activation(EX[:, i * 8:(i + 1) * 8], LG[:, i * 8:(i + 1) * 8],
                                AF.Exp, bias=NM[:, i:i + 1])
        va = self.sACT.inc(ia)
        dve.wait_ge(self.sACT.h, va)
        self.dop(dve.tensor_reduce(SM, EX.rearrange("p (a b) -> p a b", a=4), AX.X, ALU.add))
        self.dop(dve.reciprocal(RC, SM))
        ii = None
        for i in range(4):
            ii = dve.tensor_single_scalar(PRb[:, i * 8:(i + 1) * 8], EX[:, i * 8:(i + 1) * 8],
                                          RC[:, i:i + 1], ALU.mult)
        self.dop(ii)
        self.dop(dve.tensor_reduce(M1, PRb.rearrange("p (a b) -> p a b", a=4), AX.X, ALU.max))
        for i in range(4):
            ii = dve.tensor_single_scalar(GE[:, i * 8:(i + 1) * 8], PRb[:, i * 8:(i + 1) * 8],
                                          M1[:, i:i + 1], ALU.is_ge)
        self.dop(ii)
        self.dop(dve.tensor_tensor(GE, PRb, GE, ALU.subtract))
        self.dop(dve.tensor_reduce(M2, GE.rearrange("p (a b) -> p a b", a=4), AX.X, ALU.max))
        for i in range(4):
            ii = dve.tensor_single_scalar(KP[:, i * 8:(i + 1) * 8], PRb[:, i * 8:(i + 1) * 8],
                                          M2[:, i:i + 1], ALU.is_ge)
        self.dop(ii)
        if renorm:
            self.dop(dve.tensor_tensor(MX, M1, M2, ALU.add))
            self.dop(dve.reciprocal(RC, MX))
            self.dop(dve.tensor_scalar(RC, RC, scale, None, ALU.mult))
            self.dop(dve.tensor_tensor(WT, PRb, KP, ALU.mult))
            for i in range(4):
                ii = dve.tensor_single_scalar(WT[:, i * 8:(i + 1) * 8], WT[:, i * 8:(i + 1) * 8],
                                              RC[:, i:i + 1], ALU.mult)
            self.dop(ii)
        else:
            self.dop(dve.scalar_tensor_tensor(WT, PRb, scale, KP, ALU.mult, ALU.mult))
        vd = self.sDVE.n
        self.bank_wait(7)
        pe.wait_ge(self.sDVE.h, vd)
        for i in range(4):
            self.tr(pt[0:8, i * P:(i + 1) * P], WT[:, i * 8:(i + 1) * 8], self.ident_f32)
        v = self.stamp_pe()
        act.wait_ge(self.sPE.h, v)
        wfm = self.SC2[0:8, 1024:1024 + TB]
        va = self.sACT.inc(act.copy(wfm, pt[0:8, 0:TB]))
        self.bank_guard(7, self.sACT)
        return wfm, va

    def w_bcast(self, wfm, wfm_ready, e, bank):
        self.bank_wait(bank)
        self.pe.wait_ge(self.sACT.h, wfm_ready)
        sel = self.CSTB[0:8, CB_SEL8 + e * P:CB_SEL8 + (e + 1) * P]
        self.mm(self.pb[bank][:, 0:TB], sel, wfm, start=True, stop=True)
        return self.stamp_pe()



def emit_stage_A(C):
    for tb in range(NTB):
        C.barrier()
        xc = lambda c, tb=tb: C.X[:, c * T + tb * TB: c * T + tb * TB + TB]
        yc = lambda c: C.YB[:, c * TB:(c + 1) * TB]
        vy = C.ln(xc, yc, NCH)
        for eng in (C.pe,):
            eng.wait_ge(C.sDVE.h, vy)
        gate = C.CSTB[:, CB_AGATE:CB_AGATE + 64]
        wfm, wv = C.routing(gate, renorm=False, scale=0.5)
        for e in range(8):
            vb = C.w_bcast(wfm, wv, e, 4)
            for m in range(12):
                b = m % 2
                C.bank_wait(b)
                for kc in range(NCH):
                    C.mm(C.pb[b][:, 0:TB], C.wchunk(), C.YB[:, kc * TB:(kc + 1) * TB],
                         start=(kc == 0), stop=(kc == NCH - 1))
                v = C.stamp_pe()
                C.act.wait_ge(C.sPE.h, v)
                if m < 6:
                    i = C.act.copy(C.BIG[:, m * TB:(m + 1) * TB], C.pb[b][:, 0:TB])
                else:
                    i = C.act.activation(C.BIG[:, 3072 + (m - 6) * TB:3072 + (m - 5) * TB],
                                         C.pb[b][:, 0:TB], AF.Silu)
                va = C.sACT.inc(i)
                C.bank_guard(b, C.sACT, va)
            C.dve.wait_ge(C.sACT.h, C.sACT.n)
            C.dve.wait_ge(C.sPE.h, C.sPE.n)
            ii = None
            for j in range(6):
                ys = C.BIG[:, j * TB:(j + 1) * TB]
                gs = C.BIG[:, 3072 + j * TB:3072 + (j + 1) * TB]
                C.dve.tensor_tensor(ys, ys, gs, ALU.mult)
                ii = C.dve.tensor_tensor(ys, ys, C.pb[4][:, 0:TB], ALU.mult)
            vd = C.sDVE.inc(ii)
            C.bank_guard(4, C.sDVE, vd)
            C.pe.wait_ge(C.sDVE.h, vd)
            for m in range(8):
                b = 2 + (m % 2)
                C.bank_wait(b)
                for kc in range(6):
                    C.mm(C.pb[b][:, 0:TB], C.wchunk(), C.BIG[:, kc * TB:(kc + 1) * TB],
                         start=(kc == 0), stop=(kc == 5))
                v = C.stamp_pe()
                C.dve.wait_ge(C.sPE.h, v)
                x_ap = xc(m)
                vd = C.sDVE.inc(C.dve.tensor_tensor(x_ap, x_ap, C.pb[b][:, 0:TB], ALU.add))
                C.bank_guard(b, C.sDVE, vd)
        for m in range(24):
            b = m % 2
            C.bank_wait(b)
            for kc in range(NCH):
                C.mm(C.pb[b][:, 0:TB], C.wchunk(), C.YB[:, kc * TB:(kc + 1) * TB],
                     start=(kc == 0), stop=(kc == NCH - 1))
            v = C.stamp_pe()
            C.act.wait_ge(C.sPE.h, v)
            if m < 12:
                i = C.act.mul(C.BIG[:, m * TB:(m + 1) * TB], C.pb[b][:, 0:TB], 0.5)
            else:
                i = C.act.activation(C.BIG[:, 6144 + (m - 12) * TB:6144 + (m - 11) * TB],
                                     C.pb[b][:, 0:TB], AF.Silu)
            va = C.sACT.inc(i)
            C.bank_guard(b, C.sACT, va)
        C.dve.wait_ge(C.sACT.h, C.sACT.n)
        ii = None
        for j in range(12):
            ys = C.BIG[:, j * TB:(j + 1) * TB]
            gs = C.BIG[:, 6144 + j * TB:6144 + (j + 1) * TB]
            ii = C.dve.tensor_tensor(ys, ys, gs, ALU.mult)
        vd = C.sDVE.inc(ii)
        C.pe.wait_ge(C.sDVE.h, vd)
        for m in range(8):
            b = 2 + (m % 2)
            C.bank_wait(b)
            for kc in range(12):
                C.mm(C.pb[b][:, 0:TB], C.wchunk(), C.BIG[:, kc * TB:(kc + 1) * TB],
                     start=(kc == 0), stop=(kc == 11))
            v = C.stamp_pe()
            C.dve.wait_ge(C.sPE.h, v)
            x_ap = xc(m)
            vd = C.sDVE.inc(C.dve.tensor_tensor(x_ap, x_ap, C.pb[b][:, 0:TB], ALU.add))
            C.bank_guard(b, C.sDVE, vd)

def queue_stage_A(C):
    for tb in range(NTB):
        for e in range(8):
            C.ring_queue("a_w1", e * 96, 96)
            C.ring_queue("a_w2", e * 48, 48)
        C.ring_queue("a_ws1", 0, 192)
        C.ring_queue("a_ws2", 0, 96)

DB_B1, DB_B2, DB_G0, DB_B0, DB_G1, DB_B1K, DB_KB0, DB_KB1, DB_GRID = \
    0, 128, 160, 192, 224, 288, 352, 416, 448

def _zc(C, j):
    return C.ZF[:, j * TB:(j + 1) * TB] if j < 8 else C.MID[:, (j - 8) * TB:(j - 7) * TB]

def emit_stage_D(C):
    db = lambda col, n=1: C.bias(col, n)
    C.load_bias("d_bias", 448)
    for eng in (C.act, C.dve, C.pe):
        C.sX.wait(eng)
    for tb in range(NTB):
        C.barrier()
        xc = lambda c, tb=tb: C.X[:, c * T + tb * TB: c * T + tb * TB + TB]
        yc = lambda c: C.YB[:, c * TB:(c + 1) * TB]
        vy = C.ln(xc, yc, NCH)
        C.pe.wait_ge(C.sDVE.h, vy)
        gate = C.CSTB[:, CB_DGATE:CB_DGATE + 64]
        wfm, wv = C.routing(gate, renorm=True, scale=0.5)
        for e in range(4):
            vb = C.w_bcast(wfm, wv, e, 4)
            C.act.wait_ge(C.sPE.h, vb)
            wmid = C.MID[:, 0:TB]
            g = C.bankg.get("wmid")
            if g is not None:
                C.act.wait_ge(g[0], g[1])
            va = C.sACT.inc(C.act.copy(wmid, C.pb[4][:, 0:TB]))
            C.bank_guard(4, C.sACT, va)
            for m in range(32):
                b = m % 2
                C.bank_wait(b)
                for kc in range(NCH):
                    C.mm(C.pb[b][:, 0:TB], C.wchunk(), C.YB[:, kc * TB:(kc + 1) * TB],
                         start=(kc == 0), stop=(kc == NCH - 1))
                v = C.stamp_pe()
                C.act.wait_ge(C.sPE.h, v)
                va = C.sACT.inc(C.act.activation(
                    C.BIG[:, m * TB:(m + 1) * TB], C.pb[b][:, 0:TB], AF.Gelu,
                    bias=db(DB_B1 + e * 32 + m)))
                C.bank_guard(b, C.sACT, va)
            C.pe.wait_ge(C.sACT.h, C.sACT.n)
            for m in range(8):
                b = 2 + (m % 2)
                C.bank_wait(b)
                for kc in range(32):
                    C.mm(C.pb[b][:, 0:TB], C.wchunk(), C.BIG[:, kc * TB:(kc + 1) * TB],
                         start=(kc == 0), stop=(kc == 31))
                v = C.stamp_pe()
                C.dve.wait_ge(C.sPE.h, v)
                C.dve.wait_ge(C.sACT.h, va)
                tt = C.SMALL[:, S_LNSC + (m % 2) * 512:S_LNSC + (m % 2) * 512 + TB]
                C.dve.scalar_tensor_tensor(tt, C.pb[b][:, 0:TB], db(DB_B2 + e * 8 + m),
                                           wmid, ALU.add, ALU.mult)
                x_ap = xc(m)
                vd = C.sDVE.inc(C.dve.tensor_tensor(x_ap, x_ap, tt, ALU.add))
                C.bank_guard(b, C.sDVE, vd)
            C.bankg["wmid"] = (C.sDVE.h, C.sDVE.n)
        C.ln(yc, yc, NCH, src_bf=True)
        for e in range(4):
            C.barrier()
            ii = None
            for c in range(NCH):
                for gidx in range(8):
                    ii = C.dve.tensor_scalar(db(DB_GRID + c * 8 + gidx), db(DB_B0 + e * 8 + c),
                                             float(GRID[gidx]), None, ALU.subtract)
            vgb = C.sDVE.inc(ii)
            C.act.wait_ge(C.sDVE.h, vgb)
            for kh in range(2):
                if kh > 0:
                    C.barrier()
                tile_st = _build_basis(C, src=yc, nsrc=NCH, kh_g0=kh * 4, ngrp=4,
                                       scale_bias=(lambda c: (db(DB_G0 + e * 8 + c),)),
                                       grid_bias=lambda c, gi: db(DB_GRID + c * 8 + gi))
                for m in range(16):
                    b = m % 2
                    C.bank_wait(b)
                    for kc in range(32):
                        if m == 0:
                            C.pe.wait_ge(C.sDVE.h, tile_st[kc])
                        C.mm(C.pb[b][:, 0:TB], C.wchunk(),
                             C.BIG[:, kc * TB:(kc + 1) * TB], start=(kc == 0), stop=(kc == 31))
                    v = C.stamp_pe()
                    zc = _zc(C, m)
                    if kh == 0:
                        C.act.wait_ge(C.sPE.h, v)
                        va = C.sACT.inc(C.act.copy(zc, C.pb[b][:, 0:TB]))
                        C.bank_guard(b, C.sACT, va)
                    else:
                        C.dve.wait_ge(C.sPE.h, v)
                        vd = C.sDVE.inc(C.dve.scalar_tensor_tensor(
                            zc, C.pb[b][:, 0:TB], db(DB_KB0 + e * 16 + m), zc, ALU.add, ALU.add))
                        C.bank_guard(b, C.sDVE, vd)
            C.barrier()
            C.ln(lambda j: _zc(C, j), lambda j: C.ZH[:, j * TB:(j + 1) * TB], 16,
                 affine=lambda j: (db(DB_G1 + e * 16 + j), db(DB_B1K + e * 16 + j)))
            vb = C.w_bcast(wfm, wv, 4 + e, 4)
            C.act.wait_ge(C.sPE.h, vb)
            C.barrier()
            wmid = C.MID[:, 0:TB]
            va = C.sACT.inc(C.act.copy(wmid, C.pb[4][:, 0:TB]))
            C.bank_guard(4, C.sACT, va)
            C.barrier()
            for q in range(4):
                tile_st = _build_basis(C, src=lambda j: C.ZH[:, j * TB:(j + 1) * TB],
                                       nsrc=16, kh_g0=q * 2, ngrp=2,
                                       scale_bias=None,
                                       grid_bias=lambda c, gi: C.CST[:, 2019 + gi:2020 + gi])
                for m in range(8):
                    for kc in range(32):
                        if q == 0 and m == 0:
                            C.pe.wait_ge(C.sDVE.h, tile_st[kc])
                        elif m == 0:
                            C.pe.wait_ge(C.sDVE.h, tile_st[kc])
                        C.mm(C.pb[m][:, 0:TB], C.wchunk(),
                             C.BIG[:, kc * TB:(kc + 1) * TB],
                             start=(q == 0 and kc == 0), stop=(q == 3 and kc == 31))
                if q < 3:
                    C.barrier()
            v = C.stamp_pe()
            C.dve.wait_ge(C.sPE.h, v)
            C.dve.wait_ge(C.sACT.h, C.sACT.n)
            for m in range(8):
                tt = C.SMALL[:, S_LNSC + (m % 2) * 512:S_LNSC + (m % 2) * 512 + TB]
                C.dve.scalar_tensor_tensor(tt, C.pb[m][:, 0:TB], db(DB_KB1 + e * 8 + m),
                                           wmid, ALU.add, ALU.mult)
                x_ap = xc(m)
                vd = C.sDVE.inc(C.dve.tensor_tensor(x_ap, x_ap, tt, ALU.add))
                C.bank_guard(m, C.sDVE, vd)
            C.barrier()

def _build_basis(C, src, nsrc, kh_g0, ngrp, scale_bias, grid_bias):
    stamps = []
    i = 0
    prev_act = {}
    for gg in range(ngrp):
        gidx = kh_g0 + gg
        for dc in range(nsrc):
            sg = C.SC2[:, (i % 2) * 512:(i % 2) * 512 + TB]
            tile = C.BIG[:, i * TB:(i + 1) * TB]
            if i >= 2:
                C.act.wait_ge(C.sDVE.h, stamps[i - 2])
            if scale_bias is not None:
                sc = scale_bias(dc)[0]
                ia = C.act.activation(sg, src(dc), AF.Sigmoid, bias=grid_bias(dc, gidx), scale=sc)
            else:
                ia = C.act.activation(sg, src(dc), AF.Sigmoid, bias=grid_bias(dc, gidx))
            va = C.sACT.inc(ia)
            C.dve.wait_ge(C.sACT.h, va)
            C.dve.tensor_scalar(tile, sg, -4.0, 4.0, ALU.mult, ALU.add)
            vd = C.sDVE.inc(C.dve.tensor_tensor(tile, tile, sg, ALU.mult))
            stamps.append(vd)
            i += 1
    return stamps

def queue_stage_D(C):
    for tb in range(NTB):
        for e in range(4):
            C.ring_queue("d_w1", e * 256, 256)
            C.ring_queue("d_w2", e * 256, 256)
        for e in range(4):
            C.ring_queue("d_kw0", e * 1024, 1024)
            C.ring_queue("d_kw1", e * 1024, 1024)

B_TRIG = 0
B_CAR = 3072
B_SUM = 3090
B_PRE = 3108
B_TMP = 3126

def emit_stage_B(C, mode, out_dma=True, carry_init=None):
    R = C.SMALL
    db = lambda col, n=1: C.bias(col, n)
    C.load_bias("b_ln", 32, col_off=500)
    if mode == "full":
        C.load_bias("b_b1", 32, col_off=532)
        C.load_bias("b_b2", 8, col_off=564)
    for eng in (C.act, C.dve, C.pe):
        C.sX.wait(eng)
    ln1 = lambda c: (C.bias(500 + c), C.bias(508 + c))
    ln2 = lambda c: (C.bias(516 + c), C.bias(524 + c))
    dve, act, pe = C.dve, C.act, C.pe
    if mode == "sums":
        ii = dve.memset(R[:, B_SUM:B_SUM + 18], 0.0)
        C.sDVE.inc(ii)
    elif carry_init is not None:
        carry_init(C)
    else:
        ii = dve.memset(R[:, B_PRE:B_PRE + 18], 0.0)
        C.dop(ii)
        AS = C.ASUM
        PM = C.PMASK
        for r in range(8):
            ii = dve.scalar_tensor_tensor(R[:, B_PRE:B_PRE + 18], AS[:, r * 18:(r + 1) * 18],
                                          PM[:, r:r + 1], R[:, B_PRE:B_PRE + 18],
                                          ALU.mult, ALU.add)
            C.dop(ii)
        C.dop(dve.tensor_copy(R[:, B_CAR:B_CAR + 18], R[:, B_PRE:B_PRE + 18]))
    tA, tB, tC, tD, tE, tF = (R[0:16, B_TRIG + i * 512:B_TRIG + (i + 1) * 512] for i in range(6))
    for tb in range(NTB):
        C.barrier()
        xc = lambda c, tb=tb: C.X[:, c * T + tb * TB: c * T + tb * TB + TB]
        yc = lambda c: C.YB[:, c * TB:(c + 1) * TB]
        y1c = lambda c: C.BIG[:, c * TB:(c + 1) * TB]
        vy = C.ln(xc, yc, NCH)
        C.dve.wait_ge(C.sDVE.h, vy)
        C.ln(yc, y1c, NCH, affine=ln1, src_bf=True)
        C.barrier()
        C.bank_wait(6), C.bank_wait(7)
        wq = C.CSTB[:, CB_WQK:CB_WQK + 128]
        wk = C.CSTB[:, CB_WQK + 128:CB_WQK + 256]
        for kc in range(NCH):
            C.mm(C.pb[6][0:16, 0:TB], wq[:, kc * 16:(kc + 1) * 16], y1c(kc),
                 start=(kc == 0), stop=(kc == NCH - 1))
        for kc in range(NCH):
            C.mm(C.pb[7][0:16, 0:TB], wk[:, kc * 16:(kc + 1) * 16], y1c(kc),
                 start=(kc == 0), stop=(kc == NCH - 1))
        v = C.stamp_pe()
        act.wait_ge(C.sPE.h, v)
        if mode == "full":
            va = C.sACT.inc(act.activation(tC, C.pb[6][0:16, 0:TB], AF.Tanh))
            act.wait_ge(C.sACT.h, va)
            va = C.sACT.inc(act.activation(tA, tC, AF.Sin, bias=PI / 2.0, scale=PI / 4.0))
            va = C.sACT.inc(act.activation(tB, tC, AF.Sin, scale=PI / 4.0))
            C.bank_guard(6, C.sACT, va)
        va = C.sACT.inc(act.activation(tC, C.pb[7][0:16, 0:TB], AF.Tanh))
        act.wait_ge(C.sACT.h, va)
        vck = C.sACT.inc(act.activation(tD, tC, AF.Sin, scale=PI / 4.0))
        act.wait_ge(C.sACT.h, vck)
        vcc = C.sACT.inc(act.activation(tC, tC, AF.Sin, bias=PI / 2.0, scale=PI / 4.0))
        C.bank_guard(7, C.sACT, vcc)
        dve.wait_ge(C.sACT.h, vcc)
        if mode == "full":
            C.dop(dve.tensor_tensor_scan(tE, tC, tC, R[0:16, B_CAR + 16:B_CAR + 17],
                                         ALU.add, ALU.bypass))
            C.dop(dve.tensor_copy(R[0:16, B_CAR + 16:B_CAR + 17], tE[0:16, 511:512]))
            C.dop(dve.tensor_tensor_scan(tF, tD, tD, R[0:16, B_CAR + 17:B_CAR + 18],
                                         ALU.add, ALU.bypass))
            C.dop(dve.tensor_copy(R[0:16, B_CAR + 17:B_CAR + 18], tF[0:16, 511:512]))
            C.dop(dve.tensor_tensor(tE, tA, tE, ALU.mult))
            C.dop(dve.tensor_tensor(tF, tB, tF, ALU.mult))
            C.dop(dve.tensor_tensor(tE, tE, tF, ALU.add))
            C.dop(dve.tensor_scalar(tE, tE, COS_EPS, None, ALU.add))
            C.dop(dve.reciprocal(tE, tE))
        else:
            C.dop(dve.tensor_reduce(R[0:16, B_TMP:B_TMP + 1], tC, AX.X, ALU.add))
            C.dop(dve.tensor_tensor(R[0:16, B_SUM + 16:B_SUM + 17],
                                    R[0:16, B_SUM + 16:B_SUM + 17],
                                    R[0:16, B_TMP:B_TMP + 1], ALU.add))
            C.dop(dve.tensor_reduce(R[0:16, B_TMP:B_TMP + 1], tD, AX.X, ALU.add))
            C.dop(dve.tensor_tensor(R[0:16, B_SUM + 17:B_SUM + 18],
                                    R[0:16, B_SUM + 17:B_SUM + 18],
                                    R[0:16, B_TMP:B_TMP + 1], ALU.add))
        ZS = lambda i: C.ZF[:, i * 512:(i + 1) * 512]
        for c in range(NCH):
            b = c % 2
            C.bank_wait(b)
            for kc in range(NCH):
                C.mm(C.pb[b][:, 0:TB], C.wchunk(), y1c(kc), start=(kc == 0), stop=(kc == NCH - 1))
            v = C.stamp_pe()
            act.wait_ge(C.sPE.h, v)
            va = C.sACT.inc(act.copy(ZS(2), C.pb[b][:, 0:TB]))
            C.bank_guard(b, C.sACT, va)
            hs = C.CST[0:16, C_HSEL + c * P:C_HSEL + (c + 1) * P]
            C.bank_wait(4), C.bank_wait(5)
            pe.wait_ge(C.sACT.h, vcc)
            C.mm(C.pb[4][:, 0:TB], hs, tC, start=True, stop=True)
            C.mm(C.pb[5][:, 0:TB], hs, tD, start=True, stop=True)
            v = C.stamp_pe()
            dve.wait_ge(C.sPE.h, v)
            dve.wait_ge(C.sACT.h, va)
            if mode == "sums":
                C.dop(dve.tensor_tensor(ZS(3), ZS(2), C.pb[4][:, 0:TB], ALU.mult))
                C.dop(dve.tensor_reduce(R[:, B_TMP:B_TMP + 1], ZS(3), AX.X, ALU.add))
                C.dop(dve.tensor_tensor(R[:, B_SUM + c:B_SUM + c + 1],
                                        R[:, B_SUM + c:B_SUM + c + 1],
                                        R[:, B_TMP:B_TMP + 1], ALU.add))
                C.dop(dve.tensor_tensor(ZS(3), ZS(2), C.pb[5][:, 0:TB], ALU.mult))
                vg = C.dop(dve.tensor_reduce(R[:, B_TMP:B_TMP + 1], ZS(3), AX.X, ALU.add))
                C.dop(dve.tensor_tensor(R[:, B_SUM + 8 + c:B_SUM + 9 + c],
                                        R[:, B_SUM + 8 + c:B_SUM + 9 + c],
                                        R[:, B_TMP:B_TMP + 1], ALU.add))
                C.bank_guard(4, C.sDVE, vg), C.bank_guard(5, C.sDVE, C.sDVE.n)
                continue
            C.dop(dve.tensor_tensor(ZS(3), ZS(2), C.pb[4][:, 0:TB], ALU.mult))
            C.dop(dve.tensor_tensor_scan(ZS(0), ZS(3), ZS(3), R[:, B_CAR + c:B_CAR + c + 1],
                                         ALU.add, ALU.bypass))
            C.dop(dve.tensor_copy(R[:, B_CAR + c:B_CAR + c + 1], ZS(0)[:, 511:512]))
            C.dop(dve.tensor_tensor(ZS(4), ZS(2), C.pb[5][:, 0:TB], ALU.mult))
            C.dop(dve.tensor_tensor_scan(ZS(1), ZS(4), ZS(4), R[:, B_CAR + 8 + c:B_CAR + 9 + c],
                                         ALU.add, ALU.bypass))
            vg = C.dop(dve.tensor_copy(R[:, B_CAR + 8 + c:B_CAR + 9 + c], ZS(1)[:, 511:512]))
            C.bank_guard(4, C.sDVE, vg), C.bank_guard(5, C.sDVE, vg)
            C.bank_wait(4), C.bank_wait(5)
            C.mm(C.pb[4][:, 0:TB], hs, tA, start=True, stop=True)
            C.mm(C.pb[5][:, 0:TB], hs, tB, start=True, stop=True)
            v = C.stamp_pe()
            dve.wait_ge(C.sPE.h, v)
            C.dop(dve.tensor_tensor(ZS(5), ZS(0), C.pb[4][:, 0:TB], ALU.mult))
            C.dop(dve.tensor_tensor(ZS(4), ZS(1), C.pb[5][:, 0:TB], ALU.mult))
            vg = C.dop(dve.tensor_tensor(ZS(5), ZS(5), ZS(4), ALU.add))
            C.bank_guard(4, C.sDVE, vg), C.bank_guard(5, C.sDVE, vg)
            C.bank_wait(4)
            C.mm(C.pb[4][:, 0:TB], hs, tE, start=True, stop=True)
            v = C.stamp_pe()
            dve.wait_ge(C.sPE.h, v)
            vg = C.dop(dve.tensor_tensor(C.BIG[:, 4096 + c * TB:4096 + (c + 1) * TB],
                                         ZS(5), C.pb[4][:, 0:TB], ALU.mult))
            C.bank_guard(4, C.sDVE, vg)
        if mode == "sums":
            continue
        C.barrier()
        for m in range(NCH):
            b = 2 + (m % 2)
            C.bank_wait(b)
            for kc in range(NCH):
                C.mm(C.pb[b][:, 0:TB], C.wchunk(),
                     C.BIG[:, 4096 + kc * TB:4096 + (kc + 1) * TB],
                     start=(kc == 0), stop=(kc == NCH - 1))
            v = C.stamp_pe()
            dve.wait_ge(C.sPE.h, v)
            x2 = C.MID[:, m * TB:(m + 1) * TB]
            dve.tensor_tensor(x2, C.pb[b][:, 0:TB], yc(m), ALU.add)
            x_ap = xc(m)
            vd = C.sDVE.inc(dve.tensor_tensor(x_ap, x_ap, x2, ALU.add))
            C.bank_guard(b, C.sDVE, vd)
        C.barrier()
        if B2_STOP == "x2":
            continue
        vy2 = C.ln(lambda m: C.MID[:, m * TB:(m + 1) * TB], yc, NCH, affine=ln2)
        C.pe.wait_ge(C.sDVE.h, vy2)
        if B2_STOP == "y2":
            C.barrier()
            ii = None
            for c in range(NCH):
                ii = C.dve.tensor_copy(xc(c), yc(c))
            C.sDVE.inc(ii)
            continue
        for m in range(32):
            b = m % 2
            C.bank_wait(b)
            for kc in range(NCH):
                C.mm(C.pb[b][:, 0:TB], C.wchunk(), yc(kc), start=(kc == 0), stop=(kc == NCH - 1))
            v = C.stamp_pe()
            act.wait_ge(C.sPE.h, v)
            s = C.ZF[:, 0:TB]
            w_ = C.ZF[:, 512:1024]
            th = C.ZF[:, 1024:1536]
            a_ = C.ZF[:, 1536:2048]
            g = C.bankg.get("gelu")
            if g is not None:
                act.wait_ge(g[0], g[1])
            u_ = C.ZF[:, 2048:2560]
            dve.wait_ge(C.sPE.h, v)
            vd = C.sDVE.inc(dve.tensor_scalar(u_, C.pb[b][:, 0:TB], db(532 + m), None, ALU.add))
            if B2_STOP == "u" and m < NCH:
                C.dop(dve.tensor_copy(xc(m), u_))
            act.wait_ge(C.sDVE.h, vd)
            va = C.sACT.inc(act.activation(s, u_, AF.Square))
            dve.wait_ge(C.sACT.h, va)
            dve.tensor_scalar(w_, s, GELU_C, 1.0, ALU.mult, ALU.add)
            vd = C.sDVE.inc(dve.tensor_tensor(w_, w_, u_, ALU.mult))
            act.wait_ge(C.sDVE.h, vd)
            va = C.sACT.inc(act.activation(th, w_, AF.Tanh, scale=RQ2PI))
            dve.wait_ge(C.sACT.h, va)
            dve.tensor_tensor(a_, th, u_, ALU.mult)
            dve.tensor_tensor(a_, a_, u_, ALU.add)
            vd = C.sDVE.inc(dve.tensor_scalar(C.BIG[:, m * TB:(m + 1) * TB], a_, 0.5, None, ALU.mult))
            C.bank_guard(b, C.sDVE, vd)
            C.bankg["gelu"] = (C.sDVE.h, vd)
        if B2_STOP == "h":
            C.barrier()
            ii = None
            for c in range(NCH):
                ii = C.dve.tensor_copy(xc(c), C.BIG[:, c * TB:(c + 1) * TB])
            C.sDVE.inc(ii)
            continue
        C.pe.wait_ge(C.sDVE.h, C.sDVE.n)
        for m in range(NCH):
            b = 2 + (m % 2)
            C.bank_wait(b)
            for kc in range(32):
                C.mm(C.pb[b][:, 0:TB], C.wchunk(), C.BIG[:, kc * TB:(kc + 1) * TB],
                     start=(kc == 0), stop=(kc == 31))
            v = C.stamp_pe()
            dve.wait_ge(C.sPE.h, v)
            x_ap = xc(m)
            vd = C.sDVE.inc(dve.scalar_tensor_tensor(
                x_ap, C.pb[b][:, 0:TB], db(564 + m), x_ap, ALU.add, ALU.add))
            C.bank_guard(b, C.sDVE, vd)
    if mode == "sums" and out_dma:
        C.barrier()
        C.sX.inc(C.pool.dma_start(C.sums_out, R[:, B_SUM:B_SUM + 18]))

def queue_stage_B(C, mode):
    for tb in range(NTB):
        C.ring_queue("b_wv", 0, 64)
        if mode == "full":
            C.ring_queue("b_wo", 0, 64)
            C.ring_queue("b_w1", 0, 256)
            C.ring_queue("b_w2", 0, 256)


def emit_stage_E(C):
    db = lambda col, n=1: C.bias(col, n)
    C.load_bias("e_bias", 32)
    for eng in (C.act, C.dve, C.pe):
        C.sX.wait(eng)
    R = C.SMALL
    for tb in range(NTB):
        C.barrier()
        xc = lambda c, tb=tb: C.X[:, c * T + tb * TB: c * T + tb * TB + TB]
        yc = lambda c: C.YB[:, c * TB:(c + 1) * TB]
        vy = C.ln(xc, yc, NCH)
        C.pe.wait_ge(C.sDVE.h, vy)
        for ip, (dst, bcol) in enumerate([(0, 0), (1, 8), (2, 16)]):
            for m in range(NCH):
                b = m % 2
                C.bank_wait(b)
                for kc in range(NCH):
                    C.mm(C.pb[b][:, 0:TB], C.wchunk(), C.YB[:, kc * TB:(kc + 1) * TB],
                         start=(kc == 0), stop=(kc == NCH - 1))
                v = C.stamp_pe()
                C.dve.wait_ge(C.sPE.h, v)
                if ip < 2:
                    o = C.BIG[:, ip * 4096 + m * TB: ip * 4096 + (m + 1) * TB]
                else:
                    o = C.MID[:, m * TB:(m + 1) * TB]
                vd = C.sDVE.inc(C.dve.tensor_scalar(o, C.pb[b][:, 0:TB], db(bcol + m), None, ALU.add))
                C.bank_guard(b, C.sDVE, vd)
        C.barrier()
        SCALE = float(1.0 / np.sqrt(128.0))
        EX4 = R[:, S_RT:S_RT + 256]
        PRT = R[:, S_RT + 256:S_RT + 512]
        MX4 = R[:, S_RT + 512:S_RT + 516]
        NM4 = R[:, S_RT + 516:S_RT + 520]
        SM4 = R[:, S_RT + 520:S_RT + 524]
        RC4 = R[:, S_RT + 524:S_RT + 528]
        for blk in range(8):
            qb = lambda h: C.BIG[:, h * TB + blk * 64: h * TB + blk * 64 + 64]
            kb = lambda h: C.BIG[:, 4096 + h * TB + blk * 64: 4096 + h * TB + blk * 64 + 64]
            vb = lambda h: C.MID[:, h * TB + blk * 64: h * TB + blk * 64 + 64]
            C.bank_wait(6)
            for hp in range(4):
                C.mm(C.pb[6][0:64, hp * 64:(hp + 1) * 64], qb(2 * hp), kb(2 * hp),
                     start=True, stop=True)
                C.mm(C.pb[6][64:128, hp * 64:(hp + 1) * 64], qb(2 * hp + 1), kb(2 * hp + 1),
                     start=True, stop=True)
            v = C.stamp_pe()
            C.dve.wait_ge(C.sPE.h, v)
            C.dop(C.dve.tensor_reduce(MX4, C.pb[6][:, 0:256].rearrange("p (a b) -> p a b", a=4),
                                      AX.X, ALU.max))
            vd = C.dop(C.dve.tensor_scalar(NM4, MX4, -SCALE, None, ALU.mult))
            C.act.wait_ge(C.sDVE.h, vd)
            ia = None
            for hp in range(4):
                ia = C.act.activation(EX4[:, hp * 64:(hp + 1) * 64], C.pb[6][:, hp * 64:(hp + 1) * 64],
                                      AF.Exp, bias=NM4[:, hp:hp + 1], scale=SCALE)
            va = C.sACT.inc(ia)
            C.bank_guard(6, C.sACT, va)
            C.dve.wait_ge(C.sACT.h, va)
            C.dop(C.dve.tensor_reduce(SM4, EX4.rearrange("p (a b) -> p a b", a=4), AX.X, ALU.add))
            C.dop(C.dve.reciprocal(RC4, SM4))
            ii = None
            for hp in range(4):
                ii = C.dve.tensor_single_scalar(PRT[:, hp * 64:(hp + 1) * 64],
                                                EX4[:, hp * 64:(hp + 1) * 64],
                                                RC4[:, hp:hp + 1], ALU.mult)
            vd = C.dop(ii)
            C.bank_wait(7)
            C.pe.wait_ge(C.sDVE.h, vd)
            for hp in range(4):
                C.tr(C.pb[7][0:64, hp * 128:(hp + 1) * 128], PRT[:, hp * 64:(hp + 1) * 64],
                     C.ident_f32)
            v = C.stamp_pe()
            C.act.wait_ge(C.sPE.h, v)
            prT = C.SC2[0:64, 0:512]
            va = C.sACT.inc(C.act.copy(prT, C.pb[7][0:64, 0:512]))
            C.bank_guard(7, C.sACT, va)
            vT = C.SC2[0:64, 512:1536]
            for g in range(2):
                C.bank_wait(5)
                for j in range(4):
                    h = g * 4 + j
                    C.tr(C.pb[5][0:64, j * 128:(j + 1) * 128], vb(h), C.ident_f32)
                v = C.stamp_pe()
                C.act.wait_ge(C.sPE.h, v)
                va = C.sACT.inc(C.act.copy(vT[0:64, g * 512:(g + 1) * 512], C.pb[5][0:64, 0:512]))
                C.bank_guard(5, C.sACT, va)
            C.bank_wait(4)
            C.pe.wait_ge(C.sACT.h, C.sACT.n)
            for h in range(8):
                C.mm(C.pb[4][:, h * 64:(h + 1) * 64],
                     vT[0:64, h * 128:(h + 1) * 128], prT[0:64, h * 64:(h + 1) * 64],
                     start=True, stop=True)
            v = C.stamp_pe()
            C.act.wait_ge(C.sPE.h, v)
            ia = None
            for h in range(8):
                ia = C.act.copy(C.ZH[:, h * TB + blk * 64: h * TB + blk * 64 + 64],
                                C.pb[4][:, h * 64:(h + 1) * 64])
            va = C.sACT.inc(ia)
            C.bank_guard(4, C.sACT, va)
        C.pe.wait_ge(C.sACT.h, C.sACT.n)
        for m in range(NCH):
            b = 2 + (m % 2)
            C.bank_wait(b)
            for kc in range(NCH):
                C.mm(C.pb[b][:, 0:TB], C.wchunk(), C.ZH[:, kc * TB:(kc + 1) * TB],
                     start=(kc == 0), stop=(kc == NCH - 1))
            v = C.stamp_pe()
            C.dve.wait_ge(C.sPE.h, v)
            x_ap = xc(m)
            vd = C.sDVE.inc(C.dve.scalar_tensor_tensor(
                x_ap, C.pb[b][:, 0:TB], db(24 + m), x_ap, ALU.add, ALU.add))
            C.bank_guard(b, C.sDVE, vd)

def queue_stage_E(C):
    for tb in range(NTB):
        for nm in ("e_wq", "e_wk", "e_wv", "e_wo"):
            C.ring_queue(nm, 0, 64)

STAGES = {}


ALWAYS_KEYS = ("a_gate", "d_gate", "b_wqk", "c_ident_bf", "c_ident_f32", "c_ones_bf",
               "c_ones_f32", "c_onesrow", "c_sel8", "c_hsel")

def build_unfused_stage(emit_fn, queue_fn, arrs, keys=None, extra=None):
    nc = bass.Bass("TRN2", target_bir_lowering=False, debug=False)
    stk = ExitStack()
    with stk:
        if keys is None:
            shapes = dict(arrs)
        else:
            shapes = {k: arrs[k] for k in list(keys) + list(ALWAYS_KEYS)}
        shapes["xin"] = np.zeros((P, NCH * T), np.float32)
        C = Ctx(nc, stk, shapes)
        C.xout = nc.dram_tensor("xout", [P, NCH * T], F32, kind="ExternalOutput").ap()
        if extra == "b_sums":
            C.sums_out = nc.dram_tensor("sums_out", [P, 18], F32, kind="ExternalOutput").ap()
        elif extra == "b_full":
            asum_d = nc.dram_tensor("allsums", [P, 144], F32, kind="ExternalInput").ap()
            pm_d = nc.dram_tensor("pmask", [P, 8], F32, kind="ExternalInput").ap()
            C.ASUM = C.SMALL[:, 3200:3344]
            C.PMASK = C.SMALL[:, 3344:3352]
            C.sX.inc(C.pool.dma_start(C.ASUM, asum_d))
            C.sX.inc(C.pool.dma_start(C.PMASK, pm_d))
        C.setup()
        C.sX.inc(C.pool.dma_start(C.X[:, :], C.dram["xin"]))
        for eng in (C.pe, C.act, C.dve, C.pool, C.sp):
            C.sX.wait(eng)
        queue_fn(C)
        emit_fn(C)
        C.barrier()
        C.sX.inc(C.pool.dma_start(C.xout, C.X[:, :]))
        C.sX.wait(C.pool)
    return nc

FUS_FOR = 3840
FUS_PM4 = 3858

def emit_exchange(C, sh_data, sh_flag, snc, sfl, psem):
    R = C.SMALL
    g = C.pool
    dve = C.dve
    C.barrier()
    C.dop(dve.tensor_copy(C.MID[:, 0:18], R[:, B_SUM:B_SUM + 18]))
    vd = C.dop(dve.tensor_copy(C.MID[:, 18:36], R[:, FUS_FOR:FUS_FOR + 18]))
    g.wait_ge(C.sDVE.h, vd)
    pid = g.partition_id()
    lane = g.compute_val(pid % 2)
    partner = g.compute_val(1 - (pid % 2))
    C.sX.inc(g.dma_start(sh_data[bass.ts(lane, 128), :], C.MID[:, 0:36]))
    C.sX.wait(g)
    i = None
    for i2 in range(1, 16):
        i = g.tensor_copy(snc[0:1, i2:i2 + 1], snc[0:1, 0:1])
    C.sPOOL.inc(i)
    g.wait_ge(C.sPOOL.h, C.sPOOL.n)
    C.sX.inc(g.dma_start(sh_flag[bass.ts(lane, 1), :], snc[0:1, :]))
    C.sX.wait(g)
    with g.register("rt") as rt, g.register("rv") as rv, g.register("rn") as rn, \
            g.register("rcond") as rcond:
        g.reg_mov(rt, 0)
        g.reg_load(rn, snc[0:1, 0:1])

        def cond():
            g.dma_start(sfl[0:1, :], sh_flag[bass.ts(partner, 1), :]).then_inc(psem, 16)
            g.reg_add(rt, rt, 16)
            g.wait_ge(psem, rt)
            g.reg_load(rv, sfl[0:1, 0:1])
            g.reg_alu(rcond, rv, rn, ALU.not_equal)
            return rcond

        with g.While(cond):
            g.nop()
        g.dma_start(C.MID[:, 64:100], sh_data[bass.ts(partner, 128), :]).then_inc(psem, 16)
        g.reg_add(rt, rt, 16)
        g.wait_ge(psem, rt)
    C.sPOOL.inc(g.engine_nop())
    dve.wait_ge(C.sPOOL.h, C.sPOOL.n)
    C.dop(dve.memset(R[:, B_PRE:B_PRE + 18], 0.0))
    PM4 = R[:, FUS_PM4:FUS_PM4 + 4]
    for j, (c0, c1) in enumerate([(0, 18), (18, 36), (64, 82), (82, 100)]):
        ii = dve.scalar_tensor_tensor(R[:, B_PRE:B_PRE + 18], C.MID[:, c0:c1],
                                      PM4[:, j:j + 1], R[:, B_PRE:B_PRE + 18],
                                      ALU.mult, ALU.add)
        C.dop(ii)
    C.dop(dve.tensor_copy(R[:, B_CAR:B_CAR + 18], R[:, B_PRE:B_PRE + 18]))


FUSED_UPTO = "F"
FUSED_FOREIGN = True

def build_fused(arrs):
    nc = bass.Bass("TRN2", target_bir_lowering=False, debug=False)
    stk = ExitStack()
    with stk:
        keys = sorted(set(K_A + K_B1 + K_B2 + K_D + K_E))
        shapes = {k: arrs[k] for k in keys + list(ALWAYS_KEYS)}
        shapes["xin"] = np.zeros((P, NCH * T), np.float32)
        shapes["xfor"] = np.zeros((P, NCH * T), np.float32)
        shapes["pmask4"] = np.zeros((P, 4), np.float32)
        shapes["nonce"] = np.zeros((1, 1), np.int32)
        C = Ctx(nc, stk, shapes)
        C.xout = nc.dram_tensor("xout", [P, NCH * T], F32, kind="ExternalOutput").ap()
        sh_data = nc.dram_tensor("sh_data", [2 * P, 36], F32, kind="Internal",
                                 addr_space="Shared").ap()
        sh_flag = nc.dram_tensor("sh_flag", [2, 16], I32, kind="Internal",
                                 addr_space="Shared").ap()
        snc = stk.enter_context(nc.sbuf_tensor("snc", [1, 16], I32))
        sfl = stk.enter_context(nc.sbuf_tensor("sfl", [1, 16], I32))
        psem = stk.enter_context(nc.semaphore("psem"))
        C.setup()
        C.sX.inc(C.pool.dma_start(snc[0:1, 0:1], C.dram["nonce"]))
        C.sX.inc(C.pool.dma_start(C.SMALL[:, FUS_PM4:FUS_PM4 + 4], C.dram["pmask4"]))
        if FUSED_FOREIGN:
            queue_stage_A(C)
            queue_stage_B(C, "sums")
        queue_stage_A(C)
        if FUSED_UPTO != "A":
            queue_stage_B(C, "sums")
            if FUSED_UPTO != "Bdbg":
                queue_stage_B(C, "full")
        if FUSED_UPTO not in ("A", "B"):
            queue_stage_A(C)
        if FUSED_UPTO in ("D", "E", "F"):
            queue_stage_D(C)
        if FUSED_UPTO in ("E", "F"):
            queue_stage_E(C)
        if FUSED_UPTO == "F":
            queue_stage_D(C)
        if FUSED_FOREIGN:
            C.sX.inc(C.pool.dma_start(C.X[:, :], C.dram["xfor"]))
            for eng in (C.pe, C.act, C.dve, C.pool, C.sp):
                C.sX.wait(eng)
            emit_stage_A(C)
            emit_stage_B(C, "sums", out_dma=False)
            C.barrier()
            C.dop(C.dve.tensor_copy(C.SMALL[:, FUS_FOR:FUS_FOR + 18],
                                    C.SMALL[:, B_SUM:B_SUM + 18]))
            C.barrier()
        C.sX.inc(C.pool.dma_start(C.X[:, :], C.dram["xin"]))
        for eng in (C.pe, C.act, C.dve, C.pool, C.sp):
            C.sX.wait(eng)
        emit_stage_A(C)
        if FUSED_UPTO not in ("A",):
            emit_stage_B(C, "sums", out_dma=False)
            if FUSED_FOREIGN:
                emit_exchange(C, sh_data, sh_flag, snc, sfl, psem)
            else:
                ii = C.dve.memset(C.SMALL[:, B_CAR:B_CAR + 18], 0.0)
                C.dop(ii)
            if FUSED_UPTO == "Bdbg":
                C.barrier()
                C.dop(C.dve.tensor_copy(C.X[:, 0:18], C.SMALL[:, B_CAR:B_CAR + 18]))
                C.dop(C.dve.tensor_copy(C.X[:, 18:118], C.MID[:, 0:100]))
                C.dop(C.dve.tensor_copy(C.X[:, 118:122], C.SMALL[:, FUS_PM4:FUS_PM4 + 4]))
            else:
                emit_stage_B(C, "full", carry_init=lambda C: None)
            C.barrier()
        if FUSED_UPTO not in ("A", "B"):
            emit_stage_A(C)
            C.barrier()
        if FUSED_UPTO in ("D", "E", "F"):
            emit_stage_D(C)
            C.barrier()
        if FUSED_UPTO in ("E", "F"):
            emit_stage_E(C)
            C.barrier()
        if FUSED_UPTO == "F":
            emit_stage_D(C)
        C.barrier()
        C.sX.inc(C.pool.dma_start(C.xout, C.X[:, :]))
        C.sX.wait(C.pool)
    return nc

XFOR_IDX = {0: 0, 1: 1, 2: 0, 3: 1, 4: 4, 5: 5, 6: 4, 7: 5}

def _pmask4(c):
    m = np.zeros((P, 4), np.float32)
    r = c % 4
    if r == 1:
        m[:, 2] = 1.0
    elif r == 2:
        m[:, 1] = 1.0
        m[:, 3] = 1.0
    elif r == 3:
        m[:, 1] = 1.0
        m[:, 2] = 1.0
        m[:, 3] = 1.0
    return m

def kernel_fused(inputs):
    shared = _get_shared(inputs)
    if "F" not in _NCS:
        _NCS["F"] = build_fused(shared)
    nc = _NCS["F"]
    x = np.asarray(inputs["L0_attn_x"], np.float32)
    xs = [x_fm(x[c // 4, (c % 4) * T:(c % 4 + 1) * T, :]) for c in range(8)]
    keys = sorted(set(K_A + K_B1 + K_B2 + K_D + K_E))
    base = {k: shared[k] for k in keys + list(ALWAYS_KEYS)}
    nonce = np.array([[np.random.randint(1, 2 ** 30)]], np.int32)
    maps = []
    for c in range(8):
        m = dict(base)
        m["xin"] = xs[c]
        m["xfor"] = xs[XFOR_IDX[c]]
        m["pmask4"] = _pmask4(c)
        m["nonce"] = nonce
        maps.append(m)
    res = run_bass_kernel_spmd(nc, maps, core_ids=list(range(8)))
    out = np.empty((2, 8192, D), np.float32)
    for c in range(8):
        out[c // 4, (c % 4) * T:(c % 4 + 1) * T, :] = x_un(np.asarray(res.results[c]["xout"]))
    return out

_PREPPED = {}

def _get_shared(inputs):
    key = id(inputs.get("L0_attn_x"))
    if key not in _PREPPED:
        _PREPPED.clear()
        _PREPPED[key] = prep_shared(inputs)
    return _PREPPED[key]

K_A = ["a_w1", "a_w2", "a_ws1", "a_ws2"]
K_B1 = ["b_wv", "b_ln"]
K_B2 = ["b_wv", "b_wo", "b_w1", "b_w2", "b_ln", "b_b1", "b_b2"]
K_D = ["d_w1", "d_w2", "d_kw0", "d_kw1", "d_bias"]
K_E = ["e_wq", "e_wk", "e_wv", "e_wo", "e_bias"]

def _pmask(c):
    m = np.zeros((P, 8), np.float32)
    for r in range((c // 4) * 4, c):
        m[:, r] = 1.0
    return m

_NCS = {}

def _get_ncs(shared):
    if "A" not in _NCS:
        _NCS["A"] = build_unfused_stage(emit_stage_A, queue_stage_A, shared, K_A)
        _NCS["B1"] = build_unfused_stage(lambda C: emit_stage_B(C, "sums"),
                                         lambda C: queue_stage_B(C, "sums"),
                                         shared, K_B1, extra="b_sums")
        _NCS["B2"] = build_unfused_stage(lambda C: emit_stage_B(C, "full"),
                                         lambda C: queue_stage_B(C, "full"),
                                         shared, K_B2, extra="b_full")
        _NCS["D"] = build_unfused_stage(emit_stage_D, queue_stage_D, shared, K_D)
        _NCS["E"] = build_unfused_stage(emit_stage_E, queue_stage_E, shared, K_E)
    return _NCS

def kernel(**inputs):
    return kernel_fused(inputs)

def kernel_unfused(**inputs):
    shared = _get_shared(inputs)
    ncs = _get_ncs(shared)
    x = np.asarray(inputs["L0_attn_x"], np.float32)
    xs = [x_fm(x[c // 4, (c % 4) * T:(c % 4 + 1) * T, :]) for c in range(8)]
    cores = list(range(8))

    def run(name, keys, extra_fn=None):
        base = {k: shared[k] for k in keys + list(ALWAYS_KEYS)}
        maps = []
        for c in cores:
            m = dict(base)
            m["xin"] = xs[c]
            if extra_fn is not None:
                m.update(extra_fn(c))
            maps.append(m)
        return run_bass_kernel_spmd(ncs[name], maps, core_ids=cores)

    for st in ("A", "B", "A", "D", "E", "D"):
        if st == "B":
            r1 = run("B1", K_B1)
            allsums = np.concatenate([r1.results[c]["sums_out"] for c in cores], axis=1)
            res = run("B2", K_B2, lambda c: dict(allsums=allsums, pmask=_pmask(c)))
        else:
            res = run(st, {"A": K_A, "D": K_D, "E": K_E}[st])
        xs = [np.asarray(res.results[c]["xout"]) for c in cores]

    out = np.empty((2, 8192, D), np.float32)
    for c in cores:
        out[c // 4, (c % 4) * T:(c % 4 + 1) * T, :] = x_un(xs[c])
    return out


# version 03
# speedup vs baseline: 1.0549x; 1.0149x over previous
import numpy as np
import ml_dtypes
from contextlib import ExitStack

import concourse.bass as bass
import concourse.mybir as mybir
from concourse.bass_utils import run_bass_kernel_spmd

F32 = mybir.dt.float32
BF16 = mybir.dt.bfloat16
I32 = mybir.dt.int32
AF = mybir.ActivationFunctionType
ALU = mybir.AluOpType
AX = mybir.AxisListType
BF = ml_dtypes.bfloat16

P = 128
T = 2048
TB = 512
NTB = T // TB
NCH = 8
D = 1024
GRID = np.linspace(-1.2, 0.2, 8).astype(np.float32)
LN_EPS = 1e-5
COS_EPS = 1e-6
PI = float(np.pi)
RQ2PI = float(np.sqrt(2.0 / np.pi))
GELU_C = 0.044715
B2_STOP = None


def _mk(w):
    w = np.asarray(w, np.float32)
    K, M = w.shape
    KC, MC = K // P, M // P
    a = w.astype(BF).reshape(KC, P, MC, P)
    return np.ascontiguousarray(a.transpose(1, 2, 0, 3).reshape(P, MC * KC * P))

def _blocks(w, rcs):
    w = np.asarray(w, np.float32)
    out = np.empty((P, len(rcs) * P), BF)
    for i, (r, c) in enumerate(rcs):
        out[:, i * P:(i + 1) * P] = w[r:r + P, c:c + P].astype(BF)
    return out

def _vec(v, nch=None):
    v = np.asarray(v, np.float32)
    n = v.shape[0] // P if nch is None else nch
    return np.ascontiguousarray(v.reshape(n, P).T)

def _smallmat(w, ncols):
    w = np.asarray(w, np.float32)
    KC = w.shape[0] // P
    return np.ascontiguousarray(w.astype(BF).reshape(KC, P, ncols).transpose(1, 0, 2).reshape(P, KC * ncols))

def prep_shared(inp):
    d = {}
    d["a_gate"] = _smallmat(np.asarray(inp["L0_ffn_gate_w"], np.float32).T, 8)
    W1 = np.asarray(inp["L0_ffn_W1"], np.float32)
    W2 = np.asarray(inp["L0_ffn_W2"], np.float32)
    d["a_w1"] = np.concatenate([_mk(W1[e].T) for e in range(8)], axis=1)
    d["a_w2"] = np.concatenate([_mk(W2[e].T) for e in range(8)], axis=1)
    d["a_ws1"] = _mk(np.asarray(inp["L0_ffn_Ws1"], np.float32).T)
    d["a_ws2"] = _mk(np.asarray(inp["L0_ffn_Ws2"], np.float32).T)
    WQ = np.asarray(inp["L0_attn_W_Q"], np.float32)[:, :, 0].T
    WK = np.asarray(inp["L0_attn_W_K"], np.float32)[:, :, 0].T
    d["b_wqk"] = np.concatenate([_smallmat(WQ, 16), _smallmat(WK, 16)], axis=1)
    WV = np.asarray(inp["L0_attn_W_V"], np.float32)
    d["b_wv"] = _mk(WV.transpose(1, 0, 2).reshape(D, D))
    WO = np.asarray(inp["L0_attn_W_O"], np.float32)
    d["b_wo"] = _mk(WO.transpose(2, 1, 0).reshape(D, D))
    d["b_ln"] = np.concatenate([
        _vec(inp["L0_attn_ln1_w"]), _vec(inp["L0_attn_ln1_b"]),
        _vec(inp["L0_attn_ln2_w"]), _vec(inp["L0_attn_ln2_b"])], axis=1)
    d["b_w1"] = _mk(np.asarray(inp["L0_attn_W1"], np.float32))
    d["b_b1"] = _vec(inp["L0_attn_b1"])
    d["b_w2"] = _mk(np.asarray(inp["L0_attn_W2"], np.float32))
    d["b_b2"] = _vec(inp["L0_attn_b2"])
    d["d_gate"] = _smallmat(np.asarray(inp["L1_ffn_gate_W"], np.float32), 8)
    mW1 = np.asarray(inp["L1_ffn_mlp_W1"], np.float32)
    mW2 = np.asarray(inp["L1_ffn_mlp_W2"], np.float32)
    d["d_w1"] = np.concatenate([_mk(mW1[e]) for e in range(4)], axis=1)
    d["d_w2"] = np.concatenate([_mk(mW2[e]) for e in range(4)], axis=1)
    d["d_bias"] = np.concatenate(
        [np.concatenate([_vec(inp["L1_ffn_mlp_b1"][e], 32) for e in range(4)], axis=1),
         np.concatenate([_vec(inp["L1_ffn_mlp_b2"][e], 8) for e in range(4)], axis=1),
         np.concatenate([_vec(inp["L1_ffn_kan_ln_g0"][e], 8) for e in range(4)], axis=1),
         np.concatenate([_vec(inp["L1_ffn_kan_ln_b0"][e], 8) for e in range(4)], axis=1),
         np.concatenate([_vec(inp["L1_ffn_kan_ln_g1"][e], 16) for e in range(4)], axis=1),
         np.concatenate([_vec(inp["L1_ffn_kan_ln_b1"][e], 16) for e in range(4)], axis=1),
         np.concatenate([_vec(inp["L1_ffn_kan_sl_b0"][e], 16) for e in range(4)], axis=1),
         np.concatenate([_vec(inp["L1_ffn_kan_sl_b1"][e], 8) for e in range(4)], axis=1),
         ], axis=1)
    kw0 = np.asarray(inp["L1_ffn_kan_sl_W0"], np.float32)
    kw1 = np.asarray(inp["L1_ffn_kan_sl_W1"], np.float32)
    o0 = [(kh * 4096 + kc * P, m * P) for kh in range(2) for m in range(16) for kc in range(32)]
    o1 = [(q * 4096 + kc * P, m * P) for q in range(4) for m in range(8) for kc in range(32)]
    d["d_kw0"] = np.concatenate(
        [_blocks(kw0[e].reshape(D, 8, 2048).transpose(1, 0, 2).reshape(8192, 2048), o0) for e in range(4)], axis=1)
    d["d_kw1"] = np.concatenate(
        [_blocks(kw1[e].reshape(2048, 8, D).transpose(1, 0, 2).reshape(16384, D), o1) for e in range(4)], axis=1)
    for nm in ("q", "k", "v", "o"):
        d[f"e_w{nm}"] = _mk(np.asarray(inp[f"L1_attn_W{nm}"], np.float32))
    d["e_bias"] = np.concatenate([_vec(inp[f"L1_attn_b{nm}"]) for nm in ("q", "k", "v", "o")], axis=1)
    d["c_ident_bf"] = np.eye(P).astype(BF)
    d["c_ident_f32"] = np.eye(P).astype(np.float32)
    d["c_ones_bf"] = np.ones((P, 1), BF)
    d["c_ones_f32"] = np.ones((P, 1), np.float32)
    d["c_onesrow"] = np.ones((1, P), np.float32)
    sel8 = np.zeros((8, 8 * P), BF)
    for e in range(8):
        sel8[e, e * P:(e + 1) * P] = 1.0
    d["c_sel8"] = sel8
    hsel = np.zeros((16, NCH * P), np.float32)
    for c in range(NCH):
        for p in range(P):
            hsel[2 * c + (p >= 64), c * P + p] = 1.0
    d["c_hsel"] = hsel
    return d

def x_fm(xslice):
    return np.ascontiguousarray(
        np.asarray(xslice, np.float32).reshape(T, NCH, P).transpose(2, 1, 0).reshape(P, NCH * T))

def x_un(fm):
    return np.ascontiguousarray(np.asarray(fm).reshape(P, NCH, T).transpose(2, 1, 0).reshape(T, D))

def _dt_of(a):
    if a.dtype == BF:
        return BF16
    if a.dtype == np.int32:
        return I32
    return F32


class Sem:
    def __init__(self, C, name, step=1):
        self.h = C.stk.enter_context(C.nc.semaphore(name))
        self.n = 0
        self.step = step

    def inc(self, instr, k=1):
        instr.then_inc(self.h, self.step * k)
        self.n += self.step * k
        return self.n

    def wait(self, eng, val=None):
        eng.wait_ge(self.h, self.n if val is None else val)


RCS = 16
RING_N = 4

C_IDENT, C_ONES, C_ONESROW, C_SEL, C_BIAS, C_HSEL = 0, 128, 129, 257, 1281, 2048
CB_IDENT, CB_ONES, CB_AGATE, CB_DGATE, CB_WQK, CB_SEL8 = 0, 128, 129, 193, 257, 513
S_MROW, S_TROW, S_RROW, S_TMP = 0, 512, 1024, 1536
S_LNSC = 2048
S_RT = 3072


class Ctx:
    def __init__(self, nc, stk, arrs):
        self.nc = nc
        self.stk = stk
        self.pe, self.act, self.dve, self.pool, self.sp = \
            nc.tensor, nc.scalar, nc.vector, nc.gpsimd, nc.sync
        self.dram = {k: nc.dram_tensor(k, list(v.shape), _dt_of(v), kind="ExternalInput").ap()
                     for k, v in arrs.items()}
        self.sW = Sem(self, "sW", 16)
        self.sW2 = Sem(self, "sW2", 16)
        self.sX = Sem(self, "sX", 16)
        self.sPE = Sem(self, "sPE")
        self.sACT = Sem(self, "sACT")
        self.sDVE = Sem(self, "sDVE")
        self.sPOOL = Sem(self, "sPOOL")
        e = stk.enter_context
        self.X = e(nc.sbuf_tensor("Xr", [P, NCH * T], F32))
        self.BIG = e(nc.sbuf_tensor("BIGa", [P, 16384], BF16))
        self.ZF = e(nc.sbuf_tensor("ZFa", [P, 4096], F32))
        self.MID = e(nc.sbuf_tensor("MIDa", [P, 4096], F32))
        self.SMALL = e(nc.sbuf_tensor("SMALLa", [P, 4096], F32))
        self.YB = e(nc.sbuf_tensor("YBa", [P, 4096], BF16))
        self.LNY = e(nc.sbuf_tensor("LNYa", [P, 2048], BF16))
        self.ZH = e(nc.sbuf_tensor("ZHa", [P, 8192], BF16))
        self.WR = e(nc.sbuf_tensor("WRa", [P, RING_N * RCS * P], BF16))
        self.CST = e(nc.sbuf_tensor("CSTa", [P, 3072], F32))
        self.CSTB = e(nc.sbuf_tensor("CSTBa", [P, 1600], BF16))
        self.SC2 = e(nc.sbuf_tensor("SC2a", [P, 2048], BF16))
        self.pb = [e(nc.psum_tensor(f"pb{i}", [P, 512], F32)) for i in range(8)]
        self.bankg = {}
        self.r_fed = 0
        self.r_used = 0
        self.r_pending = []
        self.r_slot_target = [0] * RING_N

        self.last_mm = None
        self.mm_unstamped = 0

    def stamp_pe(self):
        if self.last_mm is not None and self.mm_unstamped > 0:
            self.sPE.inc(self.last_mm)
            self.mm_unstamped = 0
        return self.sPE.n

    def mm(self, out, lhsT, rhs, start, stop):
        i = self.pe.matmul(out, lhsT, rhs, start=start, stop=stop)
        self.last_mm = i
        self.mm_unstamped += 1
        return i

    def tr(self, out, in_, ident):
        i = self.pe.transpose(out, in_, ident)
        self.last_mm = i
        self.mm_unstamped += 1
        return i

    def dop(self, instr):
        v = self.sDVE.inc(instr)
        self.dve.wait_ge(self.sDVE.h, v)
        return v

    def aop(self, instr):
        v = self.sACT.inc(instr)
        self.act.wait_ge(self.sACT.h, v)
        return v

    def bank_wait(self, i):
        g = self.bankg.get(i)
        if g is not None:
            self.pe.wait_ge(g[0], g[1])

    def bank_guard(self, i, sem, val=None):
        self.bankg[i] = (sem.h, sem.n if val is None else val)

    def barrier(self):
        self.stamp_pe()
        sems = [self.sW, self.sW2, self.sX, self.sPE, self.sACT, self.sDVE, self.sPOOL]
        for eng in (self.pe, self.act, self.dve, self.pool, self.sp):
            for s in sems:
                if s.n > 0:
                    eng.wait_ge(s.h, s.n)

    def setup(self):
        g = self.pool
        dm = self.dram
        for dst, src in [
            (self.CSTB[:, CB_IDENT:CB_IDENT + 128], dm["c_ident_bf"]),
            (self.CSTB[:, CB_ONES:CB_ONES + 1], dm["c_ones_bf"]),
            (self.CSTB[:, CB_AGATE:CB_AGATE + 64], dm["a_gate"]),
            (self.CSTB[:, CB_DGATE:CB_DGATE + 64], dm["d_gate"]),
            (self.CSTB[:, CB_WQK:CB_WQK + 256], dm["b_wqk"]),
            (self.CSTB[0:8, CB_SEL8:CB_SEL8 + 1024].rearrange("p (a b) -> p a b", a=8),
             dm["c_sel8"].rearrange("p (a b) -> p a b", a=8)),
            (self.CST[:, C_IDENT:C_IDENT + 128], dm["c_ident_f32"]),
            (self.CST[:, C_ONES:C_ONES + 1], dm["c_ones_f32"]),
            (self.CST[0:1, C_ONESROW:C_ONESROW + 128], dm["c_onesrow"]),
            (self.CST[0:16, C_HSEL:C_HSEL + 1024].rearrange("p (a b) -> p a b", a=8),
             dm["c_hsel"].rearrange("p (a b) -> p a b", a=8)),
        ]:
            self.sX.inc(g.dma_start(dst, src))
        self.ident_bf = self.CSTB[:, CB_IDENT:CB_IDENT + 128]
        self.ident_f32 = self.CST[:, C_IDENT:C_IDENT + 128]
        self.ones_bf = self.CSTB[:, CB_ONES:CB_ONES + 1]
        self.ones_f32 = self.CST[:, C_ONES:C_ONES + 1]
        self.onesrow = self.CST[0:1, C_ONESROW:C_ONESROW + 128]
        cp = self.CST[:, 2016 - C_BIAS + C_BIAS:2032]
        cp = self.CST[:, 2016:2032]
        i = None
        for j, v in enumerate([LN_EPS, PI / 2.0, 1.0] + [-float(g) for g in GRID]):
            i = self.dve.memset(cp[:, j:j + 1], float(v))
            self.nc.const_aps.register(F32, float(v), cp[:, j:j + 1])
        self.sDVE.inc(i)
        for eng in (self.pe, self.act, self.dve, self.pool, self.sp):
            self.sX.wait(eng)
            self.sDVE.wait(eng)

    def load_bias(self, name, ncols, col_off=0, src_col=0):
        dst = self.CST[:, C_BIAS + col_off: C_BIAS + col_off + ncols]
        src = self.dram[name]
        if src.shape[1] != ncols or src_col:
            src = src[:, src_col:src_col + ncols]
        self.sX.inc(self.pool.dma_start(dst, src))
        return dst

    def bias(self, col, n=1):
        return self.CST[:, C_BIAS + col: C_BIAS + col + n]

    def ring_queue(self, name, chunk0, nchunks):
        assert nchunks % RCS == 0, nchunks
        self.r_pending.append([self.dram[name], chunk0, chunk0 + nchunks])
        self._ring_fill()

    def _ring_fill(self):
        while self.r_pending and self.r_fed < self.r_used - (self.r_used % RCS) + RING_N * RCS:
            reg = self.r_pending[0]
            dram, nxt, end = reg
            n = min(RCS, end - nxt)
            slot = (self.r_fed // RCS) % RING_N
            eng, sem = (self.sp, self.sW) if slot % 2 == 0 else (self.pool, self.sW2)
            eng.wait_ge(self.sPE.h, self.sPE.n)
            dst = self.WR[:, slot * RCS * P:(slot * RCS + n) * P]
            src = dram[:, nxt * P:(nxt + n) * P]
            self.r_slot_target[slot] = sem.inc(eng.dma_start(dst, src))
            self.r_fed += n
            reg[1] += n
            if reg[1] >= end:
                self.r_pending.pop(0)

    def wchunk(self):
        i = self.r_used
        slot = (i // RCS) % RING_N
        off = i % RCS
        if off == 0:
            self.stamp_pe()
            self._ring_fill()
            assert self.r_fed > i, "ring underflow: consumption passed queued chunks"
            wsem = self.sW if slot % 2 == 0 else self.sW2
            self.pe.wait_ge(wsem.h, self.r_slot_target[slot])
        self.r_used += 1
        return self.WR[:, (slot * RCS + off) * P:(slot * RCS + off + 1) * P]

    def ln(self, src, dst, nch, W=TB, affine=None, also_f32=None, src_bf=False):
        pe, act, dve = self.pe, self.act, self.dve
        NF = float(nch * P)
        ps1, ps2, pm, pr = self.pb[2], self.pb[3], self.pb[4], self.pb[5]
        act.wait_ge(self.sPE.h, self.stamp_pe())
        act.wait_ge(self.sDVE.h, self.sDVE.n)
        pe.wait_ge(self.sDVE.h, self.sDVE.n)
        pe.wait_ge(self.sACT.h, self.sACT.n)
        self.bank_wait(2), self.bank_wait(3)
        sq = self.SC2
        ones1 = self.ones_bf if src_bf else self.ones_f32
        pe_stamps = []
        for c in range(nch):
            s = src(c)
            sqv = sq[:, (c % 2) * 512:(c % 2) * 512 + W]
            if c >= 2:
                act.wait_ge(self.sPE.h, pe_stamps[c - 2])
            va = self.sACT.inc(act.activation(sqv, s, AF.Square))
            self.mm(ps1[0:1, 0:W], ones1, s, start=(c == 0), stop=(c == nch - 1))
            pe.wait_ge(self.sACT.h, va)
            self.mm(ps2[0:1, 0:W], self.ones_bf, sqv, start=(c == 0), stop=(c == nch - 1))
            pe_stamps.append(self.stamp_pe())
        vpe = self.sPE.n
        R = self.SMALL
        m_row = R[0:1, S_MROW:S_MROW + W]
        t_row = R[0:1, S_TROW:S_TROW + W]
        r_row = R[0:1, S_RROW:S_RROW + W]
        tmp = R[0:1, S_TMP:S_TMP + W]
        act.wait_ge(self.sPE.h, vpe)
        va = self.sACT.inc(act.mul(m_row, ps1[0:1, 0:W], 1.0 / NF))
        dve.wait_ge(self.sACT.h, va)
        dve.wait_ge(self.sPE.h, vpe)
        self.dop(dve.tensor_scalar(t_row, ps2[0:1, 0:W], 1.0 / NF, None, ALU.mult))
        self.dop(dve.tensor_tensor(tmp, m_row, m_row, ALU.mult))
        vd = self.dop(dve.tensor_tensor(t_row, t_row, tmp, ALU.subtract))
        self.bank_guard(2, self.sACT)
        self.bank_guard(3, self.sDVE)
        act.wait_ge(self.sDVE.h, vd)
        va = self.sACT.inc(act.activation(t_row, t_row, AF.Sqrt, bias=LN_EPS))
        dve.wait_ge(self.sACT.h, va)
        vd = self.dop(dve.reciprocal(r_row, t_row))
        self.bank_wait(4), self.bank_wait(5)
        pe.wait_ge(self.sDVE.h, vd)
        self.mm(pm[:, 0:W], self.onesrow, m_row, start=True, stop=True)
        self.mm(pr[:, 0:W], self.onesrow, r_row, start=True, stop=True)
        vb = self.stamp_pe()
        dve.wait_ge(self.sPE.h, vb)
        i = None
        for c in range(nch):
            s = src(c)
            o = dst(c)
            tscr = R[:, S_LNSC + (c % 2) * 512: S_LNSC + (c % 2) * 512 + W]
            dve.tensor_tensor(tscr, s, pm[:, 0:W], ALU.subtract)
            if affine is None:
                i = dve.tensor_tensor(o, tscr, pr[:, 0:W], ALU.mult)
            else:
                dve.tensor_tensor(tscr, tscr, pr[:, 0:W], ALU.mult)
                w_ap, b_ap = affine(c)
                i = dve.tensor_scalar(o, tscr, w_ap, b_ap, ALU.mult, ALU.add)
            if also_f32 is not None:
                self.sDVE.inc(i)
                i = dve.tensor_copy(also_f32(c), o)
        vd = self.sDVE.inc(i)
        self.bank_guard(4, self.sDVE)
        self.bank_guard(5, self.sDVE)
        return vd

    def routing(self, gate_ap, renorm, scale=0.5):
        pe, act, dve = self.pe, self.act, self.dve
        pg = self.pb[6]
        pt = self.pb[7]
        self.bank_wait(6)
        for kc in range(NCH):
            self.mm(pg[0:8, 0:TB], gate_ap[:, kc * 8:(kc + 1) * 8],
                    self.YB[:, kc * TB:(kc + 1) * TB], start=(kc == 0), stop=(kc == NCH - 1))
        v = self.stamp_pe()
        act.wait_ge(self.sPE.h, v)
        R = self.SMALL
        LGF = R[0:8, S_RT + 256:S_RT + 768]
        va = self.sACT.inc(act.copy(LGF, pg[0:8, 0:TB]))
        self.bank_guard(6, self.sACT)
        self.bank_wait(7)
        pe.wait_ge(self.sACT.h, va)
        for i in range(4):
            self.tr(pt[:, i * 8:(i + 1) * 8],
                    LGF[0:8, i * P:(i + 1) * P], self.ident_f32[0:8, 0:8])
        v = self.stamp_pe()
        dve.wait_ge(self.sPE.h, v)
        LG = R[:, S_RT:S_RT + 32]
        self.dop(dve.tensor_copy(LG, pt[:, 0:32]))
        self.bank_guard(7, self.sDVE)
        v3 = lambda off: R[:, S_RT + off:S_RT + off + 32].rearrange("p (a b) -> p a b", a=4)
        MX = R[:, S_RT + 32:S_RT + 36]
        NM = R[:, S_RT + 36:S_RT + 40]
        SM = R[:, S_RT + 40:S_RT + 44]
        RC = R[:, S_RT + 44:S_RT + 48]
        M1 = R[:, S_RT + 48:S_RT + 52]
        M2 = R[:, S_RT + 52:S_RT + 56]
        EX = R[:, S_RT + 64:S_RT + 96]
        PRb = R[:, S_RT + 96:S_RT + 128]
        GE = R[:, S_RT + 128:S_RT + 160]
        KP = R[:, S_RT + 160:S_RT + 192]
        WT = R[:, S_RT + 192:S_RT + 224]
        self.dop(dve.tensor_reduce(MX, LG.rearrange("p (a b) -> p a b", a=4), AX.X, ALU.max))
        vd = self.dop(dve.tensor_scalar(NM, MX, -1.0, None, ALU.mult))
        act.wait_ge(self.sDVE.h, vd)
        ia = None
        for i in range(4):
            ia = act.activation(EX[:, i * 8:(i + 1) * 8], LG[:, i * 8:(i + 1) * 8],
                                AF.Exp, bias=NM[:, i:i + 1])
        va = self.sACT.inc(ia)
        dve.wait_ge(self.sACT.h, va)
        self.dop(dve.tensor_reduce(SM, EX.rearrange("p (a b) -> p a b", a=4), AX.X, ALU.add))
        self.dop(dve.reciprocal(RC, SM))
        ii = None
        for i in range(4):
            ii = dve.tensor_single_scalar(PRb[:, i * 8:(i + 1) * 8], EX[:, i * 8:(i + 1) * 8],
                                          RC[:, i:i + 1], ALU.mult)
        self.dop(ii)
        self.dop(dve.tensor_reduce(M1, PRb.rearrange("p (a b) -> p a b", a=4), AX.X, ALU.max))
        for i in range(4):
            ii = dve.tensor_single_scalar(GE[:, i * 8:(i + 1) * 8], PRb[:, i * 8:(i + 1) * 8],
                                          M1[:, i:i + 1], ALU.is_ge)
        self.dop(ii)
        self.dop(dve.tensor_tensor(GE, PRb, GE, ALU.subtract))
        self.dop(dve.tensor_reduce(M2, GE.rearrange("p (a b) -> p a b", a=4), AX.X, ALU.max))
        for i in range(4):
            ii = dve.tensor_single_scalar(KP[:, i * 8:(i + 1) * 8], PRb[:, i * 8:(i + 1) * 8],
                                          M2[:, i:i + 1], ALU.is_ge)
        self.dop(ii)
        if renorm:
            self.dop(dve.tensor_tensor(MX, M1, M2, ALU.add))
            self.dop(dve.reciprocal(RC, MX))
            self.dop(dve.tensor_scalar(RC, RC, scale, None, ALU.mult))
            self.dop(dve.tensor_tensor(WT, PRb, KP, ALU.mult))
            for i in range(4):
                ii = dve.tensor_single_scalar(WT[:, i * 8:(i + 1) * 8], WT[:, i * 8:(i + 1) * 8],
                                              RC[:, i:i + 1], ALU.mult)
            self.dop(ii)
        else:
            self.dop(dve.scalar_tensor_tensor(WT, PRb, scale, KP, ALU.mult, ALU.mult))
        vd = self.sDVE.n
        self.bank_wait(7)
        pe.wait_ge(self.sDVE.h, vd)
        for i in range(4):
            self.tr(pt[0:8, i * P:(i + 1) * P], WT[:, i * 8:(i + 1) * 8], self.ident_f32)
        v = self.stamp_pe()
        act.wait_ge(self.sPE.h, v)
        wfm = self.SC2[0:8, 1024:1024 + TB]
        va = self.sACT.inc(act.copy(wfm, pt[0:8, 0:TB]))
        self.bank_guard(7, self.sACT)
        return wfm, va

    def w_bcast(self, wfm, wfm_ready, e, bank):
        self.bank_wait(bank)
        self.pe.wait_ge(self.sACT.h, wfm_ready)
        sel = self.CSTB[0:8, CB_SEL8 + e * P:CB_SEL8 + (e + 1) * P]
        self.mm(self.pb[bank][:, 0:TB], sel, wfm, start=True, stop=True)
        return self.stamp_pe()



def emit_stage_A(C):
    for tb in range(NTB):
        C.barrier()
        xc = lambda c, tb=tb: C.X[:, c * T + tb * TB: c * T + tb * TB + TB]
        yc = lambda c: C.YB[:, c * TB:(c + 1) * TB]
        vy = C.ln(xc, yc, NCH)
        for eng in (C.pe,):
            eng.wait_ge(C.sDVE.h, vy)
        gate = C.CSTB[:, CB_AGATE:CB_AGATE + 64]
        wfm, wv = C.routing(gate, renorm=False, scale=0.5)
        for e in range(8):
            vb = C.w_bcast(wfm, wv, e, 4)
            for m in range(12):
                b = m % 2
                C.bank_wait(b)
                for kc in range(NCH):
                    C.mm(C.pb[b][:, 0:TB], C.wchunk(), C.YB[:, kc * TB:(kc + 1) * TB],
                         start=(kc == 0), stop=(kc == NCH - 1))
                v = C.stamp_pe()
                C.act.wait_ge(C.sPE.h, v)
                if m < 6:
                    i = C.act.copy(C.BIG[:, m * TB:(m + 1) * TB], C.pb[b][:, 0:TB])
                else:
                    i = C.act.activation(C.BIG[:, 3072 + (m - 6) * TB:3072 + (m - 5) * TB],
                                         C.pb[b][:, 0:TB], AF.Silu)
                va = C.sACT.inc(i)
                C.bank_guard(b, C.sACT, va)
            C.dve.wait_ge(C.sACT.h, C.sACT.n)
            C.dve.wait_ge(C.sPE.h, C.sPE.n)
            ii = None
            for j in range(6):
                ys = C.BIG[:, j * TB:(j + 1) * TB]
                gs = C.BIG[:, 3072 + j * TB:3072 + (j + 1) * TB]
                C.dve.tensor_tensor(ys, ys, gs, ALU.mult)
                ii = C.dve.tensor_tensor(ys, ys, C.pb[4][:, 0:TB], ALU.mult)
            vd = C.sDVE.inc(ii)
            C.bank_guard(4, C.sDVE, vd)
            C.pe.wait_ge(C.sDVE.h, vd)
            for m in range(8):
                b = 2 + (m % 2)
                C.bank_wait(b)
                for kc in range(6):
                    C.mm(C.pb[b][:, 0:TB], C.wchunk(), C.BIG[:, kc * TB:(kc + 1) * TB],
                         start=(kc == 0), stop=(kc == 5))
                v = C.stamp_pe()
                C.dve.wait_ge(C.sPE.h, v)
                x_ap = xc(m)
                vd = C.sDVE.inc(C.dve.tensor_tensor(x_ap, x_ap, C.pb[b][:, 0:TB], ALU.add))
                C.bank_guard(b, C.sDVE, vd)
        for m in range(24):
            b = m % 2
            C.bank_wait(b)
            for kc in range(NCH):
                C.mm(C.pb[b][:, 0:TB], C.wchunk(), C.YB[:, kc * TB:(kc + 1) * TB],
                     start=(kc == 0), stop=(kc == NCH - 1))
            v = C.stamp_pe()
            C.act.wait_ge(C.sPE.h, v)
            if m < 12:
                i = C.act.mul(C.BIG[:, m * TB:(m + 1) * TB], C.pb[b][:, 0:TB], 0.5)
            else:
                i = C.act.activation(C.BIG[:, 6144 + (m - 12) * TB:6144 + (m - 11) * TB],
                                     C.pb[b][:, 0:TB], AF.Silu)
            va = C.sACT.inc(i)
            C.bank_guard(b, C.sACT, va)
        C.dve.wait_ge(C.sACT.h, C.sACT.n)
        ii = None
        for j in range(12):
            ys = C.BIG[:, j * TB:(j + 1) * TB]
            gs = C.BIG[:, 6144 + j * TB:6144 + (j + 1) * TB]
            ii = C.dve.tensor_tensor(ys, ys, gs, ALU.mult)
        vd = C.sDVE.inc(ii)
        C.pe.wait_ge(C.sDVE.h, vd)
        for m in range(8):
            b = 2 + (m % 2)
            C.bank_wait(b)
            for kc in range(12):
                C.mm(C.pb[b][:, 0:TB], C.wchunk(), C.BIG[:, kc * TB:(kc + 1) * TB],
                     start=(kc == 0), stop=(kc == 11))
            v = C.stamp_pe()
            C.dve.wait_ge(C.sPE.h, v)
            x_ap = xc(m)
            vd = C.sDVE.inc(C.dve.tensor_tensor(x_ap, x_ap, C.pb[b][:, 0:TB], ALU.add))
            C.bank_guard(b, C.sDVE, vd)

def queue_stage_A(C):
    for tb in range(NTB):
        for e in range(8):
            C.ring_queue("a_w1", e * 96, 96)
            C.ring_queue("a_w2", e * 48, 48)
        C.ring_queue("a_ws1", 0, 192)
        C.ring_queue("a_ws2", 0, 96)

DB_B1, DB_B2, DB_G0, DB_B0, DB_G1, DB_B1K, DB_KB0, DB_KB1, DB_GRID = \
    0, 128, 160, 192, 224, 288, 352, 416, 448

def _zc(C, j):
    return C.ZF[:, j * TB:(j + 1) * TB] if j < 8 else C.MID[:, (j - 8) * TB:(j - 7) * TB]

def emit_stage_D(C):
    db = lambda col, n=1: C.bias(col, n)
    C.load_bias("d_bias", 448)
    for eng in (C.act, C.dve, C.pe):
        C.sX.wait(eng)
    for tb in range(NTB):
        C.barrier()
        xc = lambda c, tb=tb: C.X[:, c * T + tb * TB: c * T + tb * TB + TB]
        yc = lambda c: C.YB[:, c * TB:(c + 1) * TB]
        vy = C.ln(xc, yc, NCH)
        C.pe.wait_ge(C.sDVE.h, vy)
        gate = C.CSTB[:, CB_DGATE:CB_DGATE + 64]
        wfm, wv = C.routing(gate, renorm=True, scale=0.5)
        for e in range(4):
            vb = C.w_bcast(wfm, wv, e, 4)
            C.act.wait_ge(C.sPE.h, vb)
            wmid = C.MID[:, 0:TB]
            g = C.bankg.get("wmid")
            if g is not None:
                C.act.wait_ge(g[0], g[1])
            va = C.sACT.inc(C.act.copy(wmid, C.pb[4][:, 0:TB]))
            C.bank_guard(4, C.sACT, va)
            for m in range(32):
                b = m % 2
                C.bank_wait(b)
                for kc in range(NCH):
                    C.mm(C.pb[b][:, 0:TB], C.wchunk(), C.YB[:, kc * TB:(kc + 1) * TB],
                         start=(kc == 0), stop=(kc == NCH - 1))
                v = C.stamp_pe()
                C.act.wait_ge(C.sPE.h, v)
                va = C.sACT.inc(C.act.activation(
                    C.BIG[:, m * TB:(m + 1) * TB], C.pb[b][:, 0:TB], AF.Gelu,
                    bias=db(DB_B1 + e * 32 + m)))
                C.bank_guard(b, C.sACT, va)
            C.pe.wait_ge(C.sACT.h, C.sACT.n)
            for m in range(8):
                b = 2 + (m % 2)
                C.bank_wait(b)
                for kc in range(32):
                    C.mm(C.pb[b][:, 0:TB], C.wchunk(), C.BIG[:, kc * TB:(kc + 1) * TB],
                         start=(kc == 0), stop=(kc == 31))
                v = C.stamp_pe()
                C.dve.wait_ge(C.sPE.h, v)
                C.dve.wait_ge(C.sACT.h, va)
                tt = C.SMALL[:, S_LNSC + (m % 2) * 512:S_LNSC + (m % 2) * 512 + TB]
                C.dve.scalar_tensor_tensor(tt, C.pb[b][:, 0:TB], db(DB_B2 + e * 8 + m),
                                           wmid, ALU.add, ALU.mult)
                x_ap = xc(m)
                vd = C.sDVE.inc(C.dve.tensor_tensor(x_ap, x_ap, tt, ALU.add))
                C.bank_guard(b, C.sDVE, vd)
            C.bankg["wmid"] = (C.sDVE.h, C.sDVE.n)
        C.ln(yc, yc, NCH, src_bf=True)
        for e in range(4):
            C.barrier()
            ii = None
            for c in range(NCH):
                for gidx in range(8):
                    ii = C.dve.tensor_scalar(db(DB_GRID + c * 8 + gidx), db(DB_B0 + e * 8 + c),
                                             float(GRID[gidx]), None, ALU.subtract)
            vgb = C.sDVE.inc(ii)
            C.act.wait_ge(C.sDVE.h, vgb)
            for kh in range(2):
                if kh > 0:
                    C.barrier()
                tile_st = _build_basis(C, src=yc, nsrc=NCH, kh_g0=kh * 4, ngrp=4,
                                       scale_bias=(lambda c: (db(DB_G0 + e * 8 + c),)),
                                       grid_bias=lambda c, gi: db(DB_GRID + c * 8 + gi))
                for m in range(16):
                    b = m % 2
                    C.bank_wait(b)
                    for kc in range(32):
                        if m == 0:
                            C.pe.wait_ge(C.sDVE.h, tile_st[kc])
                        C.mm(C.pb[b][:, 0:TB], C.wchunk(),
                             C.BIG[:, kc * TB:(kc + 1) * TB], start=(kc == 0), stop=(kc == 31))
                    v = C.stamp_pe()
                    zc = _zc(C, m)
                    if kh == 0:
                        C.act.wait_ge(C.sPE.h, v)
                        va = C.sACT.inc(C.act.copy(zc, C.pb[b][:, 0:TB]))
                        C.bank_guard(b, C.sACT, va)
                    else:
                        C.dve.wait_ge(C.sPE.h, v)
                        vd = C.sDVE.inc(C.dve.scalar_tensor_tensor(
                            zc, C.pb[b][:, 0:TB], db(DB_KB0 + e * 16 + m), zc, ALU.add, ALU.add))
                        C.bank_guard(b, C.sDVE, vd)
            C.barrier()
            C.ln(lambda j: _zc(C, j), lambda j: C.ZH[:, j * TB:(j + 1) * TB], 16,
                 affine=lambda j: (db(DB_G1 + e * 16 + j), db(DB_B1K + e * 16 + j)))
            vb = C.w_bcast(wfm, wv, 4 + e, 4)
            C.act.wait_ge(C.sPE.h, vb)
            C.barrier()
            wmid = C.MID[:, 0:TB]
            va = C.sACT.inc(C.act.copy(wmid, C.pb[4][:, 0:TB]))
            C.bank_guard(4, C.sACT, va)
            C.barrier()
            for q in range(4):
                tile_st = _build_basis(C, src=lambda j: C.ZH[:, j * TB:(j + 1) * TB],
                                       nsrc=16, kh_g0=q * 2, ngrp=2,
                                       scale_bias=None,
                                       grid_bias=lambda c, gi: C.CST[:, 2019 + gi:2020 + gi])
                for m in range(8):
                    for kc in range(32):
                        if q == 0 and m == 0:
                            C.pe.wait_ge(C.sDVE.h, tile_st[kc])
                        elif m == 0:
                            C.pe.wait_ge(C.sDVE.h, tile_st[kc])
                        C.mm(C.pb[m][:, 0:TB], C.wchunk(),
                             C.BIG[:, kc * TB:(kc + 1) * TB],
                             start=(q == 0 and kc == 0), stop=(q == 3 and kc == 31))
                if q < 3:
                    C.barrier()
            v = C.stamp_pe()
            C.dve.wait_ge(C.sPE.h, v)
            C.dve.wait_ge(C.sACT.h, C.sACT.n)
            for m in range(8):
                tt = C.SMALL[:, S_LNSC + (m % 2) * 512:S_LNSC + (m % 2) * 512 + TB]
                C.dve.scalar_tensor_tensor(tt, C.pb[m][:, 0:TB], db(DB_KB1 + e * 8 + m),
                                           wmid, ALU.add, ALU.mult)
                x_ap = xc(m)
                vd = C.sDVE.inc(C.dve.tensor_tensor(x_ap, x_ap, tt, ALU.add))
                C.bank_guard(m, C.sDVE, vd)
            C.barrier()

def _build_basis(C, src, nsrc, kh_g0, ngrp, scale_bias, grid_bias):
    stamps = []
    i = 0
    prev_act = {}
    for gg in range(ngrp):
        gidx = kh_g0 + gg
        for dc in range(nsrc):
            sg = C.SC2[:, (i % 2) * 512:(i % 2) * 512 + TB]
            tile = C.BIG[:, i * TB:(i + 1) * TB]
            if i >= 2:
                C.act.wait_ge(C.sDVE.h, stamps[i - 2])
            if scale_bias is not None:
                sc = scale_bias(dc)[0]
                ia = C.act.activation(sg, src(dc), AF.Sigmoid, bias=grid_bias(dc, gidx), scale=sc)
            else:
                ia = C.act.activation(sg, src(dc), AF.Sigmoid, bias=grid_bias(dc, gidx))
            va = C.sACT.inc(ia)
            C.dve.wait_ge(C.sACT.h, va)
            C.dve.tensor_scalar(tile, sg, -4.0, 4.0, ALU.mult, ALU.add)
            vd = C.sDVE.inc(C.dve.tensor_tensor(tile, tile, sg, ALU.mult))
            stamps.append(vd)
            i += 1
    return stamps

def queue_stage_D(C):
    for tb in range(NTB):
        for e in range(4):
            C.ring_queue("d_w1", e * 256, 256)
            C.ring_queue("d_w2", e * 256, 256)
        for e in range(4):
            C.ring_queue("d_kw0", e * 1024, 1024)
            C.ring_queue("d_kw1", e * 1024, 1024)

B_TRIG = 0
B_CAR = 3072
B_SUM = 3090
B_PRE = 3108
B_TMP = 3126

def emit_stage_B(C, mode, out_dma=True, carry_init=None):
    R = C.SMALL
    db = lambda col, n=1: C.bias(col, n)
    C.load_bias("b_ln", 32, col_off=500)
    if mode == "full":
        C.load_bias("b_b1", 32, col_off=532)
        C.load_bias("b_b2", 8, col_off=564)
    for eng in (C.act, C.dve, C.pe):
        C.sX.wait(eng)
    ln1 = lambda c: (C.bias(500 + c), C.bias(508 + c))
    ln2 = lambda c: (C.bias(516 + c), C.bias(524 + c))
    dve, act, pe = C.dve, C.act, C.pe
    if mode == "sums":
        ii = dve.memset(R[:, B_SUM:B_SUM + 18], 0.0)
        C.sDVE.inc(ii)
    elif carry_init is not None:
        carry_init(C)
    else:
        ii = dve.memset(R[:, B_PRE:B_PRE + 18], 0.0)
        C.dop(ii)
        AS = C.ASUM
        PM = C.PMASK
        for r in range(8):
            ii = dve.scalar_tensor_tensor(R[:, B_PRE:B_PRE + 18], AS[:, r * 18:(r + 1) * 18],
                                          PM[:, r:r + 1], R[:, B_PRE:B_PRE + 18],
                                          ALU.mult, ALU.add)
            C.dop(ii)
        C.dop(dve.tensor_copy(R[:, B_CAR:B_CAR + 18], R[:, B_PRE:B_PRE + 18]))
    tA, tB, tC, tD, tE, tF = (R[0:16, B_TRIG + i * 512:B_TRIG + (i + 1) * 512] for i in range(6))
    for tb in range(NTB):
        C.barrier()
        xc = lambda c, tb=tb: C.X[:, c * T + tb * TB: c * T + tb * TB + TB]
        yc = lambda c: C.YB[:, c * TB:(c + 1) * TB]
        y1c = lambda c: C.BIG[:, c * TB:(c + 1) * TB]
        vy = C.ln(xc, yc, NCH)
        C.dve.wait_ge(C.sDVE.h, vy)
        C.ln(yc, y1c, NCH, affine=ln1, src_bf=True)
        C.barrier()
        C.bank_wait(6), C.bank_wait(7)
        wq = C.CSTB[:, CB_WQK:CB_WQK + 128]
        wk = C.CSTB[:, CB_WQK + 128:CB_WQK + 256]
        for kc in range(NCH):
            C.mm(C.pb[6][0:16, 0:TB], wq[:, kc * 16:(kc + 1) * 16], y1c(kc),
                 start=(kc == 0), stop=(kc == NCH - 1))
        for kc in range(NCH):
            C.mm(C.pb[7][0:16, 0:TB], wk[:, kc * 16:(kc + 1) * 16], y1c(kc),
                 start=(kc == 0), stop=(kc == NCH - 1))
        v = C.stamp_pe()
        act.wait_ge(C.sPE.h, v)
        if mode == "full":
            va = C.sACT.inc(act.activation(tC, C.pb[6][0:16, 0:TB], AF.Tanh))
            act.wait_ge(C.sACT.h, va)
            va = C.sACT.inc(act.activation(tA, tC, AF.Sin, bias=PI / 2.0, scale=PI / 4.0))
            va = C.sACT.inc(act.activation(tB, tC, AF.Sin, scale=PI / 4.0))
            C.bank_guard(6, C.sACT, va)
        va = C.sACT.inc(act.activation(tC, C.pb[7][0:16, 0:TB], AF.Tanh))
        act.wait_ge(C.sACT.h, va)
        vck = C.sACT.inc(act.activation(tD, tC, AF.Sin, scale=PI / 4.0))
        act.wait_ge(C.sACT.h, vck)
        vcc = C.sACT.inc(act.activation(tC, tC, AF.Sin, bias=PI / 2.0, scale=PI / 4.0))
        C.bank_guard(7, C.sACT, vcc)
        dve.wait_ge(C.sACT.h, vcc)
        if mode == "full":
            C.dop(dve.tensor_tensor_scan(tE, tC, tC, R[0:16, B_CAR + 16:B_CAR + 17],
                                         ALU.add, ALU.bypass))
            C.dop(dve.tensor_copy(R[0:16, B_CAR + 16:B_CAR + 17], tE[0:16, 511:512]))
            C.dop(dve.tensor_tensor_scan(tF, tD, tD, R[0:16, B_CAR + 17:B_CAR + 18],
                                         ALU.add, ALU.bypass))
            C.dop(dve.tensor_copy(R[0:16, B_CAR + 17:B_CAR + 18], tF[0:16, 511:512]))
            C.dop(dve.tensor_tensor(tE, tA, tE, ALU.mult))
            C.dop(dve.tensor_tensor(tF, tB, tF, ALU.mult))
            C.dop(dve.tensor_tensor(tE, tE, tF, ALU.add))
            C.dop(dve.tensor_scalar(tE, tE, COS_EPS, None, ALU.add))
            C.dop(dve.reciprocal(tE, tE))
        else:
            C.dop(dve.tensor_reduce(R[0:16, B_TMP:B_TMP + 1], tC, AX.X, ALU.add))
            C.dop(dve.tensor_tensor(R[0:16, B_SUM + 16:B_SUM + 17],
                                    R[0:16, B_SUM + 16:B_SUM + 17],
                                    R[0:16, B_TMP:B_TMP + 1], ALU.add))
            C.dop(dve.tensor_reduce(R[0:16, B_TMP:B_TMP + 1], tD, AX.X, ALU.add))
            C.dop(dve.tensor_tensor(R[0:16, B_SUM + 17:B_SUM + 18],
                                    R[0:16, B_SUM + 17:B_SUM + 18],
                                    R[0:16, B_TMP:B_TMP + 1], ALU.add))
        ZS = lambda i: C.ZF[:, i * 512:(i + 1) * 512]
        for c in range(NCH):
            b = c % 2
            C.bank_wait(b)
            for kc in range(NCH):
                C.mm(C.pb[b][:, 0:TB], C.wchunk(), y1c(kc), start=(kc == 0), stop=(kc == NCH - 1))
            v = C.stamp_pe()
            act.wait_ge(C.sPE.h, v)
            va = C.sACT.inc(act.copy(ZS(2), C.pb[b][:, 0:TB]))
            C.bank_guard(b, C.sACT, va)
            hs = C.CST[0:16, C_HSEL + c * P:C_HSEL + (c + 1) * P]
            C.bank_wait(4), C.bank_wait(5)
            pe.wait_ge(C.sACT.h, vcc)
            C.mm(C.pb[4][:, 0:TB], hs, tC, start=True, stop=True)
            C.mm(C.pb[5][:, 0:TB], hs, tD, start=True, stop=True)
            v = C.stamp_pe()
            dve.wait_ge(C.sPE.h, v)
            dve.wait_ge(C.sACT.h, va)
            if mode == "sums":
                C.dop(dve.tensor_tensor(ZS(3), ZS(2), C.pb[4][:, 0:TB], ALU.mult))
                C.dop(dve.tensor_reduce(R[:, B_TMP:B_TMP + 1], ZS(3), AX.X, ALU.add))
                C.dop(dve.tensor_tensor(R[:, B_SUM + c:B_SUM + c + 1],
                                        R[:, B_SUM + c:B_SUM + c + 1],
                                        R[:, B_TMP:B_TMP + 1], ALU.add))
                C.dop(dve.tensor_tensor(ZS(3), ZS(2), C.pb[5][:, 0:TB], ALU.mult))
                vg = C.dop(dve.tensor_reduce(R[:, B_TMP:B_TMP + 1], ZS(3), AX.X, ALU.add))
                C.dop(dve.tensor_tensor(R[:, B_SUM + 8 + c:B_SUM + 9 + c],
                                        R[:, B_SUM + 8 + c:B_SUM + 9 + c],
                                        R[:, B_TMP:B_TMP + 1], ALU.add))
                C.bank_guard(4, C.sDVE, vg), C.bank_guard(5, C.sDVE, C.sDVE.n)
                continue
            C.dop(dve.tensor_tensor(ZS(3), ZS(2), C.pb[4][:, 0:TB], ALU.mult))
            C.dop(dve.tensor_tensor_scan(ZS(0), ZS(3), ZS(3), R[:, B_CAR + c:B_CAR + c + 1],
                                         ALU.add, ALU.bypass))
            C.dop(dve.tensor_copy(R[:, B_CAR + c:B_CAR + c + 1], ZS(0)[:, 511:512]))
            C.dop(dve.tensor_tensor(ZS(4), ZS(2), C.pb[5][:, 0:TB], ALU.mult))
            C.dop(dve.tensor_tensor_scan(ZS(1), ZS(4), ZS(4), R[:, B_CAR + 8 + c:B_CAR + 9 + c],
                                         ALU.add, ALU.bypass))
            vg = C.dop(dve.tensor_copy(R[:, B_CAR + 8 + c:B_CAR + 9 + c], ZS(1)[:, 511:512]))
            C.bank_guard(4, C.sDVE, vg), C.bank_guard(5, C.sDVE, vg)
            C.bank_wait(4), C.bank_wait(5)
            C.mm(C.pb[4][:, 0:TB], hs, tA, start=True, stop=True)
            C.mm(C.pb[5][:, 0:TB], hs, tB, start=True, stop=True)
            v = C.stamp_pe()
            dve.wait_ge(C.sPE.h, v)
            C.dop(dve.tensor_tensor(ZS(5), ZS(0), C.pb[4][:, 0:TB], ALU.mult))
            C.dop(dve.tensor_tensor(ZS(4), ZS(1), C.pb[5][:, 0:TB], ALU.mult))
            vg = C.dop(dve.tensor_tensor(ZS(5), ZS(5), ZS(4), ALU.add))
            C.bank_guard(4, C.sDVE, vg), C.bank_guard(5, C.sDVE, vg)
            C.bank_wait(4)
            C.mm(C.pb[4][:, 0:TB], hs, tE, start=True, stop=True)
            v = C.stamp_pe()
            dve.wait_ge(C.sPE.h, v)
            vg = C.dop(dve.tensor_tensor(C.BIG[:, 4096 + c * TB:4096 + (c + 1) * TB],
                                         ZS(5), C.pb[4][:, 0:TB], ALU.mult))
            C.bank_guard(4, C.sDVE, vg)
        if mode == "sums":
            continue
        C.barrier()
        for m in range(NCH):
            b = 2 + (m % 2)
            C.bank_wait(b)
            for kc in range(NCH):
                C.mm(C.pb[b][:, 0:TB], C.wchunk(),
                     C.BIG[:, 4096 + kc * TB:4096 + (kc + 1) * TB],
                     start=(kc == 0), stop=(kc == NCH - 1))
            v = C.stamp_pe()
            dve.wait_ge(C.sPE.h, v)
            x2 = C.MID[:, m * TB:(m + 1) * TB]
            dve.tensor_tensor(x2, C.pb[b][:, 0:TB], yc(m), ALU.add)
            x_ap = xc(m)
            vd = C.sDVE.inc(dve.tensor_tensor(x_ap, x_ap, x2, ALU.add))
            C.bank_guard(b, C.sDVE, vd)
        C.barrier()
        if B2_STOP == "x2":
            continue
        vy2 = C.ln(lambda m: C.MID[:, m * TB:(m + 1) * TB], yc, NCH, affine=ln2)
        C.pe.wait_ge(C.sDVE.h, vy2)
        if B2_STOP == "y2":
            C.barrier()
            ii = None
            for c in range(NCH):
                ii = C.dve.tensor_copy(xc(c), yc(c))
            C.sDVE.inc(ii)
            continue
        for m in range(32):
            b = m % 2
            C.bank_wait(b)
            for kc in range(NCH):
                C.mm(C.pb[b][:, 0:TB], C.wchunk(), yc(kc), start=(kc == 0), stop=(kc == NCH - 1))
            v = C.stamp_pe()
            act.wait_ge(C.sPE.h, v)
            s = C.ZF[:, 0:TB]
            w_ = C.ZF[:, 512:1024]
            th = C.ZF[:, 1024:1536]
            a_ = C.ZF[:, 1536:2048]
            g = C.bankg.get("gelu")
            if g is not None:
                act.wait_ge(g[0], g[1])
            u_ = C.ZF[:, 2048:2560]
            dve.wait_ge(C.sPE.h, v)
            vd = C.sDVE.inc(dve.tensor_scalar(u_, C.pb[b][:, 0:TB], db(532 + m), None, ALU.add))
            if B2_STOP == "u" and m < NCH:
                C.dop(dve.tensor_copy(xc(m), u_))
            act.wait_ge(C.sDVE.h, vd)
            va = C.sACT.inc(act.activation(s, u_, AF.Square))
            dve.wait_ge(C.sACT.h, va)
            dve.tensor_scalar(w_, s, GELU_C, 1.0, ALU.mult, ALU.add)
            vd = C.sDVE.inc(dve.tensor_tensor(w_, w_, u_, ALU.mult))
            act.wait_ge(C.sDVE.h, vd)
            va = C.sACT.inc(act.activation(th, w_, AF.Tanh, scale=RQ2PI))
            dve.wait_ge(C.sACT.h, va)
            dve.tensor_tensor(a_, th, u_, ALU.mult)
            dve.tensor_tensor(a_, a_, u_, ALU.add)
            vd = C.sDVE.inc(dve.tensor_scalar(C.BIG[:, m * TB:(m + 1) * TB], a_, 0.5, None, ALU.mult))
            C.bank_guard(b, C.sDVE, vd)
            C.bankg["gelu"] = (C.sDVE.h, vd)
        if B2_STOP == "h":
            C.barrier()
            ii = None
            for c in range(NCH):
                ii = C.dve.tensor_copy(xc(c), C.BIG[:, c * TB:(c + 1) * TB])
            C.sDVE.inc(ii)
            continue
        C.pe.wait_ge(C.sDVE.h, C.sDVE.n)
        for m in range(NCH):
            b = 2 + (m % 2)
            C.bank_wait(b)
            for kc in range(32):
                C.mm(C.pb[b][:, 0:TB], C.wchunk(), C.BIG[:, kc * TB:(kc + 1) * TB],
                     start=(kc == 0), stop=(kc == 31))
            v = C.stamp_pe()
            dve.wait_ge(C.sPE.h, v)
            x_ap = xc(m)
            vd = C.sDVE.inc(dve.scalar_tensor_tensor(
                x_ap, C.pb[b][:, 0:TB], db(564 + m), x_ap, ALU.add, ALU.add))
            C.bank_guard(b, C.sDVE, vd)
    if mode == "sums" and out_dma:
        C.barrier()
        C.sX.inc(C.pool.dma_start(C.sums_out, R[:, B_SUM:B_SUM + 18]))

def queue_stage_B(C, mode):
    for tb in range(NTB):
        C.ring_queue("b_wv", 0, 64)
        if mode == "full":
            C.ring_queue("b_wo", 0, 64)
            C.ring_queue("b_w1", 0, 256)
            C.ring_queue("b_w2", 0, 256)


def emit_stage_E(C):
    db = lambda col, n=1: C.bias(col, n)
    C.load_bias("e_bias", 32)
    for eng in (C.act, C.dve, C.pe):
        C.sX.wait(eng)
    R = C.SMALL
    for tb in range(NTB):
        C.barrier()
        xc = lambda c, tb=tb: C.X[:, c * T + tb * TB: c * T + tb * TB + TB]
        yc = lambda c: C.YB[:, c * TB:(c + 1) * TB]
        vy = C.ln(xc, yc, NCH)
        C.pe.wait_ge(C.sDVE.h, vy)
        for ip, (dst, bcol) in enumerate([(0, 0), (1, 8), (2, 16)]):
            for m in range(NCH):
                b = m % 2
                C.bank_wait(b)
                for kc in range(NCH):
                    C.mm(C.pb[b][:, 0:TB], C.wchunk(), C.YB[:, kc * TB:(kc + 1) * TB],
                         start=(kc == 0), stop=(kc == NCH - 1))
                v = C.stamp_pe()
                C.dve.wait_ge(C.sPE.h, v)
                if ip < 2:
                    o = C.BIG[:, ip * 4096 + m * TB: ip * 4096 + (m + 1) * TB]
                else:
                    o = C.MID[:, m * TB:(m + 1) * TB]
                vd = C.sDVE.inc(C.dve.tensor_scalar(o, C.pb[b][:, 0:TB], db(bcol + m), None, ALU.add))
                C.bank_guard(b, C.sDVE, vd)
        C.barrier()
        SCALE = float(1.0 / np.sqrt(128.0))
        EX4 = R[:, S_RT:S_RT + 256]
        PRT = R[:, S_RT + 256:S_RT + 512]
        MX4 = R[:, S_RT + 512:S_RT + 516]
        NM4 = R[:, S_RT + 516:S_RT + 520]
        SM4 = R[:, S_RT + 520:S_RT + 524]
        RC4 = R[:, S_RT + 524:S_RT + 528]
        for blk in range(8):
            qb = lambda h: C.BIG[:, h * TB + blk * 64: h * TB + blk * 64 + 64]
            kb = lambda h: C.BIG[:, 4096 + h * TB + blk * 64: 4096 + h * TB + blk * 64 + 64]
            vb = lambda h: C.MID[:, h * TB + blk * 64: h * TB + blk * 64 + 64]
            C.bank_wait(6)
            for hp in range(4):
                C.mm(C.pb[6][0:64, hp * 64:(hp + 1) * 64], qb(2 * hp), kb(2 * hp),
                     start=True, stop=True)
                C.mm(C.pb[6][64:128, hp * 64:(hp + 1) * 64], qb(2 * hp + 1), kb(2 * hp + 1),
                     start=True, stop=True)
            v = C.stamp_pe()
            C.dve.wait_ge(C.sPE.h, v)
            C.dop(C.dve.tensor_reduce(MX4, C.pb[6][:, 0:256].rearrange("p (a b) -> p a b", a=4),
                                      AX.X, ALU.max))
            vd = C.dop(C.dve.tensor_scalar(NM4, MX4, -SCALE, None, ALU.mult))
            C.act.wait_ge(C.sDVE.h, vd)
            ia = None
            for hp in range(4):
                ia = C.act.activation(EX4[:, hp * 64:(hp + 1) * 64], C.pb[6][:, hp * 64:(hp + 1) * 64],
                                      AF.Exp, bias=NM4[:, hp:hp + 1], scale=SCALE)
            va = C.sACT.inc(ia)
            C.bank_guard(6, C.sACT, va)
            C.dve.wait_ge(C.sACT.h, va)
            C.dop(C.dve.tensor_reduce(SM4, EX4.rearrange("p (a b) -> p a b", a=4), AX.X, ALU.add))
            C.dop(C.dve.reciprocal(RC4, SM4))
            ii = None
            for hp in range(4):
                ii = C.dve.tensor_single_scalar(PRT[:, hp * 64:(hp + 1) * 64],
                                                EX4[:, hp * 64:(hp + 1) * 64],
                                                RC4[:, hp:hp + 1], ALU.mult)
            vd = C.dop(ii)
            C.bank_wait(7)
            C.pe.wait_ge(C.sDVE.h, vd)
            for hp in range(4):
                C.tr(C.pb[7][0:64, hp * 128:(hp + 1) * 128], PRT[:, hp * 64:(hp + 1) * 64],
                     C.ident_f32)
            v = C.stamp_pe()
            C.act.wait_ge(C.sPE.h, v)
            prT = C.SC2[0:64, 0:512]
            va = C.sACT.inc(C.act.copy(prT, C.pb[7][0:64, 0:512]))
            C.bank_guard(7, C.sACT, va)
            vT = C.SC2[0:64, 512:1536]
            for g in range(2):
                C.bank_wait(5)
                for j in range(4):
                    h = g * 4 + j
                    C.tr(C.pb[5][0:64, j * 128:(j + 1) * 128], vb(h), C.ident_f32)
                v = C.stamp_pe()
                C.act.wait_ge(C.sPE.h, v)
                va = C.sACT.inc(C.act.copy(vT[0:64, g * 512:(g + 1) * 512], C.pb[5][0:64, 0:512]))
                C.bank_guard(5, C.sACT, va)
            C.bank_wait(4)
            C.pe.wait_ge(C.sACT.h, C.sACT.n)
            for h in range(8):
                C.mm(C.pb[4][:, h * 64:(h + 1) * 64],
                     vT[0:64, h * 128:(h + 1) * 128], prT[0:64, h * 64:(h + 1) * 64],
                     start=True, stop=True)
            v = C.stamp_pe()
            C.act.wait_ge(C.sPE.h, v)
            ia = None
            for h in range(8):
                ia = C.act.copy(C.ZH[:, h * TB + blk * 64: h * TB + blk * 64 + 64],
                                C.pb[4][:, h * 64:(h + 1) * 64])
            va = C.sACT.inc(ia)
            C.bank_guard(4, C.sACT, va)
        C.pe.wait_ge(C.sACT.h, C.sACT.n)
        for m in range(NCH):
            b = 2 + (m % 2)
            C.bank_wait(b)
            for kc in range(NCH):
                C.mm(C.pb[b][:, 0:TB], C.wchunk(), C.ZH[:, kc * TB:(kc + 1) * TB],
                     start=(kc == 0), stop=(kc == NCH - 1))
            v = C.stamp_pe()
            C.dve.wait_ge(C.sPE.h, v)
            x_ap = xc(m)
            vd = C.sDVE.inc(C.dve.scalar_tensor_tensor(
                x_ap, C.pb[b][:, 0:TB], db(24 + m), x_ap, ALU.add, ALU.add))
            C.bank_guard(b, C.sDVE, vd)

def queue_stage_E(C):
    for tb in range(NTB):
        for nm in ("e_wq", "e_wk", "e_wv", "e_wo"):
            C.ring_queue(nm, 0, 64)

STAGES = {}


ALWAYS_KEYS = ("a_gate", "d_gate", "b_wqk", "c_ident_bf", "c_ident_f32", "c_ones_bf",
               "c_ones_f32", "c_onesrow", "c_sel8", "c_hsel")

def build_unfused_stage(emit_fn, queue_fn, arrs, keys=None, extra=None):
    nc = bass.Bass("TRN2", target_bir_lowering=False, debug=False)
    stk = ExitStack()
    with stk:
        if keys is None:
            shapes = dict(arrs)
        else:
            shapes = {k: arrs[k] for k in list(keys) + list(ALWAYS_KEYS)}
        shapes["xin"] = np.zeros((P, NCH * T), np.float32)
        C = Ctx(nc, stk, shapes)
        C.xout = nc.dram_tensor("xout", [P, NCH * T], F32, kind="ExternalOutput").ap()
        if extra == "b_sums":
            C.sums_out = nc.dram_tensor("sums_out", [P, 18], F32, kind="ExternalOutput").ap()
        elif extra == "b_full":
            asum_d = nc.dram_tensor("allsums", [P, 144], F32, kind="ExternalInput").ap()
            pm_d = nc.dram_tensor("pmask", [P, 8], F32, kind="ExternalInput").ap()
            C.ASUM = C.SMALL[:, 3200:3344]
            C.PMASK = C.SMALL[:, 3344:3352]
            C.sX.inc(C.pool.dma_start(C.ASUM, asum_d))
            C.sX.inc(C.pool.dma_start(C.PMASK, pm_d))
        C.setup()
        C.sX.inc(C.pool.dma_start(C.X[:, :], C.dram["xin"]))
        for eng in (C.pe, C.act, C.dve, C.pool, C.sp):
            C.sX.wait(eng)
        queue_fn(C)
        emit_fn(C)
        C.barrier()
        C.sX.inc(C.pool.dma_start(C.xout, C.X[:, :]))
        C.sX.wait(C.pool)
    return nc

FUS_FOR = 3840
FUS_PM4 = 3858

def emit_exchange(C, sh_data, sh_flag, snc, sfl, psem):
    R = C.SMALL
    g = C.pool
    dve = C.dve
    C.barrier()
    C.dop(dve.tensor_copy(C.MID[:, 0:18], R[:, B_SUM:B_SUM + 18]))
    vd = C.dop(dve.tensor_copy(C.MID[:, 18:36], R[:, FUS_FOR:FUS_FOR + 18]))
    g.wait_ge(C.sDVE.h, vd)
    pid = g.partition_id()
    lane = g.compute_val(pid % 2)
    partner = g.compute_val(1 - (pid % 2))
    C.sX.inc(g.dma_start(sh_data[bass.ts(lane, 128), :], C.MID[:, 0:36]))
    C.sX.wait(g)
    i = None
    for i2 in range(1, 16):
        i = g.tensor_copy(snc[0:1, i2:i2 + 1], snc[0:1, 0:1])
    C.sPOOL.inc(i)
    g.wait_ge(C.sPOOL.h, C.sPOOL.n)
    C.sX.inc(g.dma_start(sh_flag[bass.ts(lane, 1), :], snc[0:1, :]))
    C.sX.wait(g)
    with g.register("rt") as rt, g.register("rv") as rv, g.register("rn") as rn, \
            g.register("rcond") as rcond:
        g.reg_mov(rt, 0)
        g.reg_load(rn, snc[0:1, 0:1])

        def cond():
            g.dma_start(sfl[0:1, :], sh_flag[bass.ts(partner, 1), :]).then_inc(psem, 16)
            g.reg_add(rt, rt, 16)
            g.wait_ge(psem, rt)
            g.reg_load(rv, sfl[0:1, 0:1])
            g.reg_alu(rcond, rv, rn, ALU.not_equal)
            return rcond

        with g.While(cond):
            g.nop()
        g.dma_start(C.MID[:, 64:100], sh_data[bass.ts(partner, 128), :]).then_inc(psem, 16)
        g.reg_add(rt, rt, 16)
        g.wait_ge(psem, rt)
    C.sPOOL.inc(g.engine_nop())
    dve.wait_ge(C.sPOOL.h, C.sPOOL.n)
    C.dop(dve.memset(R[:, B_PRE:B_PRE + 18], 0.0))
    PM4 = R[:, FUS_PM4:FUS_PM4 + 4]
    for j, (c0, c1) in enumerate([(0, 18), (18, 36), (64, 82), (82, 100)]):
        ii = dve.scalar_tensor_tensor(R[:, B_PRE:B_PRE + 18], C.MID[:, c0:c1],
                                      PM4[:, j:j + 1], R[:, B_PRE:B_PRE + 18],
                                      ALU.mult, ALU.add)
        C.dop(ii)
    C.dop(dve.tensor_copy(R[:, B_CAR:B_CAR + 18], R[:, B_PRE:B_PRE + 18]))


FUSED_UPTO = "F"
FUSED_FOREIGN = True

def build_fused(arrs):
    nc = bass.Bass("TRN2", target_bir_lowering=False, debug=False)
    stk = ExitStack()
    with stk:
        keys = sorted(set(K_A + K_B1 + K_B2 + K_D + K_E))
        shapes = {k: arrs[k] for k in keys + list(ALWAYS_KEYS)}
        shapes["xin"] = np.zeros((P, NCH * T), np.float32)
        shapes["xfor"] = np.zeros((P, NCH * T), np.float32)
        shapes["pmask4"] = np.zeros((P, 4), np.float32)
        shapes["nonce"] = np.zeros((1, 1), np.int32)
        C = Ctx(nc, stk, shapes)
        C.xout = nc.dram_tensor("xout", [P, NCH * T], F32, kind="ExternalOutput").ap()
        sh_data = nc.dram_tensor("sh_data", [2 * P, 36], F32, kind="Internal",
                                 addr_space="Shared").ap()
        sh_flag = nc.dram_tensor("sh_flag", [2, 16], I32, kind="Internal",
                                 addr_space="Shared").ap()
        snc = stk.enter_context(nc.sbuf_tensor("snc", [1, 16], I32))
        sfl = stk.enter_context(nc.sbuf_tensor("sfl", [1, 16], I32))
        psem = stk.enter_context(nc.semaphore("psem"))
        C.setup()
        C.sX.inc(C.pool.dma_start(snc[0:1, 0:1], C.dram["nonce"]))
        C.sX.inc(C.pool.dma_start(C.SMALL[:, FUS_PM4:FUS_PM4 + 4], C.dram["pmask4"]))
        if FUSED_FOREIGN:
            queue_stage_A(C)
            queue_stage_B(C, "sums")
        queue_stage_A(C)
        if FUSED_UPTO != "A":
            queue_stage_B(C, "sums")
            if FUSED_UPTO != "Bdbg":
                queue_stage_B(C, "full")
        if FUSED_UPTO not in ("A", "B"):
            queue_stage_A(C)
        if FUSED_UPTO in ("D", "E", "F"):
            queue_stage_D(C)
        if FUSED_UPTO in ("E", "F"):
            queue_stage_E(C)
        if FUSED_UPTO == "F":
            queue_stage_D(C)
        if FUSED_FOREIGN:
            C.sX.inc(C.pool.dma_start(C.X[:, :], C.dram["xfor"]))
            for eng in (C.pe, C.act, C.dve, C.pool, C.sp):
                C.sX.wait(eng)
            emit_stage_A(C)
            emit_stage_B(C, "sums", out_dma=False)
            C.barrier()
            C.dop(C.dve.tensor_copy(C.SMALL[:, FUS_FOR:FUS_FOR + 18],
                                    C.SMALL[:, B_SUM:B_SUM + 18]))
            C.barrier()
        C.sX.inc(C.pool.dma_start(C.X[:, :], C.dram["xin"]))
        for eng in (C.pe, C.act, C.dve, C.pool, C.sp):
            C.sX.wait(eng)
        emit_stage_A(C)
        if FUSED_UPTO not in ("A",):
            emit_stage_B(C, "sums", out_dma=False)
            if FUSED_FOREIGN:
                emit_exchange(C, sh_data, sh_flag, snc, sfl, psem)
            else:
                ii = C.dve.memset(C.SMALL[:, B_CAR:B_CAR + 18], 0.0)
                C.dop(ii)
            if FUSED_UPTO == "Bdbg":
                C.barrier()
                C.dop(C.dve.tensor_copy(C.X[:, 0:18], C.SMALL[:, B_CAR:B_CAR + 18]))
                C.dop(C.dve.tensor_copy(C.X[:, 18:118], C.MID[:, 0:100]))
                C.dop(C.dve.tensor_copy(C.X[:, 118:122], C.SMALL[:, FUS_PM4:FUS_PM4 + 4]))
            else:
                emit_stage_B(C, "full", carry_init=lambda C: None)
            C.barrier()
        if FUSED_UPTO not in ("A", "B"):
            emit_stage_A(C)
            C.barrier()
        if FUSED_UPTO in ("D", "E", "F"):
            emit_stage_D(C)
            C.barrier()
        if FUSED_UPTO in ("E", "F"):
            emit_stage_E(C)
            C.barrier()
        if FUSED_UPTO == "F":
            emit_stage_D(C)
        C.barrier()
        C.sX.inc(C.pool.dma_start(C.xout, C.X[:, :]))
        C.sX.wait(C.pool)
    return nc

XFOR_IDX = {0: 0, 1: 1, 2: 0, 3: 1, 4: 4, 5: 5, 6: 4, 7: 5}

def _pmask4(c):
    m = np.zeros((P, 4), np.float32)
    r = c % 4
    if r == 1:
        m[:, 2] = 1.0
    elif r == 2:
        m[:, 1] = 1.0
        m[:, 3] = 1.0
    elif r == 3:
        m[:, 1] = 1.0
        m[:, 2] = 1.0
        m[:, 3] = 1.0
    return m

def kernel_fused(inputs):
    shared = _get_shared(inputs)
    if "F" not in _NCS:
        _NCS["F"] = build_fused(shared)
    nc = _NCS["F"]
    x = np.asarray(inputs["L0_attn_x"], np.float32)
    xs = [x_fm(x[c // 4, (c % 4) * T:(c % 4 + 1) * T, :]) for c in range(8)]
    keys = sorted(set(K_A + K_B1 + K_B2 + K_D + K_E))
    base = {k: shared[k] for k in keys + list(ALWAYS_KEYS)}
    nonce = np.array([[np.random.randint(1, 2 ** 30)]], np.int32)
    maps = []
    for c in range(8):
        m = dict(base)
        m["xin"] = xs[c]
        m["xfor"] = xs[XFOR_IDX[c]]
        m["pmask4"] = _pmask4(c)
        m["nonce"] = nonce
        maps.append(m)
    res = run_bass_kernel_spmd(nc, maps, core_ids=list(range(8)))
    out = np.empty((2, 8192, D), np.float32)
    for c in range(8):
        out[c // 4, (c % 4) * T:(c % 4 + 1) * T, :] = x_un(np.asarray(res.results[c]["xout"]))
    return out

_PREPPED = {}

def _get_shared(inputs):
    key = id(inputs.get("L0_attn_x"))
    if key not in _PREPPED:
        _PREPPED.clear()
        _PREPPED[key] = prep_shared(inputs)
    return _PREPPED[key]

K_A = ["a_w1", "a_w2", "a_ws1", "a_ws2"]
K_B1 = ["b_wv", "b_ln"]
K_B2 = ["b_wv", "b_wo", "b_w1", "b_w2", "b_ln", "b_b1", "b_b2"]
K_D = ["d_w1", "d_w2", "d_kw0", "d_kw1", "d_bias"]
K_E = ["e_wq", "e_wk", "e_wv", "e_wo", "e_bias"]

def _pmask(c):
    m = np.zeros((P, 8), np.float32)
    for r in range((c // 4) * 4, c):
        m[:, r] = 1.0
    return m

_NCS = {}

def _get_ncs(shared):
    if "A" not in _NCS:
        _NCS["A"] = build_unfused_stage(emit_stage_A, queue_stage_A, shared, K_A)
        _NCS["B1"] = build_unfused_stage(lambda C: emit_stage_B(C, "sums"),
                                         lambda C: queue_stage_B(C, "sums"),
                                         shared, K_B1, extra="b_sums")
        _NCS["B2"] = build_unfused_stage(lambda C: emit_stage_B(C, "full"),
                                         lambda C: queue_stage_B(C, "full"),
                                         shared, K_B2, extra="b_full")
        _NCS["D"] = build_unfused_stage(emit_stage_D, queue_stage_D, shared, K_D)
        _NCS["E"] = build_unfused_stage(emit_stage_E, queue_stage_E, shared, K_E)
    return _NCS

def kernel(**inputs):
    return kernel_fused(inputs)

def kernel_unfused(**inputs):
    shared = _get_shared(inputs)
    ncs = _get_ncs(shared)
    x = np.asarray(inputs["L0_attn_x"], np.float32)
    xs = [x_fm(x[c // 4, (c % 4) * T:(c % 4 + 1) * T, :]) for c in range(8)]
    cores = list(range(8))

    def run(name, keys, extra_fn=None):
        base = {k: shared[k] for k in keys + list(ALWAYS_KEYS)}
        maps = []
        for c in cores:
            m = dict(base)
            m["xin"] = xs[c]
            if extra_fn is not None:
                m.update(extra_fn(c))
            maps.append(m)
        return run_bass_kernel_spmd(ncs[name], maps, core_ids=cores)

    for st in ("A", "B", "A", "D", "E", "D"):
        if st == "B":
            r1 = run("B1", K_B1)
            allsums = np.concatenate([r1.results[c]["sums_out"] for c in cores], axis=1)
            res = run("B2", K_B2, lambda c: dict(allsums=allsums, pmask=_pmask(c)))
        else:
            res = run(st, {"A": K_A, "D": K_D, "E": K_E}[st])
        xs = [np.asarray(res.results[c]["xout"]) for c in cores]

    out = np.empty((2, 8192, D), np.float32)
    for c in cores:
        out[c // 4, (c % 4) * T:(c % 4 + 1) * T, :] = x_un(xs[c])
    return out
